# version 06
# baseline (speedup 1.0000x reference)
import numpy as np
import ml_dtypes
import concourse.bass as bass
import concourse.mybir as mybir
from concourse.bass_utils import run_bass_kernel_spmd
from contextlib import ExitStack

F32 = mybir.dt.float32
BF16 = mybir.dt.bfloat16
AF = mybir.ActivationFunctionType
ALU = mybir.AluOpType

D = 2048
NH = 32
CONV_DIM = 6144
HID = 8192
CHUNK = 64
LN_EPS = 1e-5
RMS_EPS = 1.1920929e-07
NCORES = 8

CSTREAMS = {"pe": "tensor", "dve": "vector", "act": "scalar",
            "pool": "gpsimd", "cc": "gpsimd"}
QUEUE_ENG = {"qs": "sync", "qa": "scalar", "qg": "gpsimd"}


def _splits(total, size, base=0):
    out, off = [], 0
    while off < total:
        w = min(size, total - off)
        out.append((base + off, w))
        off += w
    return out


def _hadamard(n):
    i = np.arange(n)
    a = i[:, None] & i[None, :]
    pc = np.zeros((n, n), dtype=np.int64)
    while a.any():
        pc += a & 1
        a >>= 1
    return np.where(pc % 2 == 0, 1.0, -1.0).astype(np.float32)


class Sched:

    def __init__(self):
        self.count = {s: 0 for s in CSTREAMS}
        self.ops = {e: [] for e in set(CSTREAMS.values()) | set(QUEUE_ENG.values())}
        self.buf = {}
        self.seen = {e: {} for e in self.ops}
        self.bar = {}
        self.key_q = {}

    def barrier(self):
        self.bar = dict(self.count)

    def _rec(self, eng, stream, inc, fn, reads, writes, extra_waits):
        need = dict(self.bar)
        for t in reads:
            b = self.buf.get(t)
            if b is None:
                continue
            if b[0] is not None:
                s, c = b[0]
                if need.get(s, 0) < c:
                    need[s] = c
        for t in writes:
            b = self.buf.get(t)
            if b is None:
                continue
            if b[0] is not None:
                s, c = b[0]
                if need.get(s, 0) < c:
                    need[s] = c
            for s, c in b[1].items():
                if need.get(s, 0) < c:
                    need[s] = c
        for s, c in extra_waits:
            if need.get(s, 0) < c:
                need[s] = c
        waits = []
        seen = self.seen[eng]
        for s in sorted(need):
            c = need[s]
            if c <= 0:
                continue
            if s == stream and s == "pe":
                continue
            if seen.get(s, 0) >= c:
                continue
            waits.append((s, c))
            seen[s] = c
        cnt = self.count[stream] + inc
        self.count[stream] = cnt
        self.ops[eng].append((waits, fn, stream, inc))
        for t in reads:
            b = self.buf.setdefault(t, [None, {}])
            b[1][stream] = cnt
        for t in writes:
            self.buf[t] = [(stream, cnt), {}]

    def op(self, stream, fn, reads=(), writes=(), inc=1, extra_waits=()):
        self._rec(CSTREAMS[stream], stream, inc, fn, reads, writes, extra_waits)

    def dma(self, queue, key, fn, reads=(), writes=()):
        stream = "D:" + key
        q0 = self.key_q.setdefault(key, queue)
        assert q0 == queue, f"dma key {key} used on two queues {q0}/{queue}"
        if stream not in self.count:
            self.count[stream] = 0
        self._rec(QUEUE_ENG[queue], stream, 16, fn, reads, writes, ())

    def emit(self, block, mk_sem):
        sems = {s: mk_sem(s) for s in self.count}
        for eng_name in ("sync", "gpsimd", "vector", "scalar", "tensor"):
            ops = self.ops[eng_name]

            def make(ops_local):
                def body(e):
                    for waits, fn, stream, inc in ops_local:
                        for s, c in waits:
                            e.wait_ge(sems[s], c)
                        inst = fn(e)
                        if stream == "cc":
                            inst.then_inc(sems[stream])
                        else:
                            inst.then_inc(sems[stream], inc)
                return body

            getattr(block, eng_name)(make(ops))


class Arena:

    def __init__(self, nc, base, top):
        self.nc = nc
        self.base = base
        self.top = top
        self.off = base
        self.uid = [0]

    def alloc(self, name, shape, dtype):
        nb = int(np.prod(shape[1:])) * mybir.dt.size(dtype)
        off = (self.off + 31) // 32 * 32
        assert off + nb <= self.top, (
            f"SBUF arena overflow: {name} needs {nb}B at {off}, top {self.top}")
        self.uid[0] += 1
        h = self.nc.alloc_sbuf_tensor_at(f"{name}_{self.uid[0]}", list(shape), dtype, offset=off)
        self.off = off + nb
        return h

    def sub(self):
        a = Arena(self.nc, (self.off + 31) // 32 * 32, self.top)
        a.uid = self.uid
        return a


DBG = {"skip_p0": False, "halves": (0, 1), "skip_rest": False, "skip_p3": False, "skip_ssd": False, "sprint_only": False, "no_cc": False, "gv": "tp", "parts": {"tp", "g", "ge", "y1", "y2", "gate", "epi"}}


def build_program(T, ncores=NCORES):
    Tc = T + 2
    Th = T // 2
    ngc = T // CHUNK

    nc = bass.Bass("TRN2", target_bir_lowering=False, debug=False)

    u_h = nc.dram_tensor("u", [D, Tc], F32, kind="ExternalInput")
    win_h = nc.dram_tensor("win_t", [D, 8224], F32, kind="ExternalInput")
    wout_h = nc.dram_tensor("wout_t", [D, D], F32, kind="ExternalInput")
    w1_h = nc.dram_tensor("w1_t", [D, HID], F32, kind="ExternalInput")
    w3_h = nc.dram_tensor("w3_t", [D, HID], F32, kind="ExternalInput")
    w2_h = nc.dram_tensor("w2_t", [HID, D], F32, kind="ExternalInput")
    convp_h = nc.dram_tensor("convp", [128, 192], F32, kind="ExternalInput")
    rmsp_h = nc.dram_tensor("rmsp", [128, 16], F32, kind="ExternalInput")
    alog_h = nc.dram_tensor("alog", [1, NH], F32, kind="ExternalInput")
    ident_h = nc.dram_tensor("ident", [128, 128], BF16, kind="ExternalInput")
    onesb_h = nc.dram_tensor("onesb", [128, 128], BF16, kind="ExternalInput")
    ones1_h = nc.dram_tensor("ones1", [1, 128], F32, kind="ExternalInput")
    h128_h = nc.dram_tensor("h128", [128, 128], BF16, kind="ExternalInput")
    h64_h = nc.dram_tensor("h64", [64, 64], BF16, kind="ExternalInput")
    mask4_h = nc.dram_tensor("mask4", [64, 256], F32, kind="ExternalInput")
    wsel_h = nc.dram_tensor("wsel", [128, 4], F32, kind="ExternalInput")
    ohsel_h = nc.dram_tensor("ohsel", [128, 4], F32, kind="ExternalInput")
    epsc_h = nc.dram_tensor("epsc", [1, 2], F32, kind="ExternalInput")
    out_h = nc.dram_tensor("out", [D, T], F32, kind="ExternalOutput")

    xbc_d = nc.dram_tensor("xbc_d", [CONV_DIM, T], BF16)
    zs_d = nc.dram_tensor("zs_d", [D, T], BF16)
    x1_d = nc.dram_tensor("x1_d", [D, T], F32)
    lnx1_d = nc.dram_tensor("lnx1_d", [D, T], BF16)
    hid_d = nc.dram_tensor("hid_d", [HID, T], BF16)
    p1_d = nc.dram_tensor("p1_d", [HID, D], BF16)
    m2_d = nc.dram_tensor("m2_d", [HID, D], BF16)
    cc_in = nc.dram_tensor("cc_in", [4, D], F32)
    cc_out = nc.dram_tensor("cc_out", [4, D], F32)

    S = Sched()
    es = ExitStack()
    with es:
        pb = [es.enter_context(nc.psum_tensor(f"pb{i}", [128, 512], F32)) for i in range(4)]
        pT = es.enter_context(nc.psum_tensor("pT", [128, 1024], BF16))
        pS = [es.enter_context(nc.psum_tensor(f"pS{i}", [128, 512], F32)) for i in range(2)]
        pst = es.enter_context(nc.psum_tensor("pst", [128, 512], F32))

        SB_BASE = 16512
        SB_TOP = 229344
        root = Arena(nc, SB_BASE, SB_TOP)
        ident = root.alloc("ident", [128, 128], BF16)
        onesb = root.alloc("onesb", [128, 128], BF16)
        ones1 = root.alloc("ones1", [1, 128], F32)
        h128 = root.alloc("h128", [128, 128], BF16)
        h64 = root.alloc("h64", [64, 64], BF16)
        mask4 = root.alloc("mask4", [64, 256], F32)
        convp = root.alloc("convp", [128, 192], F32)
        rmsp = root.alloc("rmsp", [128, 16], F32)
        wsel = root.alloc("wsel", [128, 4], F32)
        ohsel = root.alloc("ohsel", [128, 4], F32)
        alog = root.alloc("alog", [1, NH], F32)
        d16 = root.alloc("d16", [128, 16], F32)
        rows = root.alloc("rows", [1, 1536], F32)
        zn = root.alloc("zn", [128, 512], F32)
        st16 = root.alloc("st16", [128, 16 * 256], BF16)
        sq16 = root.alloc("sq16", [128, 16 * 256], BF16)
        zeros64 = root.alloc("zeros64", [128, CHUNK], F32)
        epsc = root.alloc("epsc", [1, 2], F32)

        block = es.enter_context(nc.Block())

        for name, dst, src in [
            ("c:ident", ident, ident_h), ("c:onesb", onesb, onesb_h),
            ("c:ones1", ones1, ones1_h), ("c:h128", h128, h128_h),
            ("c:h64", h64, h64_h),
            ("c:convp", convp, convp_h), ("c:rmsp", rmsp, rmsp_h),
            ("c:wsel", wsel, wsel_h), ("c:ohsel", ohsel, ohsel_h),
            ("c:alog", alog, alog_h), ("c:epsc", epsc, epsc_h),
            ("c:mask4", mask4, mask4_h),
        ]:
            S.dma("qs", "c", (lambda d, s: lambda e: e.dma_start(out=d[:, :], in_=s[:, :]))(dst, src),
                 writes=(name, "cR"))
        S.op("pe", lambda e: e.matmul(pb[0][:, 0:16], ones1[0:1, :],
                                      alog[0:1, 0:32:2], start=True, stop=True),
             reads=("c:ones1", "c:alog"), writes=("pb0",))
        S.op("act", lambda e: e.activation(d16[0:64, :], pb[0][0:64, :16], AF.Exp, scale=63.0),
             reads=("pb0",), writes=("c:d16",))
        S.op("pe", lambda e: e.matmul(pb[1][:, 0:16], ones1[0:1, :],
                                      alog[0:1, 1:32:2], start=True, stop=True),
             reads=("c:ones1", "c:alog"), writes=("pb1",))
        S.op("act", lambda e: e.activation(d16[64:128, :], pb[1][64:128, :16], AF.Exp, scale=63.0),
             reads=("pb1", "c:d16"), writes=("c:d16",))
        S.op("dve", lambda e: e.memset(zeros64[:, :], 0.0), writes=("c:zeros64",))

        SC = 1.0 / float(np.sqrt(HID))
        P0_RANGE = (0, 0) if DBG["skip_p0"] else (64, 128)
        ar = root.sub()
        wbufA = ar.alloc("p0wA", [128, 2048], BF16)
        wbufB = ar.alloc("p0wB", [128, 2048], BF16)
        stg = [ar.alloc(f"p0s{i}", [128, 2048], BF16) for i in range(2)]
        for hh in range(P0_RANGE[0]):
            par = hh % 2
            wb = wbufA if par == 0 else wbufB
            S.dma("qg", f"p0w{par}", (lambda hh, wb: lambda e: e.dma_start(
                out=wb[:, :], in_=w2_h[hh * 128:(hh + 1) * 128, :]))(hh, wb),
                writes=(f"p0w:{par}",))
            for dt_i in range(4):
                bi = (hh * 4 + dt_i) % 4
                S.op("pe", (lambda wb, dt_i, bi: lambda e: e.matmul(
                    pb[bi][:, :], h128[:, :], wb[:, dt_i * 512:(dt_i + 1) * 512],
                    start=True, stop=True))(wb, dt_i, bi),
                    reads=(f"p0w:{par}", "c:h128"), writes=(f"pb{bi}",))
                S.op("dve", (lambda dt_i, bi, par: lambda e: e.tensor_copy(
                    stg[par][:, dt_i * 512:(dt_i + 1) * 512], pb[bi][:, :]))(dt_i, bi, par),
                    reads=(f"pb{bi}", f"p0s:{par}"), writes=(f"p0s:{par}",))
            S.dma("qa", f"p0s{par}", (lambda hh, par: lambda e: e.dma_start(
                out=p1_d[hh * 128:(hh + 1) * 128, :], in_=stg[par][:, :]))(hh, par),
                reads=(f"p0s:{par}",), writes=("d:p1",))
        p1_v = p1_d.ap().rearrange("(hh jl) d -> hh jl d", jl=128)
        m2_v = m2_d.ap().rearrange("(jh jl) d -> jh jl d", jl=128)
        for jl in range(P0_RANGE[1]):
            par = jl % 2
            wb = wbufA if par == 0 else wbufB
            S.dma("qg", f"p0w{par}", (lambda jl, wb: lambda e: e.dma_start(
                out=wb[0:64, :], in_=p1_v[:, jl, :]))(jl, wb),
                reads=("d:p1",), writes=(f"p0w:{par}",))
            for dt_i in range(4):
                bi = (jl * 4 + dt_i) % 4
                S.op("pe", (lambda wb, dt_i, bi: lambda e: e.matmul(
                    pb[bi][0:64, :], h64[:, :], wb[0:64, dt_i * 512:(dt_i + 1) * 512],
                    start=True, stop=True))(wb, dt_i, bi),
                    reads=(f"p0w:{par}", "c:h64"), writes=(f"pb{bi}",))
                S.op("dve", (lambda dt_i, bi, par: lambda e: e.tensor_scalar(
                    stg[par][0:64, dt_i * 512:(dt_i + 1) * 512], pb[bi][0:64, :],
                    SC, None, ALU.mult))(dt_i, bi, par),
                    reads=(f"pb{bi}", f"p0s:{par}"), writes=(f"p0s:{par}",))
            S.dma("qa", f"p0s{par}", (lambda jl, par: lambda e: e.dma_start(
                out=m2_v[:, jl, :], in_=stg[par][0:64, :]))(jl, par),
                reads=(f"p0s:{par}",), writes=("d:m2",))

        def ln_stats(get_val, vtoks, W, eps, do_mean=True):
            vals = [get_val(g) for g in range(16)]
            for g in range(16):
                S.op("pe", (lambda g, v, W: lambda e: e.matmul(
                    pst[0:1, 0:W], onesb[:, 0:1], v,
                    start=(g == 0), stop=(g == 15)))(g, vals[g], W),
                    reads=vtoks(g) + ("c:onesb",), writes=("pst",))
            if do_mean:
                for g in range(16):
                    S.op("act", (lambda g, v, W: lambda e: e.activation(
                        sq16[:, g * 256:g * 256 + W], v, AF.Square))(g, vals[g], W),
                        reads=vtoks(g) + (f"sq16:{g}",), writes=(f"sq16:{g}",))
                for g in range(16):
                    S.op("pe", (lambda g, W: lambda e: e.matmul(
                        pb[3][0:1, 0:W], onesb[:, 0:1], sq16[:, g * 256:g * 256 + W],
                        start=(g == 0), stop=(g == 15)))(g, W),
                        reads=(f"sq16:{g}", "c:onesb"), writes=("pb3",))
                S.op("dve", (lambda W: lambda e: e.tensor_scalar(
                    rows[0:1, 0:W], pst[0:1, 0:W], 1.0 / D, None, ALU.mult))(W),
                    reads=("pst", "r_mean"), writes=("r_mean",))
                S.op("dve", (lambda W: lambda e: e.tensor_scalar(
                    rows[0:1, 256:256 + W], pb[3][0:1, 0:W], 1.0 / D, None, ALU.mult))(W),
                    reads=("pb3", "r_msq"), writes=("r_msq",))
                S.op("dve", (lambda W: lambda e: e.tensor_mul(
                    rows[0:1, 512:512 + W], rows[0:1, 0:W], rows[0:1, 0:W]))(W),
                    reads=("r_mean", "r_m2"), writes=("r_m2",))
                S.op("dve", (lambda W: lambda e: e.tensor_sub(
                    rows[0:1, 768:768 + W], rows[0:1, 256:256 + W],
                    rows[0:1, 512:512 + W]))(W),
                    reads=("r_msq", "r_m2", "r_var"), writes=("r_var",))
                S.op("act", (lambda W, eps: lambda e: e.activation(
                    rows[0:1, 1024:1024 + W], rows[0:1, 768:768 + W], AF.Sqrt,
                    bias=eps))(W, eps),
                    reads=("r_var", "r_sd", "c:epsc"), writes=("r_sd",))
                S.op("dve", (lambda W: lambda e: e.reciprocal(
                    rows[0:1, 1280:1280 + W], rows[0:1, 1024:1024 + W]))(W),
                    reads=("r_sd", "r_rstd"), writes=("r_rstd",))
                S.op("pe", (lambda W: lambda e: e.matmul(
                    pst[:, 0:W], ones1[0:1, :], rows[0:1, 0:W], start=True, stop=True))(W),
                    reads=("r_mean", "c:ones1", "pst"), writes=("pst",))
                S.op("pe", (lambda W: lambda e: e.matmul(
                    pb[3][:, 0:W], ones1[0:1, :], rows[0:1, 1280:1280 + W],
                    start=True, stop=True))(W),
                    reads=("r_rstd", "c:ones1", "pb3"), writes=("pb3",))
            else:
                S.op("dve", (lambda W: lambda e: e.tensor_scalar(
                    rows[0:1, 256:256 + W], pst[0:1, 0:W], 1.0 / D, None, ALU.mult))(W),
                    reads=("pst", "r_msq"), writes=("r_msq",))
                S.op("act", (lambda W, eps: lambda e: e.activation(
                    rows[0:1, 1024:1024 + W], rows[0:1, 256:256 + W], AF.Sqrt,
                    bias=eps))(W, eps),
                    reads=("r_msq", "r_sd", "c:epsc"), writes=("r_sd",))
                S.op("dve", (lambda W: lambda e: e.reciprocal(
                    rows[0:1, 1280:1280 + W], rows[0:1, 1024:1024 + W]))(W),
                    reads=("r_sd", "r_rstd"), writes=("r_rstd",))
                S.op("pe", (lambda W: lambda e: e.matmul(
                    pb[3][:, 0:W], ones1[0:1, :], rows[0:1, 1280:1280 + W],
                    start=True, stop=True))(W),
                    reads=("r_rstd", "c:ones1", "pb3"), writes=("pb3",))

        for half in DBG["halves"]:
            S.barrier()
            ar = root.sub()
            lnu = ar.alloc("lnu", [128, 16 * (Th + 2)], BF16)
            wbufA = ar.alloc("p1wA", [128, 16 * 512], BF16)
            wbufB = ar.alloc("p1wB", [128, 16 * 512], BF16)
            uwin2 = [ar.alloc(f"uwin{i}", [128, 16 * 256], F32) for i in range(2)]
            zxA = ar.alloc("zxA", [128, Th + 2], F32)
            zxB = ar.alloc("zxB", [128, Th + 2], F32)
            cb0 = ar.alloc("cb0", [128, Th], BF16)
            cb1 = ar.alloc("cb1", [128, Th], BF16)
            cb2 = ar.alloc("cb2", [128, Th], BF16)
            ucol0 = half * Th
            ncols = Th + 2
            n_win = 0
            for wi, (woff, W) in enumerate(_splits(ncols, 256)):
                wp = wi % 2
                uwin = uwin2[wp]
                n_win += 1
                for g in range(16):
                    S.dma("qs", f"uw{wp}", (lambda g, woff, W, ucol0, uwin: lambda e: e.dma_start(
                        out=uwin[:, g * 256:g * 256 + W],
                        in_=u_h[g * 128:(g + 1) * 128, ucol0 + woff:ucol0 + woff + W]))(g, woff, W, ucol0, uwin),
                        reads=(f"uw:{wp}",), writes=(f"uw:{wp}",))
                    S.op("act", (lambda g, W, uwin: lambda e: e.activation(
                        st16[:, g * 256:g * 256 + W], uwin[:, g * 256:g * 256 + W],
                        AF.Identity))(g, W, uwin),
                        reads=(f"uw:{wp}", f"st16:{g}"), writes=(f"st16:{g}",))
                ln_stats(lambda g: st16[:, g * 256:g * 256 + W],
                         lambda g: (f"st16:{g}",), W, epsc[0:1, 0:1])
                for g in range(16):
                    S.op("dve", (lambda g, W, uwin: lambda e: e.tensor_sub(
                        zn[:, (g % 2) * 256:(g % 2) * 256 + W],
                        uwin[:, g * 256:g * 256 + W], pst[:, 0:W]))(g, W, uwin),
                        reads=(f"uw:{wp}", "pst", f"zn:{g % 2}"), writes=(f"zn:{g % 2}",))
                    S.op("dve", (lambda g, W, woff: lambda e: e.tensor_mul(
                        lnu[:, g * (Th + 2) + woff:g * (Th + 2) + woff + W],
                        zn[:, (g % 2) * 256:(g % 2) * 256 + W], pb[3][:, 0:W]))(g, W, woff),
                        reads=(f"zn:{g % 2}", "pb3", f"lnw:{wi}"), writes=(f"lnw:{wi}",))
            def lnu_tok_range(toff, tw):
                w0 = toff // 256
                w1 = (toff + tw - 1) // 256
                return tuple(f"lnw:{w}" for w in range(w0, min(w1, n_win - 1) + 1))
            ttiles = _splits(ncols, 512)
            for osl in range(16):
                par = osl % 2
                wb = wbufA if par == 0 else wbufB
                win_v = win_h.ap().rearrange("(kc p) o -> p kc o", p=128)
                S.dma("qg", f"p1w{par}", (lambda osl, wb: lambda e: e.dma_start(
                    out=wb[:, :].rearrange("p (kc o) -> p kc o", kc=16),
                    in_=win_v[:, :, osl * 512:(osl + 1) * 512]))(osl, wb),
                    reads=(f"p1w:{par}",), writes=(f"p1w:{par}",))
                for osub in range(4):
                    og = osl * 512 + osub * 128
                    zpar = (osl * 4 + osub) % 2
                    zx = zxA if zpar == 0 else zxB
                    zx_tok = f"zx:{zpar}"
                    for ti, (toff, tw) in enumerate(ttiles):
                        bi = (osub * len(ttiles) + ti) % 4
                        for k in range(16):
                            S.op("pe", (lambda wb, osub, k, toff, tw, bi: lambda e: e.matmul(
                                pb[bi][:, 0:tw],
                                wb[:, k * 512 + osub * 128:k * 512 + osub * 128 + 128],
                                lnu[:, k * (Th + 2) + toff:k * (Th + 2) + toff + tw],
                                start=(k == 0), stop=(k == 15)))(wb, osub, k, toff, tw, bi),
                                reads=(f"p1w:{par}",) + lnu_tok_range(toff, tw), writes=(f"pb{bi}",))
                        S.op("dve", (lambda bi, toff, tw, zx: lambda e: e.tensor_copy(
                            zx[:, toff:toff + tw], pb[bi][:, 0:tw]))(bi, toff, tw, zx),
                            reads=(f"pb{bi}", zx_tok), writes=(zx_tok,))
                    if og < D:
                        S.op("act", (lambda zx: lambda e: e.activation(
                            cb0[:, 0:Th], zx[:, 1:Th + 1], AF.Silu))(zx),
                            reads=(zx_tok, "cb0"), writes=("cb0",))
                        S.dma("qa", "scb0", (lambda og, half: lambda e: e.dma_start(
                            out=zs_d[og:og + 128, half * Th:half * Th + Th],
                            in_=cb0[:, 0:Th]))(og, half),
                            reads=("cb0",), writes=("d:zs",))
                    else:
                        c0 = ((og - D) // 128) * 4
                        S.op("dve", (lambda zx, c0: lambda e: e.tensor_scalar(
                            cb0[:, 0:Th], zx[:, 0:Th], convp[:, c0:c0 + 1], None, ALU.mult))(zx, c0),
                            reads=(zx_tok, "c:convp", "cb0"), writes=("cb0",))
                        S.op("dve", (lambda zx, c0: lambda e: e.tensor_scalar(
                            cb1[:, 0:Th], zx[:, 1:Th + 1], convp[:, c0 + 1:c0 + 2], None, ALU.mult))(zx, c0),
                            reads=(zx_tok, "c:convp", "cb1"), writes=("cb1",))
                        S.op("dve", (lambda zx, c0: lambda e: e.tensor_scalar(
                            cb2[:, 0:Th], zx[:, 2:Th + 2], convp[:, c0 + 2:c0 + 3], None, ALU.mult))(zx, c0),
                            reads=(zx_tok, "c:convp", "cb2"), writes=("cb2",))
                        S.op("dve", lambda e: e.tensor_add(cb0[:, 0:Th], cb0[:, 0:Th], cb1[:, 0:Th]),
                             reads=("cb0", "cb1"), writes=("cb0",))
                        S.op("dve", lambda e: e.tensor_add(cb0[:, 0:Th], cb0[:, 0:Th], cb2[:, 0:Th]),
                             reads=("cb0", "cb2"), writes=("cb0",))
                        S.op("act", (lambda c0: lambda e: e.activation(
                            cb1[:, 0:Th], cb0[:, 0:Th], AF.Silu, bias=convp[:, c0 + 3:c0 + 4]))(c0),
                            reads=("cb0", "c:convp", "cb1"), writes=("cb1",))
                        S.dma("qa", "scb1", (lambda og, half: lambda e: e.dma_start(
                            out=xbc_d[og - D:og - D + 128, half * Th:half * Th + Th],
                            in_=cb1[:, 0:Th]))(og, half),
                            reads=("cb1",), writes=("d:xbc",))

        if DBG["skip_rest"]:
            S.op("dve", lambda e: e.tensor_copy(zn[0:1, 0:1], zn[0:1, 0:1]),
                 writes=("fence",),
                 extra_waits=tuple((s, c) for s, c in S.count.items()
                                   if s.startswith("D:")))
            S.emit(block, lambda s: es.enter_context(
                nc.semaphore(f"sem_{s.replace(':', '_')}")))
            return nc
        S.barrier()
        p2ar = root.sub()
        wo = p2ar.alloc("wo", [128, 16 * 2048], BF16)
        st16b = p2ar.alloc("st16b", [64, 32 * ngc], BF16)
        sin16b = p2ar.alloc("sin16b", [64, 32], BF16)
        spar_a = p2ar.sub()
        cs_t = spar_a.alloc("cs_t", [128, 16 * ngc], F32)
        stt_t = spar_a.alloc("stt_t", [128, 16 * ngc], F32)
        st16_t = spar_a.alloc("st16_t", [128, 16 * ngc], BF16)
        s63 = spar_a.alloc("s63", [128, 16], F32)
        sinb = spar_a.alloc("sinb", [128, 64], F32)
        sin_t = spar_a.alloc("sin_t", [128, 16], F32)
        sin16 = spar_a.alloc("sin16", [128, 16], BF16)
        corr = spar_a.alloc("corr", [128, CHUNK], F32)
        spx = [spar_a.alloc(f"spx{i}", [128, T], BF16) for i in range(2)]
        spb = [spar_a.alloc(f"spb{i}", [128, T], BF16) for i in range(2)]
        spP = [spar_a.alloc(f"spP{i}", [128, T], F32) for i in range(2)]
        S.dma("qg", "wo", lambda e: e.dma_start(
            out=wo[:, :].rearrange("p (kc o) -> p kc o", kc=16),
            in_=wout_h.ap().rearrange("(kc p) o -> p kc o", p=128)),
            writes=("woR",))
        wo_toks = ("woR",)
        for g in range(16):
            par = g % 2
            S.dma("qs", f"spx{par}", (lambda g, par: lambda e: e.dma_start(
                out=spx[par][:, :], in_=xbc_d[g * 128:(g + 1) * 128, :]))(g, par),
                reads=(f"spx:{par}", "d:xbc"), writes=(f"spx:{par}",))
            S.dma("qs", f"spb{par}", (lambda g, par: lambda e: e.dma_start(
                out=spb[par][:, :], in_=xbc_d[D + g * 128:D + (g + 1) * 128, :]))(g, par),
                reads=(f"spb:{par}", "d:xbc"), writes=(f"spb:{par}",))
            S.op("dve", (lambda g, par: lambda e: e.tensor_mul(
                spP[par][:, :], spx[par][:, :], spb[par][:, :]))(g, par),
                reads=(f"spx:{par}", f"spb:{par}", f"spP:{par}"), writes=(f"spP:{par}",))
            S.op("dve", (lambda g, par: lambda e: e.tensor_reduce(
                cs_t[:, g * ngc:(g + 1) * ngc],
                spP[par][:, :].rearrange("p (c t) -> p c t", t=CHUNK),
                mybir.AxisListType.X, ALU.add))(g, par),
                reads=(f"spP:{par}",), writes=(f"cs:{g}",))
            S.op("dve", (lambda g: lambda e: e.tensor_tensor_scan(
                stt_t[:, g * ngc:(g + 1) * ngc],
                d16[:, g:g + 1].broadcast_to((128, ngc)),
                cs_t[:, g * ngc:(g + 1) * ngc], 0.0, ALU.mult, ALU.add))(g),
                reads=(f"cs:{g}", "c:d16", f"stt:{g}"), writes=(f"stt:{g}",))
        S.op("dve", lambda e: e.tensor_copy(
            s63[:, :], stt_t[:, :].rearrange("p (g c) -> p g c", c=ngc)[:, :, ngc - 1]),
            reads=tuple(f"stt:{g}" for g in range(16)), writes=("s63",))
        S.op("dve", lambda e: e.memset(sinb[:, :], 0.0), writes=("sinb",))
        for b in range(4):
            S.op("dve", (lambda b: lambda e: e.tensor_scalar(
                sinb[:, b * 16:(b + 1) * 16], s63[:, :], wsel[:, b:b + 1], None, ALU.mult))(b),
                reads=("s63", "c:wsel", "sinb"), writes=("sinb",))
        S.dma("qa", "scc", lambda e: e.dma_start(
            out=cc_in.ap().rearrange("b (r g) -> b r g", r=128).transpose([1, 0, 2]),
            in_=sinb[:, :].rearrange("p (b g) -> p b g", b=4)),
            reads=("sinb",), writes=("d:ccin",))
        if DBG["no_cc"]:
            S.dma("qs", "ccX", lambda e: e.dma_start(out=cc_out.ap(), in_=cc_in.ap()),
                  reads=("d:ccin",), writes=("d:ccout",))
        else:
            S.op("cc", lambda e: e.collective_compute(
                "AllReduce", ALU.add, replica_groups=[list(range(ncores))],
                ins=[cc_in.ap().opt()], outs=[cc_out.ap().opt()]),
                reads=("d:ccin",), writes=("d:ccout",), inc=1)
        S.dma("qs", "rcc", lambda e: e.dma_start(
            out=sinb[:, :].rearrange("p (b g) -> p b g", b=4),
            in_=cc_out.ap().rearrange("b (r g) -> b r g", r=128).transpose([1, 0, 2])),
            reads=("d:ccout", "sinb"), writes=("sinb",))
        S.op("dve", lambda e: e.tensor_scalar(
            sin_t[:, :], sinb[:, 0:16], ohsel[:, 0:1], None, ALU.mult),
            reads=("sinb", "c:ohsel", "sin"), writes=("sin",))
        for b in range(1, 4):
            S.op("dve", (lambda b: lambda e: e.scalar_tensor_tensor(
                sin_t[:, :], sinb[:, b * 16:(b + 1) * 16], ohsel[:, b:b + 1],
                sin_t[:, :], ALU.mult, ALU.add))(b),
                reads=("sinb", "c:ohsel", "sin"), writes=("sin",))
        for g in range(16):
            S.op("dve", (lambda g: lambda e: e.tensor_tensor_scan(
                corr[:, 0:ngc], d16[:, g:g + 1].broadcast_to((128, ngc)),
                zeros64[:, 0:ngc], sin_t[:, g:g + 1], ALU.mult, ALU.add))(g),
                reads=("sin", "c:d16", "c:zeros64", "corrtok"), writes=("corrtok",))
            S.op("dve", (lambda g: lambda e: e.tensor_add(
                stt_t[:, g * ngc:(g + 1) * ngc], stt_t[:, g * ngc:(g + 1) * ngc],
                corr[:, 0:ngc]))(g),
                reads=("corrtok", f"stt:{g}"), writes=(f"stt:{g}",))
        S.op("dve", lambda e: e.tensor_copy(st16_t[:, :], stt_t[:, :]),
             reads=tuple(f"stt:{g}" for g in range(16)), writes=("st16t",))
        S.op("dve", lambda e: e.tensor_copy(sin16[:, :], sin_t[:, :]),
             reads=("sin",), writes=("sin16",))

        if DBG["sprint_only"]:
            S.op("dve", lambda e: e.tensor_copy(zn[0:1, 0:1], zn[0:1, 0:1]),
                 writes=("fence",),
                 extra_waits=tuple((s, c) for s, c in S.count.items()
                                   if s.startswith("D:")))
            S.emit(block, lambda s: es.enter_context(
                nc.semaphore(f"sem_{s.replace(':', '_')}")))
            return nc
        S.dma("qs", "s16b", lambda e: e.dma_start(
            out=st16b[0:64, 0:16 * ngc], in_=st16_t[0:64, :]),
            reads=("st16t",), writes=("st16b",))
        S.dma("qs", "s16b", lambda e: e.dma_start(
            out=st16b[0:64, 16 * ngc:32 * ngc], in_=st16_t[64:128, :]),
            reads=("st16t",), writes=("st16b",))
        S.dma("qs", "s16c", lambda e: e.dma_start(
            out=sin16b[0:64, 0:16], in_=sin16[0:64, :]),
            reads=("sin16",), writes=("sin16b",))
        S.dma("qs", "s16c", lambda e: e.dma_start(
            out=sin16b[0:64, 16:32], in_=sin16[64:128, :]),
            reads=("sin16",), writes=("sin16b",))

        S.barrier()
        ar = p2ar.sub()
        xg = [ar.alloc(f"xg{i}", [128, 16 * 128], BF16) for i in range(1)]
        zg = [ar.alloc(f"zg{i}", [128, 16 * 128], BF16) for i in range(2)]
        bg = [ar.alloc(f"bg{i}", [64, 32 * 128], BF16) for i in range(2)]
        cg = [ar.alloc(f"cg{i}", [64, 32 * 128], BF16) for i in range(2)]
        xTc = [ar.alloc(f"xTc{i}", [64, 2 * 2048], BF16) for i in range(2)]
        gms = ar.alloc("gms", [64, 512], BF16)
        ygrp = ar.alloc("ygrp", [128, 16 * 128], F32)
        ynb = ar.alloc("ynb", [128, 16 * 512], BF16)
        mo16 = ar.alloc("mo16", [128, 16 * 256], BF16)
        x1b = ar.alloc("x1b", [128, 16 * 256], F32)
        ut = [ar.alloc(f"ut{i}", [128, 256], F32) for i in range(1)]

        pend_gate = []
        for sgi, (sgoff, sgw) in enumerate(_splits(T, 512)):
            ngrp = sgw // 128
            for grp in range(ngrp):
                goff = sgoff + grp * 128
                gpar = grp % 2
                xbc_g = xbc_d.ap().rearrange("(g p) t -> p g t", p=128)
                xbc_h = xbc_d.ap().rearrange("(h n) t -> n h t", n=64)
                zs_g = zs_d.ap().rearrange("(g p) t -> p g t", p=128)
                S.dma("qs", "xg0", (lambda goff: lambda e: e.dma_start(
                    out=xg[0][:, :].rearrange("p (g t) -> p g t", g=16),
                    in_=xbc_g[:, 0:16, goff:goff + 128]))(goff),
                    reads=("xg:0", "d:xbc"), writes=("xg:0",))
                S.dma("qs", f"zg{gpar}", (lambda goff, gpar: lambda e: e.dma_start(
                    out=zg[gpar][:, :].rearrange("p (g t) -> p g t", g=16),
                    in_=zs_g[:, 0:16, goff:goff + 128]))(goff, gpar),
                    reads=(f"zg:{gpar}", "d:zs"), writes=(f"zg:{gpar}",))
                S.dma("qs", f"bg{gpar}", (lambda goff, gpar: lambda e: e.dma_start(
                    out=bg[gpar][0:64, :].rearrange("p (h t) -> p h t", h=32),
                    in_=xbc_h[:, 32:64, goff:goff + 128]))(goff, gpar),
                    reads=(f"bg:{gpar}", "d:xbc"), writes=(f"bg:{gpar}",))
                S.dma("qs", f"cg{gpar}", (lambda goff, gpar: lambda e: e.dma_start(
                    out=cg[gpar][0:64, :].rearrange("p (h t) -> p h t", h=32),
                    in_=xbc_h[:, 64:96, goff:goff + 128]))(goff, gpar),
                    reads=(f"cg:{gpar}", "d:xbc"), writes=(f"cg:{gpar}",))
                if "tp" in DBG["parts"]:
                    for ci in range(2):
                        for g in range(16):
                            slot = g % 8
                            S.op("pe", (lambda g, ci, gpar, slot: lambda e: e.transpose(
                                pT[0:64, slot * 128:(slot + 1) * 128],
                                xg[0][:, g * 128 + ci * 64:g * 128 + ci * 64 + 64],
                                ident[:, :]))(g, ci, gpar, slot),
                                reads=("xg:0", "c:ident", "pT"), writes=("pT",))
                            if g % 2 == 1:
                                S.op("dve", (lambda g, ci, gpar, slot: lambda e: e.tensor_copy(
                                    xTc[gpar][0:64, ci * 2048 + (g - 1) * 128:ci * 2048 + (g + 1) * 128],
                                    pT[0:64, (slot - 1) * 128:(slot + 1) * 128]))(g, ci, gpar, slot),
                                    reads=("pT", f"xT:{gpar}"), writes=(f"xT:{gpar}",))
                ssd_banks = [(pS[0], "pS0"), (pS[1], "pS1"), (pb[2], "pb2")]

                def emit_G(gh):
                    bank, btok = ssd_banks[gh % 3]
                    if "g" not in DBG["parts"]:
                        return
                    for hi in range(2):
                        h = 2 * gh + hi
                        for ci in range(2):
                            S.op("pe", (lambda h, hi, ci, gpar, bank: lambda e: e.matmul(
                                bank[0:64, (hi * 2 + ci) * 64:(hi * 2 + ci) * 64 + 64],
                                bg[gpar][0:64, h * 128 + ci * 64:h * 128 + ci * 64 + 64],
                                cg[gpar][0:64, h * 128 + ci * 64:h * 128 + ci * 64 + 64],
                                start=True, stop=True))(h, hi, ci, gpar, bank),
                                reads=(f"bg:{gpar}", f"cg:{gpar}", btok),
                                writes=(btok,))

                def emit_rest(gh):
                    bank, btok = ssd_banks[gh % 3]
                    c_base = goff // CHUNK
                    if "ge" in DBG["parts"]:
                        S.op("dve", (lambda gh, bank: lambda e: e.tensor_mul(
                            gms[0:64, (gh % 2) * 256:(gh % 2) * 256 + 256],
                            bank[0:64, 0:256], mask4[0:64, :]))(gh, bank),
                            reads=(btok, "c:mask4", f"gms:{gh % 2}"), writes=(f"gms:{gh % 2}",))
                    for hi in range(2 if ("y1" in DBG["parts"] or "y2" in DBG["parts"]) else 0):
                        h = 2 * gh + hi
                        hb = hi * 64
                        for ci in range(2):
                            c_loc = c_base + ci
                            if c_loc == 0:
                                prevap = sin16b[0:64, (h % 2) * 16 + h // 2:(h % 2) * 16 + h // 2 + 1]
                                ptok = "sin16b"
                            else:
                                pcol = ((h % 2) * 16 + h // 2) * ngc + c_loc - 1
                                prevap = st16b[0:64, pcol:pcol + 1]
                                ptok = "st16b"
                            if "y1" in DBG["parts"]:
                                S.op("pe", (lambda h, hi, hb, ci, gh, gpar, bank: lambda e: e.matmul(
                                    bank[hb:hb + 64, 256 + ci * 64:256 + ci * 64 + 64],
                                    xTc[gpar][0:64, ci * 2048 + h * 64:ci * 2048 + h * 64 + 64],
                                    gms[0:64, (gh % 2) * 256 + (hi * 2 + ci) * 64:
                                        (gh % 2) * 256 + (hi * 2 + ci) * 64 + 64],
                                    start=True, stop=("y2" not in DBG["parts"])))(h, hi, hb, ci, gh, gpar, bank),
                                    reads=(f"gms:{gh % 2}", f"xT:{gpar}", btok), writes=(btok,))
                            if "y2" in DBG["parts"]:
                                S.op("pe", (lambda h, hb, ci, gpar, prevap, bank: lambda e: e.matmul(
                                    bank[hb:hb + 64, 256 + ci * 64:256 + ci * 64 + 64],
                                    prevap.broadcast_to((64, 64)),
                                    cg[gpar][0:64, h * 128 + ci * 64:h * 128 + ci * 64 + 64],
                                    start=("y1" not in DBG["parts"]), stop=True))(h, hb, ci, gpar, prevap, bank),
                                    reads=(ptok, f"cg:{gpar}", btok), writes=(btok,))
                    if "y1" in DBG["parts"] or "y2" in DBG["parts"]:
                        S.op("dve", (lambda gh, gpar, bank: lambda e: e.tensor_mul(
                            ygrp[:, gh * 128:(gh + 1) * 128], bank[:, 256:384],
                            zg[gpar][:, gh * 128:(gh + 1) * 128]))(gh, gpar, bank),
                            reads=(btok, f"zg:{gpar}", f"yg:{gh}"), writes=(f"yg:{gh}",))

                for gh in range(16):
                    emit_G(gh)
                    if gh >= 2:
                        emit_rest(gh - 2)
                emit_rest(14)
                emit_rest(15)
                def gate_seg(grp, gpar):
                    for g in range(16 if "gate" in DBG["parts"] else 0):
                        S.op("act", (lambda g: lambda e: e.activation(
                            sq16[:, g * 256:g * 256 + 128], ygrp[:, g * 128:(g + 1) * 128],
                            AF.Square))(g),
                            reads=(f"yg:{g}", f"sq16:{g}"), writes=(f"sq16:{g}",))
                    for g in range(16 if "gate" in DBG["parts"] else 0):
                        S.op("pe", (lambda g: lambda e: e.matmul(
                            pst[0:1, 0:128], onesb[:, 0:1], sq16[:, g * 256:g * 256 + 128],
                            start=(g == 0), stop=(g == 15)))(g),
                            reads=(f"sq16:{g}", "c:onesb"), writes=("pst",))
                    S.op("dve", lambda e: e.tensor_scalar(
                        rows[0:1, 256:384], pst[0:1, 0:128], 1.0 / D, None, ALU.mult),
                        reads=("pst", "r_msq"), writes=("r_msq",))
                    S.op("act", lambda e: e.activation(
                        rows[0:1, 1024:1152], rows[0:1, 256:384], AF.Sqrt, bias=epsc[0:1, 1:2]),
                        reads=("r_msq", "r_sd", "c:epsc"), writes=("r_sd",))
                    S.op("dve", lambda e: e.reciprocal(
                        rows[0:1, 1280:1408], rows[0:1, 1024:1152]),
                        reads=("r_sd", "r_rstd"), writes=("r_rstd",))
                    S.op("pe", lambda e: e.matmul(
                        pb[3][:, 0:128], ones1[0:1, :], rows[0:1, 1280:1408],
                        start=True, stop=True),
                        reads=("r_rstd", "c:ones1", "pb3"), writes=("pb3",))
                    for g in range(16):
                        S.op("dve", (lambda g, grp: lambda e: e.scalar_tensor_tensor(
                            ynb[:, g * 512 + grp * 128:g * 512 + grp * 128 + 128],
                            ygrp[:, g * 128:(g + 1) * 128], rmsp[:, g:g + 1],
                            pb[3][:, 0:128], ALU.mult, ALU.mult))(g, grp),
                            reads=(f"yg:{g}", "c:rmsp", "pb3", f"yn:{g}"), writes=(f"yn:{g}",))
                pend_gate.append((grp, gpar))
                gate_seg(*pend_gate.pop(0))
            while pend_gate:
                gate_seg(*pend_gate.pop(0))
            yn_toks = tuple(f"yn:{g}" for g in range(16))
            for wi, (woff, W) in enumerate(_splits(sgw, 256)):
                for od in range(16):
                    bi = od % 2
                    for k in range(16):
                        S.op("pe", (lambda od, k, bi, woff, W: lambda e: e.matmul(
                            pb[bi][:, 0:W],
                            wo[:, k * 2048 + od * 128:k * 2048 + od * 128 + 128],
                            ynb[:, k * 512 + woff:k * 512 + woff + W],
                            start=(k == 0), stop=(k == 15)))(od, k, bi, woff, W),
                            reads=wo_toks + yn_toks + (f"pb{bi}",), writes=(f"pb{bi}",))
                    S.op("act", (lambda od, bi, W: lambda e: e.activation(
                        mo16[:, od * 256:od * 256 + W], pb[bi][:, 0:W], AF.Identity))(od, bi, W),
                        reads=(f"pb{bi}", f"mo:{od}"), writes=(f"mo:{od}",))
                for g in range(4):
                    pass
                ln_stats(lambda g: mo16[:, g * 256:g * 256 + W],
                         lambda g: (f"mo:{g}",), W, epsc[0:1, 0:1])
                for g in range(16):
                    up = 0
                    S.dma("qs", f"ut{up}", (lambda g, up, sgoff, woff, W: lambda e: e.dma_start(
                        out=ut[up][:, 0:W],
                        in_=u_h[g * 128:(g + 1) * 128,
                                sgoff + woff + 1:sgoff + woff + 1 + W]))(g, up, sgoff, woff, W),
                        reads=(f"ut:{up}",), writes=(f"ut:{up}",))
                    S.op("dve", (lambda g, W: lambda e: e.tensor_sub(
                        zn[:, (g % 2) * 256:(g % 2) * 256 + W],
                        mo16[:, g * 256:g * 256 + W], pst[:, 0:W]))(g, W),
                        reads=(f"mo:{g}", "pst", f"zn:{g % 2}"), writes=(f"zn:{g % 2}",))
                    S.op("dve", (lambda g, W: lambda e: e.tensor_mul(
                        zn[:, (g % 2) * 256:(g % 2) * 256 + W],
                        zn[:, (g % 2) * 256:(g % 2) * 256 + W], pb[3][:, 0:W]))(g, W),
                        reads=(f"zn:{g % 2}", "pb3"), writes=(f"zn:{g % 2}",))
                    S.op("dve", (lambda g, up, W: lambda e: e.tensor_add(
                        x1b[:, g * 256:g * 256 + W],
                        zn[:, (g % 2) * 256:(g % 2) * 256 + W], ut[up][:, 0:W]))(g, up, W),
                        reads=(f"zn:{g % 2}", f"ut:{up}", f"x1:{g}", "sx1R"), writes=(f"x1:{g}",))
                    S.dma("qa", "sx1", (lambda g, sgoff, woff, W: lambda e: e.dma_start(
                        out=x1_d[g * 128:(g + 1) * 128,
                                 sgoff + woff:sgoff + woff + W],
                        in_=x1b[:, g * 256:g * 256 + W]))(g, sgoff, woff, W),
                        reads=(f"x1:{g}",), writes=("d:x1", "sx1R"))
                    S.op("act", (lambda g, W: lambda e: e.activation(
                        st16[:, g * 256:g * 256 + W], x1b[:, g * 256:g * 256 + W],
                        AF.Identity))(g, W),
                        reads=(f"x1:{g}", f"st16:{g}", "slxR"), writes=(f"st16:{g}",))
                ln_stats(lambda g: st16[:, g * 256:g * 256 + W],
                         lambda g: (f"st16:{g}",), W, epsc[0:1, 0:1])
                for g in range(16):
                    S.op("dve", (lambda g, W: lambda e: e.tensor_sub(
                        zn[:, (g % 2) * 256:(g % 2) * 256 + W],
                        x1b[:, g * 256:g * 256 + W], pst[:, 0:W]))(g, W),
                        reads=(f"x1:{g}", "pst", f"zn:{g % 2}"), writes=(f"zn:{g % 2}",))
                    S.op("dve", (lambda g, W: lambda e: e.tensor_mul(
                        st16[:, g * 256:g * 256 + W],
                        zn[:, (g % 2) * 256:(g % 2) * 256 + W], pb[3][:, 0:W]))(g, W),
                        reads=(f"zn:{g % 2}", "pb3", f"st16:{g}"), writes=(f"st16:{g}",))
                    S.dma("qa", "slx", (lambda g, sgoff, woff, W: lambda e: e.dma_start(
                        out=lnx1_d[g * 128:(g + 1) * 128,
                                   sgoff + woff:sgoff + woff + W],
                        in_=st16[:, g * 256:g * 256 + W]))(g, sgoff, woff, W),
                        reads=(f"st16:{g}",), writes=("d:lnx1", "slxR"))

        if DBG["skip_p3"]:
            S.op("dve", lambda e: e.tensor_copy(zn[0:1, 0:1], zn[0:1, 0:1]),
                 writes=("fence",),
                 extra_waits=tuple((s, c) for s, c in S.count.items()
                                   if s.startswith("D:")))
            S.emit(block, lambda s: es.enter_context(
                nc.semaphore(f"sem_{s.replace(':', '_')}")))
            return nc
        S.barrier()
        ar = root.sub()
        lnx1r = ar.alloc("lnx1r", [128, 16 * T], BF16)
        wbufA = ar.alloc("p3wA", [128, 16 * 512], BF16)
        wbufB = ar.alloc("p3wB", [128, 16 * 512], BF16)
        gs = [ar.alloc(f"gs{i}", [128, 512], BF16) for i in range(2)]
        hs = [ar.alloc(f"hs{i}", [128, 512], BF16) for i in range(2)]
        for g in range(16):
            S.dma("qs", "lx", (lambda g: lambda e: e.dma_start(
                out=lnx1r[:, g * T:(g + 1) * T],
                in_=lnx1_d[g * 128:(g + 1) * 128, :]))(g),
                reads=("d:lnx1",), writes=("lxR",))
        lx_toks = ("lxR",)
        t3 = _splits(T, 512)
        w1_v = w1_h.ap().rearrange("(kc p) o -> p kc o", p=128)
        w3_v = w3_h.ap().rearrange("(kc p) o -> p kc o", p=128)
        for osl in range(16):
            S.dma("qg", "p3w1", (lambda osl: lambda e: e.dma_start(
                out=wbufA[:, :].rearrange("p (kc o) -> p kc o", kc=16),
                in_=w1_v[:, :, osl * 512:(osl + 1) * 512]))(osl),
                reads=("p3w1R",), writes=("p3w1R",))
            S.dma("qg", "p3w3", (lambda osl: lambda e: e.dma_start(
                out=wbufB[:, :].rearrange("p (kc o) -> p kc o", kc=16),
                in_=w3_v[:, :, osl * 512:(osl + 1) * 512]))(osl),
                reads=("p3w3R",), writes=("p3w3R",))
            for osub in range(4):
                og = osl * 512 + osub * 128
                for ti, (toff, tw) in enumerate(t3):
                    bi1 = (2 * (osub * len(t3) + ti)) % 4
                    bi3 = bi1 + 1 if bi1 % 2 == 0 else bi1 - 1
                    bi3 = (bi1 + 1) % 4
                    for k in range(16):
                        S.op("pe", (lambda osub, k, toff, tw, bi1: lambda e: e.matmul(
                            pb[bi1][:, 0:tw],
                            wbufA[:, k * 512 + osub * 128:k * 512 + osub * 128 + 128],
                            lnx1r[:, k * T + toff:k * T + toff + tw],
                            start=(k == 0), stop=(k == 15)))(osub, k, toff, tw, bi1),
                            reads=("p3w1R",) + lx_toks, writes=(f"pb{bi1}",))
                    for k in range(16):
                        S.op("pe", (lambda osub, k, toff, tw, bi3: lambda e: e.matmul(
                            pb[bi3][:, 0:tw],
                            wbufB[:, k * 512 + osub * 128:k * 512 + osub * 128 + 128],
                            lnx1r[:, k * T + toff:k * T + toff + tw],
                            start=(k == 0), stop=(k == 15)))(osub, k, toff, tw, bi3),
                            reads=("p3w3R",) + lx_toks, writes=(f"pb{bi3}",))
                    sp = ti % 2
                    S.op("act", (lambda bi1, tw, sp: lambda e: e.activation(
                        gs[sp][:, 0:tw], pb[bi1][:, 0:tw], AF.Silu))(bi1, tw, sp),
                        reads=(f"pb{bi1}", f"gs:{sp}"), writes=(f"gs:{sp}",))
                    S.op("dve", (lambda bi3, tw, sp: lambda e: e.tensor_mul(
                        hs[sp][:, 0:tw], pb[bi3][:, 0:tw], gs[sp][:, 0:tw]))(bi3, tw, sp),
                        reads=(f"pb{bi3}", f"gs:{sp}", f"hs:{sp}"), writes=(f"hs:{sp}",))
                    S.dma("qa", f"shid{sp}", (lambda og, toff, tw, sp: lambda e: e.dma_start(
                        out=hid_d[og:og + 128, toff:toff + tw],
                        in_=hs[sp][:, 0:tw]))(og, toff, tw, sp),
                        reads=(f"hs:{sp}",), writes=("d:hid",))

        S.barrier()
        ar = root.sub()
        pend_p3 = []
        hidb = ar.alloc("hidb", [128, 64 * 512], BF16)
        m2b = [ar.alloc(f"m2b{i}", [128, 64 * 128], BF16) for i in range(2)]
        fo16p = [ar.alloc(f"fo16{i}", [128, 16 * 512], BF16) for i in range(2)]
        x1t = [ar.alloc(f"x1t{i}", [128, 256], F32) for i in range(1)]
        outb = ar.alloc("outb", [128, 2 * 256], F32)
        m2_k = m2_d.ap().rearrange("(kc p) d -> p kc d", p=128)
        for ti, (toff, tw) in enumerate(t3):
            hid_v = hid_d.ap().rearrange("(hc p) t -> p hc t", p=128)
            for hh2 in range(2):
                S.dma("qs", "h3", (lambda hh2, toff, tw: lambda e: e.dma_start(
                    out=hidb[:, :].rearrange("p (hc t) -> p hc t", hc=64)[
                        :, hh2 * 32:(hh2 + 1) * 32, 0:tw],
                    in_=hid_v[:, hh2 * 32:(hh2 + 1) * 32, toff:toff + tw]))(hh2, toff, tw),
                    reads=("h3R", "d:hid"), writes=("h3R",))
            for dsub in range(16):
                wpar = dsub % 2
                S.dma("qa" if wpar == 0 else "qs", f"m2w{wpar}", (lambda dsub, wpar: lambda e: e.dma_start(
                    out=m2b[wpar][:, :].rearrange("p (kc d) -> p kc d", d=128),
                    in_=m2_k[:, :, dsub * 128:(dsub + 1) * 128]))(dsub, wpar),
                    reads=(f"m2w:{wpar}", "d:m2"), writes=(f"m2w:{wpar}",))
                bi = dsub % 2
                for k in range(64):
                    S.op("pe", (lambda k, wpar, bi, tw: lambda e: e.matmul(
                        pb[bi][:, 0:tw], m2b[wpar][:, k * 128:(k + 1) * 128],
                        hidb[:, k * 512:k * 512 + tw],
                        start=(k == 0), stop=(k == 63)))(k, wpar, bi, tw),
                        reads=(f"m2w:{wpar}", "h3R", f"pb{bi}"), writes=(f"pb{bi}",))
                tpar = ti % 2
                S.op("act", (lambda dsub, bi, tw, tpar: lambda e: e.activation(
                    fo16p[tpar][:, dsub * 512:dsub * 512 + tw], pb[bi][:, 0:tw],
                    AF.Identity))(dsub, bi, tw, tpar),
                    reads=(f"pb{bi}", f"fo:{tpar}:{dsub}"), writes=(f"fo:{tpar}:{dsub}",))
            def p3b_ln(toff, tw, tpar):
                for woff, W in _splits(tw, 256):
                    ln_stats((lambda g, tpar=tpar: fo16p[tpar][:, g * 512 + woff:g * 512 + woff + W]),
                         (lambda g, tpar=tpar: (f"fo:{tpar}:{g}",)), W, epsc[0:1, 0:1])
                    for g in range(16):
                        up = 0
                        S.dma("qs", f"x1t{up}", (lambda g, up, toff, woff, W: lambda e: e.dma_start(
                        out=x1t[up][:, 0:W],
                        in_=x1_d[g * 128:(g + 1) * 128,
                                 toff + woff:toff + woff + W]))(g, up, toff, woff, W),
                        reads=(f"x1t:{up}", "d:x1"), writes=(f"x1t:{up}",))
                        S.op("dve", (lambda g, woff, W, tpar: lambda e: e.tensor_sub(
                            zn[:, (g % 2) * 256:(g % 2) * 256 + W],
                            fo16p[tpar][:, g * 512 + woff:g * 512 + woff + W], pst[:, 0:W]))(g, woff, W, tpar),
                            reads=(f"fo:{tpar}:{g}", "pst", f"zn:{g % 2}"), writes=(f"zn:{g % 2}",))
                        S.op("dve", (lambda g, W: lambda e: e.tensor_mul(
                        zn[:, (g % 2) * 256:(g % 2) * 256 + W],
                        zn[:, (g % 2) * 256:(g % 2) * 256 + W], pb[3][:, 0:W]))(g, W),
                        reads=(f"zn:{g % 2}", "pb3"), writes=(f"zn:{g % 2}",))
                        S.op("dve", (lambda g, up, W: lambda e: e.tensor_add(
                        outb[:, (g % 2) * 256:(g % 2) * 256 + W],
                        zn[:, (g % 2) * 256:(g % 2) * 256 + W], x1t[up][:, 0:W]))(g, up, W),
                        reads=(f"zn:{g % 2}", f"x1t:{up}", f"ob:{g % 2}"), writes=(f"ob:{g % 2}",))
                        S.dma("qa", f"so{g % 2}", (lambda g, toff, woff, W: lambda e: e.dma_start(
                        out=out_h[g * 128:(g + 1) * 128,
                                  toff + woff:toff + woff + W],
                        in_=outb[:, (g % 2) * 256:(g % 2) * 256 + W]))(g, toff, woff, W),
                        reads=(f"ob:{g % 2}",), writes=("d:out",))


            pend_p3.append((toff, tw, ti % 2, p3b_ln))
            if len(pend_p3) > 1:
                a, b, c, f = pend_p3.pop(0)
                f(a, b, c)
        while pend_p3:
            a, b, c, f = pend_p3.pop(0)
            f(a, b, c)
        S.op("dve", lambda e: e.tensor_copy(zn[0:1, 0:1], zn[0:1, 0:1]),
             writes=("fence",),
             extra_waits=tuple((s, c) for s, c in S.count.items()
                               if s.startswith("D:")))
        S.emit(block, lambda s: es.enter_context(
            nc.semaphore(f"sem_{s.replace(':', '_')}")))

    return nc


_CACHE = {}


def _get_program(T, ncores=NCORES):
    key = (T, ncores)
    if key not in _CACHE:
        _CACHE[key] = build_program(T, ncores)
    return _CACHE[key]


def _host_inputs(W_in, W_out, conv_w, conv_b, A_log, rms_w, W1, W2, W3):
    c = {}
    c["win_t"] = np.ascontiguousarray(np.asarray(W_in).T).astype(np.float32)
    c["wout_t"] = np.ascontiguousarray(np.asarray(W_out).T).astype(np.float32)
    c["w1_t"] = np.ascontiguousarray(np.asarray(W1).T).astype(np.float32)
    c["w3_t"] = np.ascontiguousarray(np.asarray(W3).T).astype(np.float32)
    c["w2_t"] = np.ascontiguousarray(np.asarray(W2).T).astype(np.float32)
    cw = np.asarray(conv_w)[:, 0, :]
    cb = np.asarray(conv_b)
    pack = np.concatenate(
        [cw.reshape(48, 128, 3).transpose(1, 0, 2),
         cb.reshape(48, 128).T[:, :, None]], axis=2)
    c["convp"] = np.ascontiguousarray(pack.reshape(128, 192)).astype(np.float32)
    c["rmsp"] = np.ascontiguousarray(np.asarray(rms_w).reshape(16, 128).T).astype(np.float32)
    c["alog"] = np.asarray(A_log).reshape(1, 32).astype(np.float32)
    c["ident"] = np.eye(128, dtype=ml_dtypes.bfloat16)
    c["onesb"] = np.ones((128, 128), dtype=ml_dtypes.bfloat16)
    c["ones1"] = np.ones((1, 128), dtype=np.float32)
    c["h128"] = _hadamard(128).astype(ml_dtypes.bfloat16)
    c["h64"] = _hadamard(64).astype(ml_dtypes.bfloat16)
    m = (np.arange(64)[:, None] <= np.arange(64)[None, :]).astype(np.float32)
    c["mask4"] = np.ascontiguousarray(np.tile(m, (1, 4)))
    c["epsc"] = np.array([[LN_EPS, RMS_EPS]], dtype=np.float32)
    return c


def kernel(L0_attn_u, L0_attn_W_in, L0_attn_W_out, L0_attn_conv_w,
           L0_attn_conv_b, L0_attn_A_log, L0_attn_rms_w, L0_ffn_W1,
           L0_ffn_W2, L0_ffn_W3):
    u = np.asarray(L0_attn_u)
    B, L, _ = u.shape
    T = L // 2
    nc = _get_program(T)
    shared = _host_inputs(L0_attn_W_in, L0_attn_W_out, L0_attn_conv_w,
                          L0_attn_conv_b, L0_attn_A_log, L0_attn_rms_w,
                          L0_ffn_W1, L0_ffn_W2, L0_ffn_W3)
    in_maps = []
    for k in range(NCORES):
        b, half = k // 2, k % 2
        t0 = half * T
        win = np.zeros((T + 2, 2048), dtype=np.float32)
        lo, hi = t0 - 1, t0 + T + 1
        rlo, rhi = max(lo, 0), min(hi, L)
        win[rlo - lo:rhi - lo] = u[b][rlo:rhi]
        m = dict(shared)
        m["u"] = np.ascontiguousarray(win.T)
        ws = np.zeros((128, 4), dtype=np.float32)
        oh = np.zeros((128, 4), dtype=np.float32)
        if half == 0:
            ws[:, b] = 1.0
        else:
            oh[:, b] = 1.0
        m["wsel"] = ws
        m["ohsel"] = oh
        in_maps.append(m)
    res = run_bass_kernel_spmd(nc, in_maps, core_ids=list(range(NCORES)))
    out = np.empty((B, L, 2048), dtype=np.float32)
    for k in range(NCORES):
        b, half = k // 2, k % 2
        out[b, half * T:(half + 1) * T] = np.asarray(res.results[k]["out"]).T
    return out


# version 07
# speedup vs baseline: 1.0653x; 1.0653x over previous
import numpy as np
import ml_dtypes
import concourse.bass as bass
import concourse.mybir as mybir
from concourse.bass_utils import run_bass_kernel_spmd
from contextlib import ExitStack

F32 = mybir.dt.float32
BF16 = mybir.dt.bfloat16
AF = mybir.ActivationFunctionType
ALU = mybir.AluOpType

D = 2048
NH = 32
CONV_DIM = 6144
HID = 8192
CHUNK = 64
LN_EPS = 1e-5
RMS_EPS = 1.1920929e-07
NCORES = 8

CSTREAMS = {"pe": "tensor", "dve": "vector", "act": "scalar",
            "pool": "gpsimd", "cc": "gpsimd"}
QUEUE_ENG = {"qs": "sync", "qa": "scalar", "qg": "gpsimd"}


def _splits(total, size, base=0):
    out, off = [], 0
    while off < total:
        w = min(size, total - off)
        out.append((base + off, w))
        off += w
    return out


def _hadamard(n):
    i = np.arange(n)
    a = i[:, None] & i[None, :]
    pc = np.zeros((n, n), dtype=np.int64)
    while a.any():
        pc += a & 1
        a >>= 1
    return np.where(pc % 2 == 0, 1.0, -1.0).astype(np.float32)


class Sched:

    def __init__(self):
        self.count = {s: 0 for s in CSTREAMS}
        self.ops = {e: [] for e in set(CSTREAMS.values()) | set(QUEUE_ENG.values())}
        self.buf = {}
        self.seen = {e: {} for e in self.ops}
        self.bar = {}
        self.key_q = {}

    def barrier(self):
        self.bar = dict(self.count)

    def _rec(self, eng, stream, inc, fn, reads, writes, extra_waits):
        need = dict(self.bar)
        for t in reads:
            b = self.buf.get(t)
            if b is None:
                continue
            if b[0] is not None:
                s, c = b[0]
                if need.get(s, 0) < c:
                    need[s] = c
        for t in writes:
            b = self.buf.get(t)
            if b is None:
                continue
            if b[0] is not None:
                s, c = b[0]
                if need.get(s, 0) < c:
                    need[s] = c
            for s, c in b[1].items():
                if need.get(s, 0) < c:
                    need[s] = c
        for s, c in extra_waits:
            if need.get(s, 0) < c:
                need[s] = c
        waits = []
        seen = self.seen[eng]
        for s in sorted(need):
            c = need[s]
            if c <= 0:
                continue
            if s == stream and s == "pe":
                continue
            if seen.get(s, 0) >= c:
                continue
            waits.append((s, c))
            seen[s] = c
        cnt = self.count[stream] + inc
        self.count[stream] = cnt
        self.ops[eng].append((waits, fn, stream, inc))
        for t in reads:
            b = self.buf.setdefault(t, [None, {}])
            b[1][stream] = cnt
        for t in writes:
            self.buf[t] = [(stream, cnt), {}]

    def op(self, stream, fn, reads=(), writes=(), inc=1, extra_waits=()):
        self._rec(CSTREAMS[stream], stream, inc, fn, reads, writes, extra_waits)

    def dma(self, queue, key, fn, reads=(), writes=()):
        stream = "D:" + key
        q0 = self.key_q.setdefault(key, queue)
        assert q0 == queue, f"dma key {key} used on two queues {q0}/{queue}"
        if stream not in self.count:
            self.count[stream] = 0
        self._rec(QUEUE_ENG[queue], stream, 16, fn, reads, writes, ())

    def emit(self, block, mk_sem):
        sems = {s: mk_sem(s) for s in self.count}
        for eng_name in ("sync", "gpsimd", "vector", "scalar", "tensor"):
            ops = self.ops[eng_name]

            def make(ops_local):
                def body(e):
                    for waits, fn, stream, inc in ops_local:
                        for s, c in waits:
                            e.wait_ge(sems[s], c)
                        inst = fn(e)
                        if stream == "cc":
                            inst.then_inc(sems[stream])
                        else:
                            inst.then_inc(sems[stream], inc)
                return body

            getattr(block, eng_name)(make(ops))


class Arena:

    def __init__(self, nc, base, top):
        self.nc = nc
        self.base = base
        self.top = top
        self.off = base
        self.uid = [0]

    def alloc(self, name, shape, dtype):
        nb = int(np.prod(shape[1:])) * mybir.dt.size(dtype)
        off = (self.off + 31) // 32 * 32
        assert off + nb <= self.top, (
            f"SBUF arena overflow: {name} needs {nb}B at {off}, top {self.top}")
        self.uid[0] += 1
        h = self.nc.alloc_sbuf_tensor_at(f"{name}_{self.uid[0]}", list(shape), dtype, offset=off)
        self.off = off + nb
        return h

    def sub(self):
        a = Arena(self.nc, (self.off + 31) // 32 * 32, self.top)
        a.uid = self.uid
        return a


DBG = {"skip_p0": False, "halves": (0, 1), "skip_rest": False, "skip_p3": False, "skip_ssd": False, "sprint_only": False, "no_cc": False, "gv": "tp", "parts": {"tp", "g", "ge", "y1", "y2", "gate", "epi"}}


def build_program(T, ncores=NCORES):
    Tc = T + 2
    Th = T // 2
    ngc = T // CHUNK

    nc = bass.Bass("TRN2", target_bir_lowering=False, debug=False)

    u_h = nc.dram_tensor("u", [D, Tc], F32, kind="ExternalInput")
    win_h = nc.dram_tensor("win_t", [D, 8224], F32, kind="ExternalInput")
    wout_h = nc.dram_tensor("wout_t", [D, D], F32, kind="ExternalInput")
    w1_h = nc.dram_tensor("w1_t", [D, HID], F32, kind="ExternalInput")
    w3_h = nc.dram_tensor("w3_t", [D, HID], F32, kind="ExternalInput")
    w2_h = nc.dram_tensor("w2_t", [HID, D], F32, kind="ExternalInput")
    convp_h = nc.dram_tensor("convp", [128, 192], F32, kind="ExternalInput")
    rmsp_h = nc.dram_tensor("rmsp", [128, 16], F32, kind="ExternalInput")
    alog_h = nc.dram_tensor("alog", [1, NH], F32, kind="ExternalInput")
    ident_h = nc.dram_tensor("ident", [128, 128], BF16, kind="ExternalInput")
    onesb_h = nc.dram_tensor("onesb", [128, 128], BF16, kind="ExternalInput")
    ones1_h = nc.dram_tensor("ones1", [1, 128], F32, kind="ExternalInput")
    h128_h = nc.dram_tensor("h128", [128, 128], BF16, kind="ExternalInput")
    h64_h = nc.dram_tensor("h64", [64, 64], BF16, kind="ExternalInput")
    mask4_h = nc.dram_tensor("mask4", [64, 256], F32, kind="ExternalInput")
    wsel_h = nc.dram_tensor("wsel", [128, 4], F32, kind="ExternalInput")
    ohsel_h = nc.dram_tensor("ohsel", [128, 4], F32, kind="ExternalInput")
    epsc_h = nc.dram_tensor("epsc", [1, 2], F32, kind="ExternalInput")
    out_h = nc.dram_tensor("out", [D, T], F32, kind="ExternalOutput")

    xbc_d = nc.dram_tensor("xbc_d", [CONV_DIM, T], BF16)
    zs_d = nc.dram_tensor("zs_d", [D, T], BF16)
    x1_d = nc.dram_tensor("x1_d", [D, T], F32)
    lnx1_d = nc.dram_tensor("lnx1_d", [D, T], BF16)
    hid_d = nc.dram_tensor("hid_d", [HID, T], BF16)
    p1_d = nc.dram_tensor("p1_d", [HID, D], BF16)
    m2_d = nc.dram_tensor("m2_d", [HID, D], BF16)
    cc_in = nc.dram_tensor("cc_in", [4, D], F32)
    cc_out = nc.dram_tensor("cc_out", [4, D], F32)

    S = Sched()
    es = ExitStack()
    with es:
        pb = [es.enter_context(nc.psum_tensor(f"pb{i}", [128, 512], F32)) for i in range(4)]
        pT = es.enter_context(nc.psum_tensor("pT", [128, 1024], BF16))
        pS = [es.enter_context(nc.psum_tensor(f"pS{i}", [128, 512], F32)) for i in range(2)]
        pst = es.enter_context(nc.psum_tensor("pst", [128, 512], F32))

        SB_BASE = 16512
        SB_TOP = 229344
        root = Arena(nc, SB_BASE, SB_TOP)
        ident = root.alloc("ident", [128, 128], BF16)
        onesb = root.alloc("onesb", [128, 128], BF16)
        ones1 = root.alloc("ones1", [1, 128], F32)
        h128 = root.alloc("h128", [128, 128], BF16)
        h64 = root.alloc("h64", [64, 64], BF16)
        mask4 = root.alloc("mask4", [64, 256], F32)
        convp = root.alloc("convp", [128, 192], F32)
        rmsp = root.alloc("rmsp", [128, 16], F32)
        wsel = root.alloc("wsel", [128, 4], F32)
        ohsel = root.alloc("ohsel", [128, 4], F32)
        alog = root.alloc("alog", [1, NH], F32)
        d16 = root.alloc("d16", [128, 16], F32)
        rows = root.alloc("rows", [1, 1536], F32)
        zn = root.alloc("zn", [128, 512], F32)
        st16 = root.alloc("st16", [128, 16 * 256], BF16)
        sq16 = root.alloc("sq16", [128, 16 * 256], BF16)
        zeros64 = root.alloc("zeros64", [128, CHUNK], F32)
        epsc = root.alloc("epsc", [1, 2], F32)

        block = es.enter_context(nc.Block())

        for name, dst, src in [
            ("c:ident", ident, ident_h), ("c:onesb", onesb, onesb_h),
            ("c:ones1", ones1, ones1_h), ("c:h128", h128, h128_h),
            ("c:h64", h64, h64_h),
            ("c:convp", convp, convp_h), ("c:rmsp", rmsp, rmsp_h),
            ("c:wsel", wsel, wsel_h), ("c:ohsel", ohsel, ohsel_h),
            ("c:alog", alog, alog_h), ("c:epsc", epsc, epsc_h),
            ("c:mask4", mask4, mask4_h),
        ]:
            S.dma("qs", "c", (lambda d, s: lambda e: e.dma_start(out=d[:, :], in_=s[:, :]))(dst, src),
                 writes=(name, "cR"))
        S.op("pe", lambda e: e.matmul(pb[0][:, 0:16], ones1[0:1, :],
                                      alog[0:1, 0:32:2], start=True, stop=True),
             reads=("c:ones1", "c:alog"), writes=("pb0",))
        S.op("act", lambda e: e.activation(d16[0:64, :], pb[0][0:64, :16], AF.Exp, scale=63.0),
             reads=("pb0",), writes=("c:d16",))
        S.op("pe", lambda e: e.matmul(pb[1][:, 0:16], ones1[0:1, :],
                                      alog[0:1, 1:32:2], start=True, stop=True),
             reads=("c:ones1", "c:alog"), writes=("pb1",))
        S.op("act", lambda e: e.activation(d16[64:128, :], pb[1][64:128, :16], AF.Exp, scale=63.0),
             reads=("pb1", "c:d16"), writes=("c:d16",))
        S.op("dve", lambda e: e.memset(zeros64[:, :], 0.0), writes=("c:zeros64",))

        SC = 1.0 / float(np.sqrt(HID))
        P0_RANGE = (0, 0) if DBG["skip_p0"] else (64, 128)
        ar = root.sub()
        wbufA = ar.alloc("p0wA", [128, 2048], BF16)
        wbufB = ar.alloc("p0wB", [128, 2048], BF16)
        stg = [ar.alloc(f"p0s{i}", [128, 2048], BF16) for i in range(2)]
        for hh in range(P0_RANGE[0]):
            par = hh % 2
            wb = wbufA if par == 0 else wbufB
            S.dma("qg", f"p0w{par}", (lambda hh, wb: lambda e: e.dma_start(
                out=wb[:, :], in_=w2_h[hh * 128:(hh + 1) * 128, :]))(hh, wb),
                writes=(f"p0w:{par}",))
            for dt_i in range(4):
                bi = (hh * 4 + dt_i) % 4
                S.op("pe", (lambda wb, dt_i, bi: lambda e: e.matmul(
                    pb[bi][:, :], h128[:, :], wb[:, dt_i * 512:(dt_i + 1) * 512],
                    start=True, stop=True))(wb, dt_i, bi),
                    reads=(f"p0w:{par}", "c:h128"), writes=(f"pb{bi}",))
                S.op("dve", (lambda dt_i, bi, par: lambda e: e.tensor_copy(
                    stg[par][:, dt_i * 512:(dt_i + 1) * 512], pb[bi][:, :]))(dt_i, bi, par),
                    reads=(f"pb{bi}", f"p0s:{par}"), writes=(f"p0s:{par}",))
            S.dma("qa", f"p0s{par}", (lambda hh, par: lambda e: e.dma_start(
                out=p1_d[hh * 128:(hh + 1) * 128, :], in_=stg[par][:, :]))(hh, par),
                reads=(f"p0s:{par}",), writes=("d:p1",))
        p1_v = p1_d.ap().rearrange("(hh jl) d -> hh jl d", jl=128)
        m2_v = m2_d.ap().rearrange("(jh jl) d -> jh jl d", jl=128)
        for jl in range(P0_RANGE[1]):
            par = jl % 2
            wb = wbufA if par == 0 else wbufB
            S.dma("qg", f"p0w{par}", (lambda jl, wb: lambda e: e.dma_start(
                out=wb[0:64, :], in_=p1_v[:, jl, :]))(jl, wb),
                reads=("d:p1",), writes=(f"p0w:{par}",))
            for dt_i in range(4):
                bi = (jl * 4 + dt_i) % 4
                S.op("pe", (lambda wb, dt_i, bi: lambda e: e.matmul(
                    pb[bi][0:64, :], h64[:, :], wb[0:64, dt_i * 512:(dt_i + 1) * 512],
                    start=True, stop=True))(wb, dt_i, bi),
                    reads=(f"p0w:{par}", "c:h64"), writes=(f"pb{bi}",))
                S.op("dve", (lambda dt_i, bi, par: lambda e: e.tensor_scalar(
                    stg[par][0:64, dt_i * 512:(dt_i + 1) * 512], pb[bi][0:64, :],
                    SC, None, ALU.mult))(dt_i, bi, par),
                    reads=(f"pb{bi}", f"p0s:{par}"), writes=(f"p0s:{par}",))
            S.dma("qa", f"p0s{par}", (lambda jl, par: lambda e: e.dma_start(
                out=m2_v[:, jl, :], in_=stg[par][0:64, :]))(jl, par),
                reads=(f"p0s:{par}",), writes=("d:m2",))

        def ln_stats(get_val, vtoks, W, eps, do_mean=True):
            vals = [get_val(g) for g in range(16)]
            for g in range(16):
                S.op("pe", (lambda g, v, W: lambda e: e.matmul(
                    pst[0:1, 0:W], onesb[:, 0:1], v,
                    start=(g == 0), stop=(g == 15)))(g, vals[g], W),
                    reads=vtoks(g) + ("c:onesb",), writes=("pst",))
            if do_mean:
                for g in range(16):
                    S.op("act", (lambda g, v, W: lambda e: e.activation(
                        sq16[:, g * 256:g * 256 + W], v, AF.Square))(g, vals[g], W),
                        reads=vtoks(g) + (f"sq16:{g}",), writes=(f"sq16:{g}",))
                for g in range(16):
                    S.op("pe", (lambda g, W: lambda e: e.matmul(
                        pb[3][0:1, 0:W], onesb[:, 0:1], sq16[:, g * 256:g * 256 + W],
                        start=(g == 0), stop=(g == 15)))(g, W),
                        reads=(f"sq16:{g}", "c:onesb"), writes=("pb3",))
                S.op("dve", (lambda W: lambda e: e.tensor_scalar(
                    rows[0:1, 0:W], pst[0:1, 0:W], 1.0 / D, None, ALU.mult))(W),
                    reads=("pst", "r_mean"), writes=("r_mean",))
                S.op("dve", (lambda W: lambda e: e.tensor_scalar(
                    rows[0:1, 256:256 + W], pb[3][0:1, 0:W], 1.0 / D, None, ALU.mult))(W),
                    reads=("pb3", "r_msq"), writes=("r_msq",))
                S.op("dve", (lambda W: lambda e: e.tensor_mul(
                    rows[0:1, 512:512 + W], rows[0:1, 0:W], rows[0:1, 0:W]))(W),
                    reads=("r_mean", "r_m2"), writes=("r_m2",))
                S.op("dve", (lambda W: lambda e: e.tensor_sub(
                    rows[0:1, 768:768 + W], rows[0:1, 256:256 + W],
                    rows[0:1, 512:512 + W]))(W),
                    reads=("r_msq", "r_m2", "r_var"), writes=("r_var",))
                S.op("act", (lambda W, eps: lambda e: e.activation(
                    rows[0:1, 1024:1024 + W], rows[0:1, 768:768 + W], AF.Sqrt,
                    bias=eps))(W, eps),
                    reads=("r_var", "r_sd", "c:epsc"), writes=("r_sd",))
                S.op("dve", (lambda W: lambda e: e.reciprocal(
                    rows[0:1, 1280:1280 + W], rows[0:1, 1024:1024 + W]))(W),
                    reads=("r_sd", "r_rstd"), writes=("r_rstd",))
                S.op("pe", (lambda W: lambda e: e.matmul(
                    pst[:, 0:W], ones1[0:1, :], rows[0:1, 0:W], start=True, stop=True))(W),
                    reads=("r_mean", "c:ones1", "pst"), writes=("pst",))
                S.op("pe", (lambda W: lambda e: e.matmul(
                    pb[3][:, 0:W], ones1[0:1, :], rows[0:1, 1280:1280 + W],
                    start=True, stop=True))(W),
                    reads=("r_rstd", "c:ones1", "pb3"), writes=("pb3",))
            else:
                S.op("dve", (lambda W: lambda e: e.tensor_scalar(
                    rows[0:1, 256:256 + W], pst[0:1, 0:W], 1.0 / D, None, ALU.mult))(W),
                    reads=("pst", "r_msq"), writes=("r_msq",))
                S.op("act", (lambda W, eps: lambda e: e.activation(
                    rows[0:1, 1024:1024 + W], rows[0:1, 256:256 + W], AF.Sqrt,
                    bias=eps))(W, eps),
                    reads=("r_msq", "r_sd", "c:epsc"), writes=("r_sd",))
                S.op("dve", (lambda W: lambda e: e.reciprocal(
                    rows[0:1, 1280:1280 + W], rows[0:1, 1024:1024 + W]))(W),
                    reads=("r_sd", "r_rstd"), writes=("r_rstd",))
                S.op("pe", (lambda W: lambda e: e.matmul(
                    pb[3][:, 0:W], ones1[0:1, :], rows[0:1, 1280:1280 + W],
                    start=True, stop=True))(W),
                    reads=("r_rstd", "c:ones1", "pb3"), writes=("pb3",))

        for half in DBG["halves"]:
            S.barrier()
            ar = root.sub()
            lnu = ar.alloc("lnu", [128, 16 * (Th + 2)], BF16)
            wbufA = ar.alloc("p1wA", [128, 16 * 512], BF16)
            wbufB = ar.alloc("p1wB", [128, 16 * 512], BF16)
            uwin2 = [ar.alloc(f"uwin{i}", [128, 16 * 256], F32) for i in range(2)]
            zxA = ar.alloc("zxA", [128, Th + 2], F32)
            zxB = ar.alloc("zxB", [128, Th + 2], F32)
            cb0 = ar.alloc("cb0", [128, Th], BF16)
            cb1 = ar.alloc("cb1", [128, Th], BF16)
            cb2 = ar.alloc("cb2", [128, Th], BF16)
            ucol0 = half * Th
            ncols = Th + 2
            n_win = 0
            for wi, (woff, W) in enumerate(_splits(ncols, 256)):
                wp = wi % 2
                uwin = uwin2[wp]
                n_win += 1
                for g in range(16):
                    S.dma("qs", f"uw{wp}", (lambda g, woff, W, ucol0, uwin: lambda e: e.dma_start(
                        out=uwin[:, g * 256:g * 256 + W],
                        in_=u_h[g * 128:(g + 1) * 128, ucol0 + woff:ucol0 + woff + W]))(g, woff, W, ucol0, uwin),
                        reads=(f"uw:{wp}",), writes=(f"uw:{wp}",))
                    S.op("act", (lambda g, W, uwin: lambda e: e.activation(
                        st16[:, g * 256:g * 256 + W], uwin[:, g * 256:g * 256 + W],
                        AF.Identity))(g, W, uwin),
                        reads=(f"uw:{wp}", f"st16:{g}"), writes=(f"st16:{g}",))
                ln_stats(lambda g: st16[:, g * 256:g * 256 + W],
                         lambda g: (f"st16:{g}",), W, epsc[0:1, 0:1])
                for g in range(16):
                    S.op("dve", (lambda g, W, uwin: lambda e: e.tensor_sub(
                        zn[:, (g % 2) * 256:(g % 2) * 256 + W],
                        uwin[:, g * 256:g * 256 + W], pst[:, 0:W]))(g, W, uwin),
                        reads=(f"uw:{wp}", "pst", f"zn:{g % 2}"), writes=(f"zn:{g % 2}",))
                    S.op("dve", (lambda g, W, woff: lambda e: e.tensor_mul(
                        lnu[:, g * (Th + 2) + woff:g * (Th + 2) + woff + W],
                        zn[:, (g % 2) * 256:(g % 2) * 256 + W], pb[3][:, 0:W]))(g, W, woff),
                        reads=(f"zn:{g % 2}", "pb3", f"lnw:{wi}"), writes=(f"lnw:{wi}",))
            def lnu_tok_range(toff, tw):
                w0 = toff // 256
                w1 = (toff + tw - 1) // 256
                return tuple(f"lnw:{w}" for w in range(w0, min(w1, n_win - 1) + 1))
            ttiles = _splits(ncols, 512)
            for osl in range(16):
                par = osl % 2
                wb = wbufA if par == 0 else wbufB
                win_v = win_h.ap().rearrange("(kc p) o -> p kc o", p=128)
                S.dma("qg", f"p1w{par}", (lambda osl, wb: lambda e: e.dma_start(
                    out=wb[:, :].rearrange("p (kc o) -> p kc o", kc=16),
                    in_=win_v[:, :, osl * 512:(osl + 1) * 512]))(osl, wb),
                    reads=(f"p1w:{par}",), writes=(f"p1w:{par}",))
                for osub in range(4):
                    og = osl * 512 + osub * 128
                    zpar = (osl * 4 + osub) % 2
                    zx = zxA if zpar == 0 else zxB
                    zx_tok = f"zx:{zpar}"
                    for ti, (toff, tw) in enumerate(ttiles):
                        bi = (osub * len(ttiles) + ti) % 4
                        for k in range(16):
                            S.op("pe", (lambda wb, osub, k, toff, tw, bi: lambda e: e.matmul(
                                pb[bi][:, 0:tw],
                                wb[:, k * 512 + osub * 128:k * 512 + osub * 128 + 128],
                                lnu[:, k * (Th + 2) + toff:k * (Th + 2) + toff + tw],
                                start=(k == 0), stop=(k == 15)))(wb, osub, k, toff, tw, bi),
                                reads=(f"p1w:{par}",) + lnu_tok_range(toff, tw), writes=(f"pb{bi}",))
                        S.op("dve", (lambda bi, toff, tw, zx: lambda e: e.tensor_copy(
                            zx[:, toff:toff + tw], pb[bi][:, 0:tw]))(bi, toff, tw, zx),
                            reads=(f"pb{bi}", zx_tok), writes=(zx_tok,))
                    if og < D:
                        S.op("act", (lambda zx: lambda e: e.activation(
                            cb0[:, 0:Th], zx[:, 1:Th + 1], AF.Silu))(zx),
                            reads=(zx_tok, "cb0"), writes=("cb0",))
                        S.dma("qa", "scb0", (lambda og, half: lambda e: e.dma_start(
                            out=zs_d[og:og + 128, half * Th:half * Th + Th],
                            in_=cb0[:, 0:Th]))(og, half),
                            reads=("cb0",), writes=("d:zs",))
                    else:
                        c0 = ((og - D) // 128) * 4
                        S.op("dve", (lambda zx, c0: lambda e: e.tensor_scalar(
                            cb0[:, 0:Th], zx[:, 0:Th], convp[:, c0:c0 + 1], None, ALU.mult))(zx, c0),
                            reads=(zx_tok, "c:convp", "cb0"), writes=("cb0",))
                        S.op("dve", (lambda zx, c0: lambda e: e.tensor_scalar(
                            cb1[:, 0:Th], zx[:, 1:Th + 1], convp[:, c0 + 1:c0 + 2], None, ALU.mult))(zx, c0),
                            reads=(zx_tok, "c:convp", "cb1"), writes=("cb1",))
                        S.op("dve", (lambda zx, c0: lambda e: e.tensor_scalar(
                            cb2[:, 0:Th], zx[:, 2:Th + 2], convp[:, c0 + 2:c0 + 3], None, ALU.mult))(zx, c0),
                            reads=(zx_tok, "c:convp", "cb2"), writes=("cb2",))
                        S.op("dve", lambda e: e.tensor_add(cb0[:, 0:Th], cb0[:, 0:Th], cb1[:, 0:Th]),
                             reads=("cb0", "cb1"), writes=("cb0",))
                        S.op("dve", lambda e: e.tensor_add(cb0[:, 0:Th], cb0[:, 0:Th], cb2[:, 0:Th]),
                             reads=("cb0", "cb2"), writes=("cb0",))
                        S.op("act", (lambda c0: lambda e: e.activation(
                            cb1[:, 0:Th], cb0[:, 0:Th], AF.Silu, bias=convp[:, c0 + 3:c0 + 4]))(c0),
                            reads=("cb0", "c:convp", "cb1"), writes=("cb1",))
                        S.dma("qa", "scb1", (lambda og, half: lambda e: e.dma_start(
                            out=xbc_d[og - D:og - D + 128, half * Th:half * Th + Th],
                            in_=cb1[:, 0:Th]))(og, half),
                            reads=("cb1",), writes=("d:xbc",))

        if DBG["skip_rest"]:
            S.op("dve", lambda e: e.tensor_copy(zn[0:1, 0:1], zn[0:1, 0:1]),
                 writes=("fence",),
                 extra_waits=tuple((s, c) for s, c in S.count.items()
                                   if s.startswith("D:")))
            S.emit(block, lambda s: es.enter_context(
                nc.semaphore(f"sem_{s.replace(':', '_')}")))
            return nc
        S.barrier()
        p2ar = root.sub()
        wo = p2ar.alloc("wo", [128, 16 * 2048], BF16)
        st16b = p2ar.alloc("st16b", [64, 32 * ngc], BF16)
        sin16b = p2ar.alloc("sin16b", [64, 32], BF16)
        spar_a = p2ar.sub()
        cs_t = spar_a.alloc("cs_t", [128, 16 * ngc], F32)
        stt_t = spar_a.alloc("stt_t", [128, 16 * ngc], F32)
        st16_t = spar_a.alloc("st16_t", [128, 16 * ngc], BF16)
        s63 = spar_a.alloc("s63", [128, 16], F32)
        sinb = spar_a.alloc("sinb", [128, 64], F32)
        sin_t = spar_a.alloc("sin_t", [128, 16], F32)
        sin16 = spar_a.alloc("sin16", [128, 16], BF16)
        corr = spar_a.alloc("corr", [128, CHUNK], F32)
        spx = [spar_a.alloc(f"spx{i}", [128, T], BF16) for i in range(2)]
        spb = [spar_a.alloc(f"spb{i}", [128, T], BF16) for i in range(2)]
        spP = [spar_a.alloc(f"spP{i}", [128, T], F32) for i in range(2)]
        S.dma("qg", "wo", lambda e: e.dma_start(
            out=wo[:, :].rearrange("p (kc o) -> p kc o", kc=16),
            in_=wout_h.ap().rearrange("(kc p) o -> p kc o", p=128)),
            writes=("woR",))
        wo_toks = ("woR",)
        for g in range(16):
            par = g % 2
            S.dma("qs", f"spx{par}", (lambda g, par: lambda e: e.dma_start(
                out=spx[par][:, :], in_=xbc_d[g * 128:(g + 1) * 128, :]))(g, par),
                reads=(f"spx:{par}", "d:xbc"), writes=(f"spx:{par}",))
            S.dma("qs", f"spb{par}", (lambda g, par: lambda e: e.dma_start(
                out=spb[par][:, :], in_=xbc_d[D + g * 128:D + (g + 1) * 128, :]))(g, par),
                reads=(f"spb:{par}", "d:xbc"), writes=(f"spb:{par}",))
            S.op("dve", (lambda g, par: lambda e: e.tensor_mul(
                spP[par][:, :], spx[par][:, :], spb[par][:, :]))(g, par),
                reads=(f"spx:{par}", f"spb:{par}", f"spP:{par}"), writes=(f"spP:{par}",))
            S.op("dve", (lambda g, par: lambda e: e.tensor_reduce(
                cs_t[:, g * ngc:(g + 1) * ngc],
                spP[par][:, :].rearrange("p (c t) -> p c t", t=CHUNK),
                mybir.AxisListType.X, ALU.add))(g, par),
                reads=(f"spP:{par}",), writes=(f"cs:{g}",))
            S.op("dve", (lambda g: lambda e: e.tensor_tensor_scan(
                stt_t[:, g * ngc:(g + 1) * ngc],
                d16[:, g:g + 1].broadcast_to((128, ngc)),
                cs_t[:, g * ngc:(g + 1) * ngc], 0.0, ALU.mult, ALU.add))(g),
                reads=(f"cs:{g}", "c:d16", f"stt:{g}"), writes=(f"stt:{g}",))
        S.op("dve", lambda e: e.tensor_copy(
            s63[:, :], stt_t[:, :].rearrange("p (g c) -> p g c", c=ngc)[:, :, ngc - 1]),
            reads=tuple(f"stt:{g}" for g in range(16)), writes=("s63",))
        S.op("dve", lambda e: e.memset(sinb[:, :], 0.0), writes=("sinb",))
        for b in range(4):
            S.op("dve", (lambda b: lambda e: e.tensor_scalar(
                sinb[:, b * 16:(b + 1) * 16], s63[:, :], wsel[:, b:b + 1], None, ALU.mult))(b),
                reads=("s63", "c:wsel", "sinb"), writes=("sinb",))
        S.dma("qa", "scc", lambda e: e.dma_start(
            out=cc_in.ap().rearrange("b (r g) -> b r g", r=128).transpose([1, 0, 2]),
            in_=sinb[:, :].rearrange("p (b g) -> p b g", b=4)),
            reads=("sinb",), writes=("d:ccin",))
        if DBG["no_cc"]:
            S.dma("qs", "ccX", lambda e: e.dma_start(out=cc_out.ap(), in_=cc_in.ap()),
                  reads=("d:ccin",), writes=("d:ccout",))
        else:
            S.op("cc", lambda e: e.collective_compute(
                "AllReduce", ALU.add, replica_groups=[list(range(ncores))],
                ins=[cc_in.ap().opt()], outs=[cc_out.ap().opt()]),
                reads=("d:ccin",), writes=("d:ccout",), inc=1)
        S.dma("qs", "rcc", lambda e: e.dma_start(
            out=sinb[:, :].rearrange("p (b g) -> p b g", b=4),
            in_=cc_out.ap().rearrange("b (r g) -> b r g", r=128).transpose([1, 0, 2])),
            reads=("d:ccout", "sinb"), writes=("sinb",))
        S.op("dve", lambda e: e.tensor_scalar(
            sin_t[:, :], sinb[:, 0:16], ohsel[:, 0:1], None, ALU.mult),
            reads=("sinb", "c:ohsel", "sin"), writes=("sin",))
        for b in range(1, 4):
            S.op("dve", (lambda b: lambda e: e.scalar_tensor_tensor(
                sin_t[:, :], sinb[:, b * 16:(b + 1) * 16], ohsel[:, b:b + 1],
                sin_t[:, :], ALU.mult, ALU.add))(b),
                reads=("sinb", "c:ohsel", "sin"), writes=("sin",))
        for g in range(16):
            S.op("dve", (lambda g: lambda e: e.tensor_tensor_scan(
                corr[:, 0:ngc], d16[:, g:g + 1].broadcast_to((128, ngc)),
                zeros64[:, 0:ngc], sin_t[:, g:g + 1], ALU.mult, ALU.add))(g),
                reads=("sin", "c:d16", "c:zeros64", "corrtok"), writes=("corrtok",))
            S.op("dve", (lambda g: lambda e: e.tensor_add(
                stt_t[:, g * ngc:(g + 1) * ngc], stt_t[:, g * ngc:(g + 1) * ngc],
                corr[:, 0:ngc]))(g),
                reads=("corrtok", f"stt:{g}"), writes=(f"stt:{g}",))
        S.op("dve", lambda e: e.tensor_copy(st16_t[:, :], stt_t[:, :]),
             reads=tuple(f"stt:{g}" for g in range(16)), writes=("st16t",))
        S.op("dve", lambda e: e.tensor_copy(sin16[:, :], sin_t[:, :]),
             reads=("sin",), writes=("sin16",))

        if DBG["sprint_only"]:
            S.op("dve", lambda e: e.tensor_copy(zn[0:1, 0:1], zn[0:1, 0:1]),
                 writes=("fence",),
                 extra_waits=tuple((s, c) for s, c in S.count.items()
                                   if s.startswith("D:")))
            S.emit(block, lambda s: es.enter_context(
                nc.semaphore(f"sem_{s.replace(':', '_')}")))
            return nc
        S.dma("qs", "s16b", lambda e: e.dma_start(
            out=st16b[0:64, 0:16 * ngc], in_=st16_t[0:64, :]),
            reads=("st16t",), writes=("st16b",))
        S.dma("qs", "s16b", lambda e: e.dma_start(
            out=st16b[0:64, 16 * ngc:32 * ngc], in_=st16_t[64:128, :]),
            reads=("st16t",), writes=("st16b",))
        S.dma("qs", "s16c", lambda e: e.dma_start(
            out=sin16b[0:64, 0:16], in_=sin16[0:64, :]),
            reads=("sin16",), writes=("sin16b",))
        S.dma("qs", "s16c", lambda e: e.dma_start(
            out=sin16b[0:64, 16:32], in_=sin16[64:128, :]),
            reads=("sin16",), writes=("sin16b",))

        S.barrier()
        ar = p2ar.sub()
        xg = [ar.alloc(f"xg{i}", [128, 16 * 128], BF16) for i in range(1)]
        zg = [ar.alloc(f"zg{i}", [128, 16 * 128], BF16) for i in range(2)]
        bg = [ar.alloc(f"bg{i}", [64, 32 * 128], BF16) for i in range(2)]
        cg = [ar.alloc(f"cg{i}", [64, 32 * 128], BF16) for i in range(2)]
        xTc = [ar.alloc(f"xTc{i}", [64, 2 * 2048], BF16) for i in range(2)]
        gms = ar.alloc("gms", [64, 512], BF16)
        ygrp = ar.alloc("ygrp", [128, 16 * 128], F32)
        ynb = ar.alloc("ynb", [128, 16 * 512], BF16)
        mo16 = ar.alloc("mo16", [128, 16 * 256], BF16)
        x1b = ar.alloc("x1b", [128, 16 * 256], F32)
        ut = [ar.alloc(f"ut{i}", [128, 256], F32) for i in range(1)]

        pend_gate = []
        for sgi, (sgoff, sgw) in enumerate(_splits(T, 512)):
            ngrp = sgw // 128
            for grp in range(ngrp):
                goff = sgoff + grp * 128
                gpar = grp % 2
                xbc_g = xbc_d.ap().rearrange("(g p) t -> p g t", p=128)
                xbc_h = xbc_d.ap().rearrange("(h n) t -> n h t", n=64)
                zs_g = zs_d.ap().rearrange("(g p) t -> p g t", p=128)
                S.dma("qs", "xg0", (lambda goff: lambda e: e.dma_start(
                    out=xg[0][:, :].rearrange("p (g t) -> p g t", g=16),
                    in_=xbc_g[:, 0:16, goff:goff + 128]))(goff),
                    reads=("xg:0", "d:xbc"), writes=("xg:0",))
                S.dma("qs", f"zg{gpar}", (lambda goff, gpar: lambda e: e.dma_start(
                    out=zg[gpar][:, :].rearrange("p (g t) -> p g t", g=16),
                    in_=zs_g[:, 0:16, goff:goff + 128]))(goff, gpar),
                    reads=(f"zg:{gpar}", "d:zs"), writes=(f"zg:{gpar}",))
                S.dma("qs", f"bg{gpar}", (lambda goff, gpar: lambda e: e.dma_start(
                    out=bg[gpar][0:64, :].rearrange("p (h t) -> p h t", h=32),
                    in_=xbc_h[:, 32:64, goff:goff + 128]))(goff, gpar),
                    reads=(f"bg:{gpar}", "d:xbc"), writes=(f"bg:{gpar}",))
                S.dma("qs", f"cg{gpar}", (lambda goff, gpar: lambda e: e.dma_start(
                    out=cg[gpar][0:64, :].rearrange("p (h t) -> p h t", h=32),
                    in_=xbc_h[:, 64:96, goff:goff + 128]))(goff, gpar),
                    reads=(f"cg:{gpar}", "d:xbc"), writes=(f"cg:{gpar}",))
                if "tp" in DBG["parts"]:
                    for ci in range(2):
                        for g in range(16):
                            slot = g % 8
                            S.op("pe", (lambda g, ci, gpar, slot: lambda e: e.transpose(
                                pT[0:64, slot * 128:(slot + 1) * 128],
                                xg[0][:, g * 128 + ci * 64:g * 128 + ci * 64 + 64],
                                ident[:, :]))(g, ci, gpar, slot),
                                reads=("xg:0", "c:ident", "pT"), writes=("pT",))
                            if g % 2 == 1:
                                S.op("dve", (lambda g, ci, gpar, slot: lambda e: e.tensor_copy(
                                    xTc[gpar][0:64, ci * 2048 + (g - 1) * 128:ci * 2048 + (g + 1) * 128],
                                    pT[0:64, (slot - 1) * 128:(slot + 1) * 128]))(g, ci, gpar, slot),
                                    reads=("pT", f"xT:{gpar}"), writes=(f"xT:{gpar}",))
                ssd_banks = [(pS[0], "pS0"), (pS[1], "pS1"), (pb[2], "pb2")]

                def emit_G(gh):
                    bank, btok = ssd_banks[gh % 3]
                    if "g" not in DBG["parts"]:
                        return
                    for hi in range(2):
                        h = 2 * gh + hi
                        for ci in range(2):
                            S.op("pe", (lambda h, hi, ci, gpar, bank: lambda e: e.matmul(
                                bank[0:64, (hi * 2 + ci) * 64:(hi * 2 + ci) * 64 + 64],
                                bg[gpar][0:64, h * 128 + ci * 64:h * 128 + ci * 64 + 64],
                                cg[gpar][0:64, h * 128 + ci * 64:h * 128 + ci * 64 + 64],
                                start=True, stop=True))(h, hi, ci, gpar, bank),
                                reads=(f"bg:{gpar}", f"cg:{gpar}", btok),
                                writes=(btok,))

                def emit_rest(gh):
                    bank, btok = ssd_banks[gh % 3]
                    c_base = goff // CHUNK
                    if "ge" in DBG["parts"]:
                        S.op("dve", (lambda gh, bank: lambda e: e.tensor_mul(
                            gms[0:64, (gh % 2) * 256:(gh % 2) * 256 + 256],
                            bank[0:64, 0:256], mask4[0:64, :]))(gh, bank),
                            reads=(btok, "c:mask4", f"gms:{gh % 2}"), writes=(f"gms:{gh % 2}",))
                    for hi in range(2 if ("y1" in DBG["parts"] or "y2" in DBG["parts"]) else 0):
                        h = 2 * gh + hi
                        hb = hi * 64
                        for ci in range(2):
                            c_loc = c_base + ci
                            if c_loc == 0:
                                prevap = sin16b[0:64, (h % 2) * 16 + h // 2:(h % 2) * 16 + h // 2 + 1]
                                ptok = "sin16b"
                            else:
                                pcol = ((h % 2) * 16 + h // 2) * ngc + c_loc - 1
                                prevap = st16b[0:64, pcol:pcol + 1]
                                ptok = "st16b"
                            if "y1" in DBG["parts"]:
                                S.op("pe", (lambda h, hi, hb, ci, gh, gpar, bank: lambda e: e.matmul(
                                    bank[hb:hb + 64, 256 + ci * 64:256 + ci * 64 + 64],
                                    xTc[gpar][0:64, ci * 2048 + h * 64:ci * 2048 + h * 64 + 64],
                                    gms[0:64, (gh % 2) * 256 + (hi * 2 + ci) * 64:
                                        (gh % 2) * 256 + (hi * 2 + ci) * 64 + 64],
                                    start=True, stop=("y2" not in DBG["parts"])))(h, hi, hb, ci, gh, gpar, bank),
                                    reads=(f"gms:{gh % 2}", f"xT:{gpar}", btok), writes=(btok,))
                            if "y2" in DBG["parts"]:
                                S.op("pe", (lambda h, hb, ci, gpar, prevap, bank: lambda e: e.matmul(
                                    bank[hb:hb + 64, 256 + ci * 64:256 + ci * 64 + 64],
                                    prevap.broadcast_to((64, 64)),
                                    cg[gpar][0:64, h * 128 + ci * 64:h * 128 + ci * 64 + 64],
                                    start=("y1" not in DBG["parts"]), stop=True))(h, hb, ci, gpar, prevap, bank),
                                    reads=(ptok, f"cg:{gpar}", btok), writes=(btok,))
                    if "y1" in DBG["parts"] or "y2" in DBG["parts"]:
                        S.op("dve", (lambda gh, gpar, bank: lambda e: e.tensor_mul(
                            ygrp[:, gh * 128:(gh + 1) * 128], bank[:, 256:384],
                            zg[gpar][:, gh * 128:(gh + 1) * 128]))(gh, gpar, bank),
                            reads=(btok, f"zg:{gpar}", f"yg:{gh}"), writes=(f"yg:{gh}",))

                for gh in range(16):
                    emit_G(gh)
                    if gh >= 2:
                        emit_rest(gh - 2)
                emit_rest(14)
                emit_rest(15)
                def gate_seg(grp, gpar):
                    for g in range(16 if "gate" in DBG["parts"] else 0):
                        S.op("act", (lambda g: lambda e: e.activation(
                            sq16[:, g * 256:g * 256 + 128], ygrp[:, g * 128:(g + 1) * 128],
                            AF.Square))(g),
                            reads=(f"yg:{g}", f"sq16:{g}"), writes=(f"sq16:{g}",))
                    for g in range(16 if "gate" in DBG["parts"] else 0):
                        S.op("pe", (lambda g: lambda e: e.matmul(
                            pst[0:1, 0:128], onesb[:, 0:1], sq16[:, g * 256:g * 256 + 128],
                            start=(g == 0), stop=(g == 15)))(g),
                            reads=(f"sq16:{g}", "c:onesb"), writes=("pst",))
                    S.op("dve", lambda e: e.tensor_scalar(
                        rows[0:1, 256:384], pst[0:1, 0:128], 1.0 / D, None, ALU.mult),
                        reads=("pst", "r_msq"), writes=("r_msq",))
                    S.op("act", lambda e: e.activation(
                        rows[0:1, 1024:1152], rows[0:1, 256:384], AF.Sqrt, bias=epsc[0:1, 1:2]),
                        reads=("r_msq", "r_sd", "c:epsc"), writes=("r_sd",))
                    S.op("dve", lambda e: e.reciprocal(
                        rows[0:1, 1280:1408], rows[0:1, 1024:1152]),
                        reads=("r_sd", "r_rstd"), writes=("r_rstd",))
                    S.op("pe", lambda e: e.matmul(
                        pb[3][:, 0:128], ones1[0:1, :], rows[0:1, 1280:1408],
                        start=True, stop=True),
                        reads=("r_rstd", "c:ones1", "pb3"), writes=("pb3",))
                    for g in range(16):
                        S.op("dve", (lambda g, grp: lambda e: e.scalar_tensor_tensor(
                            ynb[:, g * 512 + grp * 128:g * 512 + grp * 128 + 128],
                            ygrp[:, g * 128:(g + 1) * 128], rmsp[:, g:g + 1],
                            pb[3][:, 0:128], ALU.mult, ALU.mult))(g, grp),
                            reads=(f"yg:{g}", "c:rmsp", "pb3", f"yn:{g}"), writes=(f"yn:{g}",))
                pend_gate.append((grp, gpar))
                gate_seg(*pend_gate.pop(0))
            while pend_gate:
                gate_seg(*pend_gate.pop(0))
            yn_toks = tuple(f"yn:{g}" for g in range(16))
            for wi, (woff, W) in enumerate(_splits(sgw, 256)):
                for od in range(16):
                    bi = od % 2
                    for k in range(16):
                        S.op("pe", (lambda od, k, bi, woff, W: lambda e: e.matmul(
                            pb[bi][:, 0:W],
                            wo[:, k * 2048 + od * 128:k * 2048 + od * 128 + 128],
                            ynb[:, k * 512 + woff:k * 512 + woff + W],
                            start=(k == 0), stop=(k == 15)))(od, k, bi, woff, W),
                            reads=wo_toks + yn_toks + (f"pb{bi}",), writes=(f"pb{bi}",))
                    S.op("act", (lambda od, bi, W: lambda e: e.activation(
                        mo16[:, od * 256:od * 256 + W], pb[bi][:, 0:W], AF.Identity))(od, bi, W),
                        reads=(f"pb{bi}", f"mo:{od}"), writes=(f"mo:{od}",))
                for g in range(4):
                    pass
                ln_stats(lambda g: mo16[:, g * 256:g * 256 + W],
                         lambda g: (f"mo:{g}",), W, epsc[0:1, 0:1])
                for g in range(16):
                    up = 0
                    S.dma("qg", f"ut{up}", (lambda g, up, sgoff, woff, W: lambda e: e.dma_start(
                        out=ut[up][:, 0:W],
                        in_=u_h[g * 128:(g + 1) * 128,
                                sgoff + woff + 1:sgoff + woff + 1 + W]))(g, up, sgoff, woff, W),
                        reads=(f"ut:{up}",), writes=(f"ut:{up}",))
                    S.op("dve", (lambda g, W: lambda e: e.tensor_sub(
                        zn[:, (g % 2) * 256:(g % 2) * 256 + W],
                        mo16[:, g * 256:g * 256 + W], pst[:, 0:W]))(g, W),
                        reads=(f"mo:{g}", "pst", f"zn:{g % 2}"), writes=(f"zn:{g % 2}",))
                    S.op("dve", (lambda g, W: lambda e: e.tensor_mul(
                        zn[:, (g % 2) * 256:(g % 2) * 256 + W],
                        zn[:, (g % 2) * 256:(g % 2) * 256 + W], pb[3][:, 0:W]))(g, W),
                        reads=(f"zn:{g % 2}", "pb3"), writes=(f"zn:{g % 2}",))
                    S.op("dve", (lambda g, up, W: lambda e: e.tensor_add(
                        x1b[:, g * 256:g * 256 + W],
                        zn[:, (g % 2) * 256:(g % 2) * 256 + W], ut[up][:, 0:W]))(g, up, W),
                        reads=(f"zn:{g % 2}", f"ut:{up}", f"x1:{g}", "sx1R"), writes=(f"x1:{g}",))
                    S.dma("qg", "sx1", (lambda g, sgoff, woff, W: lambda e: e.dma_start(
                        out=x1_d[g * 128:(g + 1) * 128,
                                 sgoff + woff:sgoff + woff + W],
                        in_=x1b[:, g * 256:g * 256 + W]))(g, sgoff, woff, W),
                        reads=(f"x1:{g}",), writes=("d:x1", "sx1R"))
                    S.op("act", (lambda g, W: lambda e: e.activation(
                        st16[:, g * 256:g * 256 + W], x1b[:, g * 256:g * 256 + W],
                        AF.Identity))(g, W),
                        reads=(f"x1:{g}", f"st16:{g}", "slxR"), writes=(f"st16:{g}",))
                ln_stats(lambda g: st16[:, g * 256:g * 256 + W],
                         lambda g: (f"st16:{g}",), W, epsc[0:1, 0:1])
                for g in range(16):
                    S.op("dve", (lambda g, W: lambda e: e.tensor_sub(
                        zn[:, (g % 2) * 256:(g % 2) * 256 + W],
                        x1b[:, g * 256:g * 256 + W], pst[:, 0:W]))(g, W),
                        reads=(f"x1:{g}", "pst", f"zn:{g % 2}"), writes=(f"zn:{g % 2}",))
                    S.op("dve", (lambda g, W: lambda e: e.tensor_mul(
                        st16[:, g * 256:g * 256 + W],
                        zn[:, (g % 2) * 256:(g % 2) * 256 + W], pb[3][:, 0:W]))(g, W),
                        reads=(f"zn:{g % 2}", "pb3", f"st16:{g}"), writes=(f"st16:{g}",))
                    S.dma("qg", "slx", (lambda g, sgoff, woff, W: lambda e: e.dma_start(
                        out=lnx1_d[g * 128:(g + 1) * 128,
                                   sgoff + woff:sgoff + woff + W],
                        in_=st16[:, g * 256:g * 256 + W]))(g, sgoff, woff, W),
                        reads=(f"st16:{g}",), writes=("d:lnx1", "slxR"))

        if DBG["skip_p3"]:
            S.op("dve", lambda e: e.tensor_copy(zn[0:1, 0:1], zn[0:1, 0:1]),
                 writes=("fence",),
                 extra_waits=tuple((s, c) for s, c in S.count.items()
                                   if s.startswith("D:")))
            S.emit(block, lambda s: es.enter_context(
                nc.semaphore(f"sem_{s.replace(':', '_')}")))
            return nc
        S.barrier()
        ar = root.sub()
        lnx1r = ar.alloc("lnx1r", [128, 16 * T], BF16)
        wbufA = ar.alloc("p3wA", [128, 16 * 512], BF16)
        wbufB = ar.alloc("p3wB", [128, 16 * 512], BF16)
        gs = [ar.alloc(f"gs{i}", [128, 512], BF16) for i in range(2)]
        hs = [ar.alloc(f"hs{i}", [128, 512], BF16) for i in range(2)]
        for g in range(16):
            S.dma("qs", "lx", (lambda g: lambda e: e.dma_start(
                out=lnx1r[:, g * T:(g + 1) * T],
                in_=lnx1_d[g * 128:(g + 1) * 128, :]))(g),
                reads=("d:lnx1",), writes=("lxR",))
        lx_toks = ("lxR",)
        t3 = _splits(T, 512)
        w1_v = w1_h.ap().rearrange("(kc p) o -> p kc o", p=128)
        w3_v = w3_h.ap().rearrange("(kc p) o -> p kc o", p=128)
        for osl in range(16):
            S.dma("qg", "p3w1", (lambda osl: lambda e: e.dma_start(
                out=wbufA[:, :].rearrange("p (kc o) -> p kc o", kc=16),
                in_=w1_v[:, :, osl * 512:(osl + 1) * 512]))(osl),
                reads=("p3w1R",), writes=("p3w1R",))
            S.dma("qg", "p3w3", (lambda osl: lambda e: e.dma_start(
                out=wbufB[:, :].rearrange("p (kc o) -> p kc o", kc=16),
                in_=w3_v[:, :, osl * 512:(osl + 1) * 512]))(osl),
                reads=("p3w3R",), writes=("p3w3R",))
            for osub in range(4):
                og = osl * 512 + osub * 128
                for ti, (toff, tw) in enumerate(t3):
                    bi1 = (2 * (osub * len(t3) + ti)) % 4
                    bi3 = bi1 + 1 if bi1 % 2 == 0 else bi1 - 1
                    bi3 = (bi1 + 1) % 4
                    for k in range(16):
                        S.op("pe", (lambda osub, k, toff, tw, bi1: lambda e: e.matmul(
                            pb[bi1][:, 0:tw],
                            wbufA[:, k * 512 + osub * 128:k * 512 + osub * 128 + 128],
                            lnx1r[:, k * T + toff:k * T + toff + tw],
                            start=(k == 0), stop=(k == 15)))(osub, k, toff, tw, bi1),
                            reads=("p3w1R",) + lx_toks, writes=(f"pb{bi1}",))
                    for k in range(16):
                        S.op("pe", (lambda osub, k, toff, tw, bi3: lambda e: e.matmul(
                            pb[bi3][:, 0:tw],
                            wbufB[:, k * 512 + osub * 128:k * 512 + osub * 128 + 128],
                            lnx1r[:, k * T + toff:k * T + toff + tw],
                            start=(k == 0), stop=(k == 15)))(osub, k, toff, tw, bi3),
                            reads=("p3w3R",) + lx_toks, writes=(f"pb{bi3}",))
                    sp = ti % 2
                    S.op("act", (lambda bi1, tw, sp: lambda e: e.activation(
                        gs[sp][:, 0:tw], pb[bi1][:, 0:tw], AF.Silu))(bi1, tw, sp),
                        reads=(f"pb{bi1}", f"gs:{sp}"), writes=(f"gs:{sp}",))
                    S.op("dve", (lambda bi3, tw, sp: lambda e: e.tensor_mul(
                        hs[sp][:, 0:tw], pb[bi3][:, 0:tw], gs[sp][:, 0:tw]))(bi3, tw, sp),
                        reads=(f"pb{bi3}", f"gs:{sp}", f"hs:{sp}"), writes=(f"hs:{sp}",))
                    S.dma("qa", f"shid{sp}", (lambda og, toff, tw, sp: lambda e: e.dma_start(
                        out=hid_d[og:og + 128, toff:toff + tw],
                        in_=hs[sp][:, 0:tw]))(og, toff, tw, sp),
                        reads=(f"hs:{sp}",), writes=("d:hid",))

        S.barrier()
        ar = root.sub()
        pend_p3 = []
        hidb = ar.alloc("hidb", [128, 64 * 512], BF16)
        m2b = [ar.alloc(f"m2b{i}", [128, 64 * 128], BF16) for i in range(2)]
        fo16p = [ar.alloc(f"fo16{i}", [128, 16 * 512], BF16) for i in range(2)]
        x1t = [ar.alloc(f"x1t{i}", [128, 256], F32) for i in range(1)]
        outb = ar.alloc("outb", [128, 2 * 256], F32)
        m2_k = m2_d.ap().rearrange("(kc p) d -> p kc d", p=128)
        for ti, (toff, tw) in enumerate(t3):
            hid_v = hid_d.ap().rearrange("(hc p) t -> p hc t", p=128)
            for hh2 in range(2):
                S.dma("qs", "h3", (lambda hh2, toff, tw: lambda e: e.dma_start(
                    out=hidb[:, :].rearrange("p (hc t) -> p hc t", hc=64)[
                        :, hh2 * 32:(hh2 + 1) * 32, 0:tw],
                    in_=hid_v[:, hh2 * 32:(hh2 + 1) * 32, toff:toff + tw]))(hh2, toff, tw),
                    reads=("h3R", "d:hid"), writes=("h3R",))
            for dsub in range(16):
                wpar = dsub % 2
                S.dma("qa" if wpar == 0 else "qs", f"m2w{wpar}", (lambda dsub, wpar: lambda e: e.dma_start(
                    out=m2b[wpar][:, :].rearrange("p (kc d) -> p kc d", d=128),
                    in_=m2_k[:, :, dsub * 128:(dsub + 1) * 128]))(dsub, wpar),
                    reads=(f"m2w:{wpar}", "d:m2"), writes=(f"m2w:{wpar}",))
                bi = dsub % 2
                for k in range(64):
                    S.op("pe", (lambda k, wpar, bi, tw: lambda e: e.matmul(
                        pb[bi][:, 0:tw], m2b[wpar][:, k * 128:(k + 1) * 128],
                        hidb[:, k * 512:k * 512 + tw],
                        start=(k == 0), stop=(k == 63)))(k, wpar, bi, tw),
                        reads=(f"m2w:{wpar}", "h3R", f"pb{bi}"), writes=(f"pb{bi}",))
                tpar = ti % 2
                S.op("act", (lambda dsub, bi, tw, tpar: lambda e: e.activation(
                    fo16p[tpar][:, dsub * 512:dsub * 512 + tw], pb[bi][:, 0:tw],
                    AF.Identity))(dsub, bi, tw, tpar),
                    reads=(f"pb{bi}", f"fo:{tpar}:{dsub}"), writes=(f"fo:{tpar}:{dsub}",))
            def p3b_ln(toff, tw, tpar):
                for woff, W in _splits(tw, 256):
                    ln_stats((lambda g, tpar=tpar: fo16p[tpar][:, g * 512 + woff:g * 512 + woff + W]),
                         (lambda g, tpar=tpar: (f"fo:{tpar}:{g}",)), W, epsc[0:1, 0:1])
                    for g in range(16):
                        up = 0
                        S.dma("qg", f"x1t{up}", (lambda g, up, toff, woff, W: lambda e: e.dma_start(
                        out=x1t[up][:, 0:W],
                        in_=x1_d[g * 128:(g + 1) * 128,
                                 toff + woff:toff + woff + W]))(g, up, toff, woff, W),
                        reads=(f"x1t:{up}", "d:x1"), writes=(f"x1t:{up}",))
                        S.op("dve", (lambda g, woff, W, tpar: lambda e: e.tensor_sub(
                            zn[:, (g % 2) * 256:(g % 2) * 256 + W],
                            fo16p[tpar][:, g * 512 + woff:g * 512 + woff + W], pst[:, 0:W]))(g, woff, W, tpar),
                            reads=(f"fo:{tpar}:{g}", "pst", f"zn:{g % 2}"), writes=(f"zn:{g % 2}",))
                        S.op("dve", (lambda g, W: lambda e: e.tensor_mul(
                        zn[:, (g % 2) * 256:(g % 2) * 256 + W],
                        zn[:, (g % 2) * 256:(g % 2) * 256 + W], pb[3][:, 0:W]))(g, W),
                        reads=(f"zn:{g % 2}", "pb3"), writes=(f"zn:{g % 2}",))
                        S.op("dve", (lambda g, up, W: lambda e: e.tensor_add(
                        outb[:, (g % 2) * 256:(g % 2) * 256 + W],
                        zn[:, (g % 2) * 256:(g % 2) * 256 + W], x1t[up][:, 0:W]))(g, up, W),
                        reads=(f"zn:{g % 2}", f"x1t:{up}", f"ob:{g % 2}"), writes=(f"ob:{g % 2}",))
                        S.dma("qg", f"so{g % 2}", (lambda g, toff, woff, W: lambda e: e.dma_start(
                        out=out_h[g * 128:(g + 1) * 128,
                                  toff + woff:toff + woff + W],
                        in_=outb[:, (g % 2) * 256:(g % 2) * 256 + W]))(g, toff, woff, W),
                        reads=(f"ob:{g % 2}",), writes=("d:out",))


            pend_p3.append((toff, tw, ti % 2, p3b_ln))
            if len(pend_p3) > 1:
                a, b, c, f = pend_p3.pop(0)
                f(a, b, c)
        while pend_p3:
            a, b, c, f = pend_p3.pop(0)
            f(a, b, c)
        S.op("dve", lambda e: e.tensor_copy(zn[0:1, 0:1], zn[0:1, 0:1]),
             writes=("fence",),
             extra_waits=tuple((s, c) for s, c in S.count.items()
                               if s.startswith("D:")))
        S.emit(block, lambda s: es.enter_context(
            nc.semaphore(f"sem_{s.replace(':', '_')}")))

    return nc


_CACHE = {}


def _get_program(T, ncores=NCORES):
    key = (T, ncores)
    if key not in _CACHE:
        _CACHE[key] = build_program(T, ncores)
    return _CACHE[key]


def _host_inputs(W_in, W_out, conv_w, conv_b, A_log, rms_w, W1, W2, W3):
    c = {}
    c["win_t"] = np.ascontiguousarray(np.asarray(W_in).T).astype(np.float32)
    c["wout_t"] = np.ascontiguousarray(np.asarray(W_out).T).astype(np.float32)
    c["w1_t"] = np.ascontiguousarray(np.asarray(W1).T).astype(np.float32)
    c["w3_t"] = np.ascontiguousarray(np.asarray(W3).T).astype(np.float32)
    c["w2_t"] = np.ascontiguousarray(np.asarray(W2).T).astype(np.float32)
    cw = np.asarray(conv_w)[:, 0, :]
    cb = np.asarray(conv_b)
    pack = np.concatenate(
        [cw.reshape(48, 128, 3).transpose(1, 0, 2),
         cb.reshape(48, 128).T[:, :, None]], axis=2)
    c["convp"] = np.ascontiguousarray(pack.reshape(128, 192)).astype(np.float32)
    c["rmsp"] = np.ascontiguousarray(np.asarray(rms_w).reshape(16, 128).T).astype(np.float32)
    c["alog"] = np.asarray(A_log).reshape(1, 32).astype(np.float32)
    c["ident"] = np.eye(128, dtype=ml_dtypes.bfloat16)
    c["onesb"] = np.ones((128, 128), dtype=ml_dtypes.bfloat16)
    c["ones1"] = np.ones((1, 128), dtype=np.float32)
    c["h128"] = _hadamard(128).astype(ml_dtypes.bfloat16)
    c["h64"] = _hadamard(64).astype(ml_dtypes.bfloat16)
    m = (np.arange(64)[:, None] <= np.arange(64)[None, :]).astype(np.float32)
    c["mask4"] = np.ascontiguousarray(np.tile(m, (1, 4)))
    c["epsc"] = np.array([[LN_EPS, RMS_EPS]], dtype=np.float32)
    return c


def kernel(L0_attn_u, L0_attn_W_in, L0_attn_W_out, L0_attn_conv_w,
           L0_attn_conv_b, L0_attn_A_log, L0_attn_rms_w, L0_ffn_W1,
           L0_ffn_W2, L0_ffn_W3):
    u = np.asarray(L0_attn_u)
    B, L, _ = u.shape
    T = L // 2
    nc = _get_program(T)
    shared = _host_inputs(L0_attn_W_in, L0_attn_W_out, L0_attn_conv_w,
                          L0_attn_conv_b, L0_attn_A_log, L0_attn_rms_w,
                          L0_ffn_W1, L0_ffn_W2, L0_ffn_W3)
    in_maps = []
    for k in range(NCORES):
        b, half = k // 2, k % 2
        t0 = half * T
        win = np.zeros((T + 2, 2048), dtype=np.float32)
        lo, hi = t0 - 1, t0 + T + 1
        rlo, rhi = max(lo, 0), min(hi, L)
        win[rlo - lo:rhi - lo] = u[b][rlo:rhi]
        m = dict(shared)
        m["u"] = np.ascontiguousarray(win.T)
        ws = np.zeros((128, 4), dtype=np.float32)
        oh = np.zeros((128, 4), dtype=np.float32)
        if half == 0:
            ws[:, b] = 1.0
        else:
            oh[:, b] = 1.0
        m["wsel"] = ws
        m["ohsel"] = oh
        in_maps.append(m)
    res = run_bass_kernel_spmd(nc, in_maps, core_ids=list(range(NCORES)))
    out = np.empty((B, L, 2048), dtype=np.float32)
    for k in range(NCORES):
        b, half = k // 2, k % 2
        out[b, half * T:(half + 1) * T] = np.asarray(res.results[k]["out"]).T
    return out


# version 08
# speedup vs baseline: 1.1578x; 1.0868x over previous
import numpy as np
import ml_dtypes
import concourse.bass as bass
import concourse.mybir as mybir
from concourse.bass_utils import run_bass_kernel_spmd
from contextlib import ExitStack

F32 = mybir.dt.float32
BF16 = mybir.dt.bfloat16
AF = mybir.ActivationFunctionType
ALU = mybir.AluOpType

D = 2048
NH = 32
CONV_DIM = 6144
HID = 8192
CHUNK = 64
LN_EPS = 1e-5
RMS_EPS = 1.1920929e-07
NCORES = 8

CSTREAMS = {"pe": "tensor", "dve": "vector", "act": "scalar",
            "pool": "gpsimd", "cc": "gpsimd"}
QUEUE_ENG = {"qs": "sync", "qa": "scalar", "qg": "gpsimd"}


def _splits(total, size, base=0):
    out, off = [], 0
    while off < total:
        w = min(size, total - off)
        out.append((base + off, w))
        off += w
    return out


def _hadamard(n):
    i = np.arange(n)
    a = i[:, None] & i[None, :]
    pc = np.zeros((n, n), dtype=np.int64)
    while a.any():
        pc += a & 1
        a >>= 1
    return np.where(pc % 2 == 0, 1.0, -1.0).astype(np.float32)


class Sched:

    def __init__(self):
        self.count = {s: 0 for s in CSTREAMS}
        self.ops = {e: [] for e in set(CSTREAMS.values()) | set(QUEUE_ENG.values())}
        self.buf = {}
        self.seen = {e: {} for e in self.ops}
        self.bar = {}
        self.key_q = {}

    def barrier(self):
        self.bar = dict(self.count)

    def _rec(self, eng, stream, inc, fn, reads, writes, extra_waits):
        need = dict(self.bar)
        for t in reads:
            b = self.buf.get(t)
            if b is None:
                continue
            if b[0] is not None:
                s, c = b[0]
                if need.get(s, 0) < c:
                    need[s] = c
        for t in writes:
            b = self.buf.get(t)
            if b is None:
                continue
            if b[0] is not None:
                s, c = b[0]
                if need.get(s, 0) < c:
                    need[s] = c
            for s, c in b[1].items():
                if need.get(s, 0) < c:
                    need[s] = c
        for s, c in extra_waits:
            if need.get(s, 0) < c:
                need[s] = c
        waits = []
        seen = self.seen[eng]
        for s in sorted(need):
            c = need[s]
            if c <= 0:
                continue
            if s == stream and s == "pe":
                continue
            if seen.get(s, 0) >= c:
                continue
            waits.append((s, c))
            seen[s] = c
        cnt = self.count[stream] + inc
        self.count[stream] = cnt
        self.ops[eng].append((waits, fn, stream, inc))
        for t in reads:
            b = self.buf.setdefault(t, [None, {}])
            b[1][stream] = cnt
        for t in writes:
            self.buf[t] = [(stream, cnt), {}]

    def op(self, stream, fn, reads=(), writes=(), inc=1, extra_waits=()):
        self._rec(CSTREAMS[stream], stream, inc, fn, reads, writes, extra_waits)

    def dma(self, queue, key, fn, reads=(), writes=()):
        stream = "D:" + key
        q0 = self.key_q.setdefault(key, queue)
        assert q0 == queue, f"dma key {key} used on two queues {q0}/{queue}"
        if stream not in self.count:
            self.count[stream] = 0
        self._rec(QUEUE_ENG[queue], stream, 16, fn, reads, writes, ())

    def emit(self, block, mk_sem):
        sems = {s: mk_sem(s) for s in self.count}
        for eng_name in ("sync", "gpsimd", "vector", "scalar", "tensor"):
            ops = self.ops[eng_name]

            def make(ops_local):
                def body(e):
                    for waits, fn, stream, inc in ops_local:
                        for s, c in waits:
                            e.wait_ge(sems[s], c)
                        inst = fn(e)
                        if stream == "cc":
                            inst.then_inc(sems[stream])
                        else:
                            inst.then_inc(sems[stream], inc)
                return body

            getattr(block, eng_name)(make(ops))


class Arena:

    def __init__(self, nc, base, top):
        self.nc = nc
        self.base = base
        self.top = top
        self.off = base
        self.uid = [0]

    def alloc(self, name, shape, dtype):
        nb = int(np.prod(shape[1:])) * mybir.dt.size(dtype)
        off = (self.off + 31) // 32 * 32
        assert off + nb <= self.top, (
            f"SBUF arena overflow: {name} needs {nb}B at {off}, top {self.top}")
        self.uid[0] += 1
        h = self.nc.alloc_sbuf_tensor_at(f"{name}_{self.uid[0]}", list(shape), dtype, offset=off)
        self.off = off + nb
        return h

    def sub(self):
        a = Arena(self.nc, (self.off + 31) // 32 * 32, self.top)
        a.uid = self.uid
        return a


DBG = {"skip_p0": False, "halves": (0, 1), "skip_rest": False, "skip_p3": False, "skip_ssd": False, "sprint_only": False, "no_cc": False, "gv": "tp", "parts": {"tp", "g", "ge", "y1", "y2", "gate", "epi"}}


def build_program(T, ncores=NCORES):
    Tc = T + 2
    Th = T // 2
    ngc = T // CHUNK

    nc = bass.Bass("TRN2", target_bir_lowering=False, debug=False)

    u_h = nc.dram_tensor("u", [D, Tc], F32, kind="ExternalInput")
    win_h = nc.dram_tensor("win_t", [D, 8224], F32, kind="ExternalInput")
    wout_h = nc.dram_tensor("wout_t", [D, D], F32, kind="ExternalInput")
    w1_h = nc.dram_tensor("w1_t", [D, HID], F32, kind="ExternalInput")
    w3_h = nc.dram_tensor("w3_t", [D, HID], F32, kind="ExternalInput")
    w2_h = nc.dram_tensor("w2_t", [HID, D], F32, kind="ExternalInput")
    convp_h = nc.dram_tensor("convp", [128, 192], F32, kind="ExternalInput")
    rmsp_h = nc.dram_tensor("rmsp", [128, 16], F32, kind="ExternalInput")
    alog_h = nc.dram_tensor("alog", [1, NH], F32, kind="ExternalInput")
    ident_h = nc.dram_tensor("ident", [128, 128], BF16, kind="ExternalInput")
    onesb_h = nc.dram_tensor("onesb", [128, 128], BF16, kind="ExternalInput")
    ones1_h = nc.dram_tensor("ones1", [1, 128], F32, kind="ExternalInput")
    h128_h = nc.dram_tensor("h128", [128, 128], BF16, kind="ExternalInput")
    h64_h = nc.dram_tensor("h64", [64, 64], BF16, kind="ExternalInput")
    mask4_h = nc.dram_tensor("mask4", [64, 256], F32, kind="ExternalInput")
    wsel_h = nc.dram_tensor("wsel", [128, 4], F32, kind="ExternalInput")
    ohsel_h = nc.dram_tensor("ohsel", [128, 4], F32, kind="ExternalInput")
    epsc_h = nc.dram_tensor("epsc", [1, 2], F32, kind="ExternalInput")
    out_h = nc.dram_tensor("out", [D, T], F32, kind="ExternalOutput")

    xbc_d = nc.dram_tensor("xbc_d", [CONV_DIM, T], BF16)
    zs_d = nc.dram_tensor("zs_d", [D, T], BF16)
    x1_d = nc.dram_tensor("x1_d", [D, T], F32)
    lnx1_d = nc.dram_tensor("lnx1_d", [D, T], BF16)
    hid_d = nc.dram_tensor("hid_d", [HID, T], BF16)
    p1_d = nc.dram_tensor("p1_d", [HID, D], BF16)
    m2_d = nc.dram_tensor("m2_d", [16 * 128, 64 * 128], BF16)
    cc_in = nc.dram_tensor("cc_in", [4, D], F32)
    cc_out = nc.dram_tensor("cc_out", [4, D], F32)

    S = Sched()
    es = ExitStack()
    with es:
        pb = [es.enter_context(nc.psum_tensor(f"pb{i}", [128, 512], F32)) for i in range(4)]
        pT = es.enter_context(nc.psum_tensor("pT", [128, 1024], BF16))
        pS = [es.enter_context(nc.psum_tensor(f"pS{i}", [128, 512], F32)) for i in range(2)]
        pst = es.enter_context(nc.psum_tensor("pst", [128, 512], F32))

        SB_BASE = 16512
        SB_TOP = 229344
        root = Arena(nc, SB_BASE, SB_TOP)
        ident = root.alloc("ident", [128, 128], BF16)
        onesb = root.alloc("onesb", [128, 128], BF16)
        ones1 = root.alloc("ones1", [1, 128], F32)
        h128 = root.alloc("h128", [128, 128], BF16)
        h64 = root.alloc("h64", [64, 64], BF16)
        mask4 = root.alloc("mask4", [64, 256], F32)
        convp = root.alloc("convp", [128, 192], F32)
        rmsp = root.alloc("rmsp", [128, 16], F32)
        wsel = root.alloc("wsel", [128, 4], F32)
        ohsel = root.alloc("ohsel", [128, 4], F32)
        alog = root.alloc("alog", [1, NH], F32)
        d16 = root.alloc("d16", [128, 16], F32)
        rows = root.alloc("rows", [1, 1536], F32)
        zn = root.alloc("zn", [128, 512], F32)
        st16 = root.alloc("st16", [128, 16 * 256], BF16)
        sq16 = root.alloc("sq16", [128, 16 * 256], BF16)
        zeros64 = root.alloc("zeros64", [128, CHUNK], F32)
        epsc = root.alloc("epsc", [1, 2], F32)

        block = es.enter_context(nc.Block())

        for name, dst, src in [
            ("c:ident", ident, ident_h), ("c:onesb", onesb, onesb_h),
            ("c:ones1", ones1, ones1_h), ("c:h128", h128, h128_h),
            ("c:h64", h64, h64_h),
            ("c:convp", convp, convp_h), ("c:rmsp", rmsp, rmsp_h),
            ("c:wsel", wsel, wsel_h), ("c:ohsel", ohsel, ohsel_h),
            ("c:alog", alog, alog_h), ("c:epsc", epsc, epsc_h),
            ("c:mask4", mask4, mask4_h),
        ]:
            S.dma("qs", "c", (lambda d, s: lambda e: e.dma_start(out=d[:, :], in_=s[:, :]))(dst, src),
                 writes=(name, "cR"))
        S.op("pe", lambda e: e.matmul(pb[0][:, 0:16], ones1[0:1, :],
                                      alog[0:1, 0:32:2], start=True, stop=True),
             reads=("c:ones1", "c:alog"), writes=("pb0",))
        S.op("act", lambda e: e.activation(d16[0:64, :], pb[0][0:64, :16], AF.Exp, scale=63.0),
             reads=("pb0",), writes=("c:d16",))
        S.op("pe", lambda e: e.matmul(pb[1][:, 0:16], ones1[0:1, :],
                                      alog[0:1, 1:32:2], start=True, stop=True),
             reads=("c:ones1", "c:alog"), writes=("pb1",))
        S.op("act", lambda e: e.activation(d16[64:128, :], pb[1][64:128, :16], AF.Exp, scale=63.0),
             reads=("pb1", "c:d16"), writes=("c:d16",))
        S.op("dve", lambda e: e.memset(zeros64[:, :], 0.0), writes=("c:zeros64",))

        SC = 1.0 / float(np.sqrt(HID))
        P0_RANGE = (0, 0) if DBG["skip_p0"] else (64, 128)
        ar = root.sub()
        wbufA = ar.alloc("p0wA", [128, 2048], BF16)
        wbufB = ar.alloc("p0wB", [128, 2048], BF16)
        stg = [ar.alloc(f"p0s{i}", [128, 2048], BF16) for i in range(2)]
        for hh in range(P0_RANGE[0]):
            par = hh % 2
            wb = wbufA if par == 0 else wbufB
            S.dma("qg", f"p0w{par}", (lambda hh, wb: lambda e: e.dma_start(
                out=wb[:, :], in_=w2_h[hh * 128:(hh + 1) * 128, :]))(hh, wb),
                writes=(f"p0w:{par}",))
            for dt_i in range(4):
                bi = (hh * 4 + dt_i) % 4
                S.op("pe", (lambda wb, dt_i, bi: lambda e: e.matmul(
                    pb[bi][:, :], h128[:, :], wb[:, dt_i * 512:(dt_i + 1) * 512],
                    start=True, stop=True))(wb, dt_i, bi),
                    reads=(f"p0w:{par}", "c:h128"), writes=(f"pb{bi}",))
                S.op("dve", (lambda dt_i, bi, par: lambda e: e.tensor_copy(
                    stg[par][:, dt_i * 512:(dt_i + 1) * 512], pb[bi][:, :]))(dt_i, bi, par),
                    reads=(f"pb{bi}", f"p0s:{par}"), writes=(f"p0s:{par}",))
            S.dma("qa", f"p0s{par}", (lambda hh, par: lambda e: e.dma_start(
                out=p1_d[hh * 128:(hh + 1) * 128, :], in_=stg[par][:, :]))(hh, par),
                reads=(f"p0s:{par}",), writes=("d:p1",))
        p1_v = p1_d.ap().rearrange("(hh jl) d -> hh jl d", jl=128)
        m2_w = m2_d.ap().rearrange("(ds p) (kc dl) -> p ds kc dl", p=128, dl=128)
        for jl in range(P0_RANGE[1]):
            par = jl % 2
            wb = wbufA if par == 0 else wbufB
            S.dma("qg", f"p0w{par}", (lambda jl, wb: lambda e: e.dma_start(
                out=wb[0:64, :], in_=p1_v[:, jl, :]))(jl, wb),
                reads=("d:p1",), writes=(f"p0w:{par}",))
            for dt_i in range(4):
                bi = (jl * 4 + dt_i) % 4
                S.op("pe", (lambda wb, dt_i, bi: lambda e: e.matmul(
                    pb[bi][0:64, :], h64[:, :], wb[0:64, dt_i * 512:(dt_i + 1) * 512],
                    start=True, stop=True))(wb, dt_i, bi),
                    reads=(f"p0w:{par}", "c:h64"), writes=(f"pb{bi}",))
                S.op("dve", (lambda dt_i, bi, par: lambda e: e.tensor_scalar(
                    stg[par][0:64, dt_i * 512:(dt_i + 1) * 512], pb[bi][0:64, :],
                    SC, None, ALU.mult))(dt_i, bi, par),
                    reads=(f"pb{bi}", f"p0s:{par}"), writes=(f"p0s:{par}",))
            S.dma("qa" if par == 0 else "qs", f"p0t{par}", (lambda jl, par: lambda e: e.dma_start(
                out=m2_w[jl, :, :, :].transpose([1, 0, 2]),
                in_=stg[par][0:64, :].rearrange("j (ds dl) -> j ds dl", dl=128)))(jl, par),
                reads=(f"p0s:{par}",), writes=("d:m2",))

        def ln_stats(get_val, vtoks, W, eps, do_mean=True):
            vals = [get_val(g) for g in range(16)]
            for g in range(16):
                S.op("pe", (lambda g, v, W: lambda e: e.matmul(
                    pst[0:1, 0:W], onesb[:, 0:1], v,
                    start=(g == 0), stop=(g == 15)))(g, vals[g], W),
                    reads=vtoks(g) + ("c:onesb",), writes=("pst",))
            if do_mean:
                for g in range(16):
                    S.op("act", (lambda g, v, W: lambda e: e.activation(
                        sq16[:, g * 256:g * 256 + W], v, AF.Square))(g, vals[g], W),
                        reads=vtoks(g) + (f"sq16:{g}",), writes=(f"sq16:{g}",))
                for g in range(16):
                    S.op("pe", (lambda g, W: lambda e: e.matmul(
                        pb[3][0:1, 0:W], onesb[:, 0:1], sq16[:, g * 256:g * 256 + W],
                        start=(g == 0), stop=(g == 15)))(g, W),
                        reads=(f"sq16:{g}", "c:onesb"), writes=("pb3",))
                S.op("dve", (lambda W: lambda e: e.tensor_scalar(
                    rows[0:1, 0:W], pst[0:1, 0:W], 1.0 / D, None, ALU.mult))(W),
                    reads=("pst", "r_mean"), writes=("r_mean",))
                S.op("dve", (lambda W: lambda e: e.tensor_scalar(
                    rows[0:1, 256:256 + W], pb[3][0:1, 0:W], 1.0 / D, None, ALU.mult))(W),
                    reads=("pb3", "r_msq"), writes=("r_msq",))
                S.op("dve", (lambda W: lambda e: e.tensor_mul(
                    rows[0:1, 512:512 + W], rows[0:1, 0:W], rows[0:1, 0:W]))(W),
                    reads=("r_mean", "r_m2"), writes=("r_m2",))
                S.op("dve", (lambda W: lambda e: e.tensor_sub(
                    rows[0:1, 768:768 + W], rows[0:1, 256:256 + W],
                    rows[0:1, 512:512 + W]))(W),
                    reads=("r_msq", "r_m2", "r_var"), writes=("r_var",))
                S.op("act", (lambda W, eps: lambda e: e.activation(
                    rows[0:1, 1024:1024 + W], rows[0:1, 768:768 + W], AF.Sqrt,
                    bias=eps))(W, eps),
                    reads=("r_var", "r_sd", "c:epsc"), writes=("r_sd",))
                S.op("dve", (lambda W: lambda e: e.reciprocal(
                    rows[0:1, 1280:1280 + W], rows[0:1, 1024:1024 + W]))(W),
                    reads=("r_sd", "r_rstd"), writes=("r_rstd",))
                S.op("pe", (lambda W: lambda e: e.matmul(
                    pst[:, 0:W], ones1[0:1, :], rows[0:1, 0:W], start=True, stop=True))(W),
                    reads=("r_mean", "c:ones1", "pst"), writes=("pst",))
                S.op("pe", (lambda W: lambda e: e.matmul(
                    pb[3][:, 0:W], ones1[0:1, :], rows[0:1, 1280:1280 + W],
                    start=True, stop=True))(W),
                    reads=("r_rstd", "c:ones1", "pb3"), writes=("pb3",))
            else:
                S.op("dve", (lambda W: lambda e: e.tensor_scalar(
                    rows[0:1, 256:256 + W], pst[0:1, 0:W], 1.0 / D, None, ALU.mult))(W),
                    reads=("pst", "r_msq"), writes=("r_msq",))
                S.op("act", (lambda W, eps: lambda e: e.activation(
                    rows[0:1, 1024:1024 + W], rows[0:1, 256:256 + W], AF.Sqrt,
                    bias=eps))(W, eps),
                    reads=("r_msq", "r_sd", "c:epsc"), writes=("r_sd",))
                S.op("dve", (lambda W: lambda e: e.reciprocal(
                    rows[0:1, 1280:1280 + W], rows[0:1, 1024:1024 + W]))(W),
                    reads=("r_sd", "r_rstd"), writes=("r_rstd",))
                S.op("pe", (lambda W: lambda e: e.matmul(
                    pb[3][:, 0:W], ones1[0:1, :], rows[0:1, 1280:1280 + W],
                    start=True, stop=True))(W),
                    reads=("r_rstd", "c:ones1", "pb3"), writes=("pb3",))

        for half in DBG["halves"]:
            S.barrier()
            ar = root.sub()
            lnu = ar.alloc("lnu", [128, 16 * (Th + 2)], BF16)
            wbufA = ar.alloc("p1wA", [128, 16 * 512], BF16)
            wbufB = ar.alloc("p1wB", [128, 16 * 512], BF16)
            uwin2 = [ar.alloc(f"uwin{i}", [128, 16 * 256], F32) for i in range(2)]
            zxA = ar.alloc("zxA", [128, Th + 2], F32)
            zxB = ar.alloc("zxB", [128, Th + 2], F32)
            cb0 = ar.alloc("cb0", [128, Th], BF16)
            cb1 = ar.alloc("cb1", [128, Th], BF16)
            cb2 = ar.alloc("cb2", [128, Th], BF16)
            ucol0 = half * Th
            ncols = Th + 2
            n_win = 0
            for wi, (woff, W) in enumerate(_splits(ncols, 256)):
                wp = wi % 2
                uwin = uwin2[wp]
                n_win += 1
                for g in range(16):
                    S.dma("qs", f"uw{wp}", (lambda g, woff, W, ucol0, uwin: lambda e: e.dma_start(
                        out=uwin[:, g * 256:g * 256 + W],
                        in_=u_h[g * 128:(g + 1) * 128, ucol0 + woff:ucol0 + woff + W]))(g, woff, W, ucol0, uwin),
                        reads=(f"uw:{wp}",), writes=(f"uw:{wp}",))
                    S.op("act", (lambda g, W, uwin: lambda e: e.activation(
                        st16[:, g * 256:g * 256 + W], uwin[:, g * 256:g * 256 + W],
                        AF.Identity))(g, W, uwin),
                        reads=(f"uw:{wp}", f"st16:{g}"), writes=(f"st16:{g}",))
                ln_stats(lambda g: st16[:, g * 256:g * 256 + W],
                         lambda g: (f"st16:{g}",), W, epsc[0:1, 0:1])
                for g in range(16):
                    S.op("dve", (lambda g, W, uwin: lambda e: e.tensor_sub(
                        zn[:, (g % 2) * 256:(g % 2) * 256 + W],
                        uwin[:, g * 256:g * 256 + W], pst[:, 0:W]))(g, W, uwin),
                        reads=(f"uw:{wp}", "pst", f"zn:{g % 2}"), writes=(f"zn:{g % 2}",))
                    S.op("dve", (lambda g, W, woff: lambda e: e.tensor_mul(
                        lnu[:, g * (Th + 2) + woff:g * (Th + 2) + woff + W],
                        zn[:, (g % 2) * 256:(g % 2) * 256 + W], pb[3][:, 0:W]))(g, W, woff),
                        reads=(f"zn:{g % 2}", "pb3", f"lnw:{wi}"), writes=(f"lnw:{wi}",))
            def lnu_tok_range(toff, tw):
                w0 = toff // 256
                w1 = (toff + tw - 1) // 256
                return tuple(f"lnw:{w}" for w in range(w0, min(w1, n_win - 1) + 1))
            ttiles = _splits(ncols, 512)
            for osl in range(16):
                par = osl % 2
                wb = wbufA if par == 0 else wbufB
                win_v = win_h.ap().rearrange("(kc p) o -> p kc o", p=128)
                S.dma("qg", f"p1w{par}", (lambda osl, wb: lambda e: e.dma_start(
                    out=wb[:, :].rearrange("p (kc o) -> p kc o", kc=16),
                    in_=win_v[:, :, osl * 512:(osl + 1) * 512]))(osl, wb),
                    reads=(f"p1w:{par}",), writes=(f"p1w:{par}",))
                for osub in range(4):
                    og = osl * 512 + osub * 128
                    zpar = (osl * 4 + osub) % 2
                    zx = zxA if zpar == 0 else zxB
                    zx_tok = f"zx:{zpar}"
                    for ti, (toff, tw) in enumerate(ttiles):
                        bi = (osub * len(ttiles) + ti) % 4
                        for k in range(16):
                            S.op("pe", (lambda wb, osub, k, toff, tw, bi: lambda e: e.matmul(
                                pb[bi][:, 0:tw],
                                wb[:, k * 512 + osub * 128:k * 512 + osub * 128 + 128],
                                lnu[:, k * (Th + 2) + toff:k * (Th + 2) + toff + tw],
                                start=(k == 0), stop=(k == 15)))(wb, osub, k, toff, tw, bi),
                                reads=(f"p1w:{par}",) + lnu_tok_range(toff, tw), writes=(f"pb{bi}",))
                        S.op("dve", (lambda bi, toff, tw, zx: lambda e: e.tensor_copy(
                            zx[:, toff:toff + tw], pb[bi][:, 0:tw]))(bi, toff, tw, zx),
                            reads=(f"pb{bi}", zx_tok), writes=(zx_tok,))
                    if og < D:
                        S.op("act", (lambda zx: lambda e: e.activation(
                            cb0[:, 0:Th], zx[:, 1:Th + 1], AF.Silu))(zx),
                            reads=(zx_tok, "cb0"), writes=("cb0",))
                        S.dma("qa", "scb0", (lambda og, half: lambda e: e.dma_start(
                            out=zs_d[og:og + 128, half * Th:half * Th + Th],
                            in_=cb0[:, 0:Th]))(og, half),
                            reads=("cb0",), writes=("d:zs",))
                    else:
                        c0 = ((og - D) // 128) * 4
                        S.op("dve", (lambda zx, c0: lambda e: e.tensor_scalar(
                            cb0[:, 0:Th], zx[:, 0:Th], convp[:, c0:c0 + 1], None, ALU.mult))(zx, c0),
                            reads=(zx_tok, "c:convp", "cb0"), writes=("cb0",))
                        S.op("dve", (lambda zx, c0: lambda e: e.tensor_scalar(
                            cb1[:, 0:Th], zx[:, 1:Th + 1], convp[:, c0 + 1:c0 + 2], None, ALU.mult))(zx, c0),
                            reads=(zx_tok, "c:convp", "cb1"), writes=("cb1",))
                        S.op("dve", (lambda zx, c0: lambda e: e.tensor_scalar(
                            cb2[:, 0:Th], zx[:, 2:Th + 2], convp[:, c0 + 2:c0 + 3], None, ALU.mult))(zx, c0),
                            reads=(zx_tok, "c:convp", "cb2"), writes=("cb2",))
                        S.op("dve", lambda e: e.tensor_add(cb0[:, 0:Th], cb0[:, 0:Th], cb1[:, 0:Th]),
                             reads=("cb0", "cb1"), writes=("cb0",))
                        S.op("dve", lambda e: e.tensor_add(cb0[:, 0:Th], cb0[:, 0:Th], cb2[:, 0:Th]),
                             reads=("cb0", "cb2"), writes=("cb0",))
                        S.op("act", (lambda c0: lambda e: e.activation(
                            cb1[:, 0:Th], cb0[:, 0:Th], AF.Silu, bias=convp[:, c0 + 3:c0 + 4]))(c0),
                            reads=("cb0", "c:convp", "cb1"), writes=("cb1",))
                        S.dma("qa", "scb1", (lambda og, half: lambda e: e.dma_start(
                            out=xbc_d[og - D:og - D + 128, half * Th:half * Th + Th],
                            in_=cb1[:, 0:Th]))(og, half),
                            reads=("cb1",), writes=("d:xbc",))

        if DBG["skip_rest"]:
            S.op("dve", lambda e: e.tensor_copy(zn[0:1, 0:1], zn[0:1, 0:1]),
                 writes=("fence",),
                 extra_waits=tuple((s, c) for s, c in S.count.items()
                                   if s.startswith("D:")))
            S.emit(block, lambda s: es.enter_context(
                nc.semaphore(f"sem_{s.replace(':', '_')}")))
            return nc
        S.barrier()
        p2ar = root.sub()
        wo = p2ar.alloc("wo", [128, 16 * 2048], BF16)
        st16b = p2ar.alloc("st16b", [64, 32 * ngc], BF16)
        sin16b = p2ar.alloc("sin16b", [64, 32], BF16)
        spar_a = p2ar.sub()
        cs_t = spar_a.alloc("cs_t", [128, 16 * ngc], F32)
        stt_t = spar_a.alloc("stt_t", [128, 16 * ngc], F32)
        st16_t = spar_a.alloc("st16_t", [128, 16 * ngc], BF16)
        s63 = spar_a.alloc("s63", [128, 16], F32)
        sinb = spar_a.alloc("sinb", [128, 64], F32)
        sin_t = spar_a.alloc("sin_t", [128, 16], F32)
        sin16 = spar_a.alloc("sin16", [128, 16], BF16)
        corr = spar_a.alloc("corr", [128, CHUNK], F32)
        spx = [spar_a.alloc(f"spx{i}", [128, T], BF16) for i in range(2)]
        spb = [spar_a.alloc(f"spb{i}", [128, T], BF16) for i in range(2)]
        spP = [spar_a.alloc(f"spP{i}", [128, T], F32) for i in range(2)]
        S.dma("qg", "wo", lambda e: e.dma_start(
            out=wo[:, :].rearrange("p (kc o) -> p kc o", kc=16),
            in_=wout_h.ap().rearrange("(kc p) o -> p kc o", p=128)),
            writes=("woR",))
        wo_toks = ("woR",)
        for g in range(16):
            par = g % 2
            S.dma("qs", f"spx{par}", (lambda g, par: lambda e: e.dma_start(
                out=spx[par][:, :], in_=xbc_d[g * 128:(g + 1) * 128, :]))(g, par),
                reads=(f"spx:{par}", "d:xbc"), writes=(f"spx:{par}",))
            S.dma("qs", f"spb{par}", (lambda g, par: lambda e: e.dma_start(
                out=spb[par][:, :], in_=xbc_d[D + g * 128:D + (g + 1) * 128, :]))(g, par),
                reads=(f"spb:{par}", "d:xbc"), writes=(f"spb:{par}",))
            S.op("dve", (lambda g, par: lambda e: e.tensor_mul(
                spP[par][:, :], spx[par][:, :], spb[par][:, :]))(g, par),
                reads=(f"spx:{par}", f"spb:{par}", f"spP:{par}"), writes=(f"spP:{par}",))
            S.op("dve", (lambda g, par: lambda e: e.tensor_reduce(
                cs_t[:, g * ngc:(g + 1) * ngc],
                spP[par][:, :].rearrange("p (c t) -> p c t", t=CHUNK),
                mybir.AxisListType.X, ALU.add))(g, par),
                reads=(f"spP:{par}",), writes=(f"cs:{g}",))
            S.op("dve", (lambda g: lambda e: e.tensor_tensor_scan(
                stt_t[:, g * ngc:(g + 1) * ngc],
                d16[:, g:g + 1].broadcast_to((128, ngc)),
                cs_t[:, g * ngc:(g + 1) * ngc], 0.0, ALU.mult, ALU.add))(g),
                reads=(f"cs:{g}", "c:d16", f"stt:{g}"), writes=(f"stt:{g}",))
        S.op("dve", lambda e: e.tensor_copy(
            s63[:, :], stt_t[:, :].rearrange("p (g c) -> p g c", c=ngc)[:, :, ngc - 1]),
            reads=tuple(f"stt:{g}" for g in range(16)), writes=("s63",))
        S.op("dve", lambda e: e.memset(sinb[:, :], 0.0), writes=("sinb",))
        for b in range(4):
            S.op("dve", (lambda b: lambda e: e.tensor_scalar(
                sinb[:, b * 16:(b + 1) * 16], s63[:, :], wsel[:, b:b + 1], None, ALU.mult))(b),
                reads=("s63", "c:wsel", "sinb"), writes=("sinb",))
        S.dma("qa", "scc", lambda e: e.dma_start(
            out=cc_in.ap().rearrange("b (r g) -> b r g", r=128).transpose([1, 0, 2]),
            in_=sinb[:, :].rearrange("p (b g) -> p b g", b=4)),
            reads=("sinb",), writes=("d:ccin",))
        if DBG["no_cc"]:
            S.dma("qs", "ccX", lambda e: e.dma_start(out=cc_out.ap(), in_=cc_in.ap()),
                  reads=("d:ccin",), writes=("d:ccout",))
        else:
            S.op("cc", lambda e: e.collective_compute(
                "AllReduce", ALU.add, replica_groups=[list(range(ncores))],
                ins=[cc_in.ap().opt()], outs=[cc_out.ap().opt()]),
                reads=("d:ccin",), writes=("d:ccout",), inc=1)
        S.dma("qs", "rcc", lambda e: e.dma_start(
            out=sinb[:, :].rearrange("p (b g) -> p b g", b=4),
            in_=cc_out.ap().rearrange("b (r g) -> b r g", r=128).transpose([1, 0, 2])),
            reads=("d:ccout", "sinb"), writes=("sinb",))
        S.op("dve", lambda e: e.tensor_scalar(
            sin_t[:, :], sinb[:, 0:16], ohsel[:, 0:1], None, ALU.mult),
            reads=("sinb", "c:ohsel", "sin"), writes=("sin",))
        for b in range(1, 4):
            S.op("dve", (lambda b: lambda e: e.scalar_tensor_tensor(
                sin_t[:, :], sinb[:, b * 16:(b + 1) * 16], ohsel[:, b:b + 1],
                sin_t[:, :], ALU.mult, ALU.add))(b),
                reads=("sinb", "c:ohsel", "sin"), writes=("sin",))
        for g in range(16):
            S.op("dve", (lambda g: lambda e: e.tensor_tensor_scan(
                corr[:, 0:ngc], d16[:, g:g + 1].broadcast_to((128, ngc)),
                zeros64[:, 0:ngc], sin_t[:, g:g + 1], ALU.mult, ALU.add))(g),
                reads=("sin", "c:d16", "c:zeros64", "corrtok"), writes=("corrtok",))
            S.op("dve", (lambda g: lambda e: e.tensor_add(
                stt_t[:, g * ngc:(g + 1) * ngc], stt_t[:, g * ngc:(g + 1) * ngc],
                corr[:, 0:ngc]))(g),
                reads=("corrtok", f"stt:{g}"), writes=(f"stt:{g}",))
        S.op("dve", lambda e: e.tensor_copy(st16_t[:, :], stt_t[:, :]),
             reads=tuple(f"stt:{g}" for g in range(16)), writes=("st16t",))
        S.op("dve", lambda e: e.tensor_copy(sin16[:, :], sin_t[:, :]),
             reads=("sin",), writes=("sin16",))

        if DBG["sprint_only"]:
            S.op("dve", lambda e: e.tensor_copy(zn[0:1, 0:1], zn[0:1, 0:1]),
                 writes=("fence",),
                 extra_waits=tuple((s, c) for s, c in S.count.items()
                                   if s.startswith("D:")))
            S.emit(block, lambda s: es.enter_context(
                nc.semaphore(f"sem_{s.replace(':', '_')}")))
            return nc
        S.dma("qs", "s16b", lambda e: e.dma_start(
            out=st16b[0:64, 0:16 * ngc], in_=st16_t[0:64, :]),
            reads=("st16t",), writes=("st16b",))
        S.dma("qs", "s16b", lambda e: e.dma_start(
            out=st16b[0:64, 16 * ngc:32 * ngc], in_=st16_t[64:128, :]),
            reads=("st16t",), writes=("st16b",))
        S.dma("qs", "s16c", lambda e: e.dma_start(
            out=sin16b[0:64, 0:16], in_=sin16[0:64, :]),
            reads=("sin16",), writes=("sin16b",))
        S.dma("qs", "s16c", lambda e: e.dma_start(
            out=sin16b[0:64, 16:32], in_=sin16[64:128, :]),
            reads=("sin16",), writes=("sin16b",))

        S.barrier()
        ar = p2ar.sub()
        xg = [ar.alloc(f"xg{i}", [128, 16 * 128], BF16) for i in range(1)]
        zg = [ar.alloc(f"zg{i}", [128, 16 * 128], BF16) for i in range(2)]
        bg = [ar.alloc(f"bg{i}", [64, 32 * 128], BF16) for i in range(1)]
        cg = [ar.alloc(f"cg{i}", [64, 32 * 128], BF16) for i in range(1)]
        xTc = [ar.alloc(f"xTc{i}", [64, 2 * 2048], BF16) for i in range(2)]
        gms = ar.alloc("gms", [64, 512], BF16)
        ygrp = ar.alloc("ygrp", [128, 16 * 128], F32)
        ynb = ar.alloc("ynb", [128, 16 * 512], BF16)
        mo16 = ar.alloc("mo16", [128, 16 * 256], BF16)
        x1b = ar.alloc("x1b", [128, 16 * 256], F32)
        uw2 = ar.alloc("uw2", [128, 16 * 256], F32)

        pend_gate = []
        for sgi, (sgoff, sgw) in enumerate(_splits(T, 512)):
            ngrp = sgw // 128
            for grp in range(ngrp):
                goff = sgoff + grp * 128
                gpar = grp % 2
                xbc_g = xbc_d.ap().rearrange("(g p) t -> p g t", p=128)
                xbc_h = xbc_d.ap().rearrange("(h n) t -> n h t", n=64)
                zs_g = zs_d.ap().rearrange("(g p) t -> p g t", p=128)
                S.dma("qs", "xg0", (lambda goff: lambda e: e.dma_start(
                    out=xg[0][:, :].rearrange("p (g t) -> p g t", g=16),
                    in_=xbc_g[:, 0:16, goff:goff + 128]))(goff),
                    reads=("xg:0", "d:xbc"), writes=("xg:0",))
                S.dma("qs", f"zg{gpar}", (lambda goff, gpar: lambda e: e.dma_start(
                    out=zg[gpar][:, :].rearrange("p (g t) -> p g t", g=16),
                    in_=zs_g[:, 0:16, goff:goff + 128]))(goff, gpar),
                    reads=(f"zg:{gpar}", "d:zs"), writes=(f"zg:{gpar}",))
                S.dma("qs", "bgL", (lambda goff, gpar: lambda e: e.dma_start(
                    out=bg[0][0:64, :].rearrange("p (h t) -> p h t", h=32),
                    in_=xbc_h[:, 32:64, goff:goff + 128]))(goff, gpar),
                    reads=("bg:L", "d:xbc"), writes=("bg:L",))
                S.dma("qs", "cgL", (lambda goff, gpar: lambda e: e.dma_start(
                    out=cg[0][0:64, :].rearrange("p (h t) -> p h t", h=32),
                    in_=xbc_h[:, 64:96, goff:goff + 128]))(goff, gpar),
                    reads=("cg:L", "d:xbc"), writes=("cg:L",))
                if "tp" in DBG["parts"]:
                    for ci in range(2):
                        for g in range(16):
                            slot = g % 8
                            S.op("pe", (lambda g, ci, gpar, slot: lambda e: e.transpose(
                                pT[0:64, slot * 128:(slot + 1) * 128],
                                xg[0][:, g * 128 + ci * 64:g * 128 + ci * 64 + 64],
                                ident[:, :]))(g, ci, gpar, slot),
                                reads=("xg:0", "c:ident", "pT"), writes=("pT",))
                            if g % 2 == 1:
                                S.op("dve", (lambda g, ci, gpar, slot: lambda e: e.tensor_copy(
                                    xTc[gpar][0:64, ci * 2048 + (g - 1) * 128:ci * 2048 + (g + 1) * 128],
                                    pT[0:64, (slot - 1) * 128:(slot + 1) * 128]))(g, ci, gpar, slot),
                                    reads=("pT", f"xT:{gpar}"), writes=(f"xT:{gpar}",))
                ssd_banks = [(pS[0], "pS0"), (pS[1], "pS1"), (pb[2], "pb2")]

                def emit_G(gh):
                    bank, btok = ssd_banks[gh % 3]
                    if "g" not in DBG["parts"]:
                        return
                    for hi in range(2):
                        h = 2 * gh + hi
                        for ci in range(2):
                            S.op("pe", (lambda h, hi, ci, gpar, bank: lambda e: e.matmul(
                                bank[0:64, (hi * 2 + ci) * 64:(hi * 2 + ci) * 64 + 64],
                                bg[0][0:64, h * 128 + ci * 64:h * 128 + ci * 64 + 64],
                                cg[0][0:64, h * 128 + ci * 64:h * 128 + ci * 64 + 64],
                                start=True, stop=True))(h, hi, ci, gpar, bank),
                                reads=("bg:L", "cg:L", btok),
                                writes=(btok,))

                def emit_rest(gh):
                    bank, btok = ssd_banks[gh % 3]
                    c_base = goff // CHUNK
                    if "ge" in DBG["parts"]:
                        S.op("dve", (lambda gh, bank: lambda e: e.tensor_mul(
                            gms[0:64, (gh % 2) * 256:(gh % 2) * 256 + 256],
                            bank[0:64, 0:256], mask4[0:64, :]))(gh, bank),
                            reads=(btok, "c:mask4", f"gms:{gh % 2}"), writes=(f"gms:{gh % 2}",))
                    for hi in range(2 if ("y1" in DBG["parts"] or "y2" in DBG["parts"]) else 0):
                        h = 2 * gh + hi
                        hb = hi * 64
                        for ci in range(2):
                            c_loc = c_base + ci
                            if c_loc == 0:
                                prevap = sin16b[0:64, (h % 2) * 16 + h // 2:(h % 2) * 16 + h // 2 + 1]
                                ptok = "sin16b"
                            else:
                                pcol = ((h % 2) * 16 + h // 2) * ngc + c_loc - 1
                                prevap = st16b[0:64, pcol:pcol + 1]
                                ptok = "st16b"
                            if "y1" in DBG["parts"]:
                                S.op("pe", (lambda h, hi, hb, ci, gh, gpar, bank: lambda e: e.matmul(
                                    bank[hb:hb + 64, 256 + ci * 64:256 + ci * 64 + 64],
                                    xTc[gpar][0:64, ci * 2048 + h * 64:ci * 2048 + h * 64 + 64],
                                    gms[0:64, (gh % 2) * 256 + (hi * 2 + ci) * 64:
                                        (gh % 2) * 256 + (hi * 2 + ci) * 64 + 64],
                                    start=True, stop=("y2" not in DBG["parts"])))(h, hi, hb, ci, gh, gpar, bank),
                                    reads=(f"gms:{gh % 2}", f"xT:{gpar}", btok), writes=(btok,))
                            if "y2" in DBG["parts"]:
                                S.op("pe", (lambda h, hb, ci, gpar, prevap, bank: lambda e: e.matmul(
                                    bank[hb:hb + 64, 256 + ci * 64:256 + ci * 64 + 64],
                                    prevap.broadcast_to((64, 64)),
                                    cg[0][0:64, h * 128 + ci * 64:h * 128 + ci * 64 + 64],
                                    start=("y1" not in DBG["parts"]), stop=True))(h, hb, ci, gpar, prevap, bank),
                                    reads=(ptok, "cg:L", btok), writes=(btok,))
                    if "y1" in DBG["parts"] or "y2" in DBG["parts"]:
                        S.op("dve", (lambda gh, gpar, bank: lambda e: e.tensor_mul(
                            ygrp[:, gh * 128:(gh + 1) * 128], bank[:, 256:384],
                            zg[gpar][:, gh * 128:(gh + 1) * 128]))(gh, gpar, bank),
                            reads=(btok, f"zg:{gpar}", f"yg:{gh}"), writes=(f"yg:{gh}",))

                for gh in range(16):
                    emit_G(gh)
                    if gh >= 2:
                        emit_rest(gh - 2)
                emit_rest(14)
                emit_rest(15)
                def gate_seg(grp, gpar):
                    for g in range(16 if "gate" in DBG["parts"] else 0):
                        S.op("act", (lambda g: lambda e: e.activation(
                            sq16[:, g * 256:g * 256 + 128], ygrp[:, g * 128:(g + 1) * 128],
                            AF.Square))(g),
                            reads=(f"yg:{g}", f"sq16:{g}"), writes=(f"sq16:{g}",))
                    for g in range(16 if "gate" in DBG["parts"] else 0):
                        S.op("pe", (lambda g: lambda e: e.matmul(
                            pst[0:1, 0:128], onesb[:, 0:1], sq16[:, g * 256:g * 256 + 128],
                            start=(g == 0), stop=(g == 15)))(g),
                            reads=(f"sq16:{g}", "c:onesb"), writes=("pst",))
                    S.op("dve", lambda e: e.tensor_scalar(
                        rows[0:1, 256:384], pst[0:1, 0:128], 1.0 / D, None, ALU.mult),
                        reads=("pst", "r_msq"), writes=("r_msq",))
                    S.op("act", lambda e: e.activation(
                        rows[0:1, 1024:1152], rows[0:1, 256:384], AF.Sqrt, bias=epsc[0:1, 1:2]),
                        reads=("r_msq", "r_sd", "c:epsc"), writes=("r_sd",))
                    S.op("dve", lambda e: e.reciprocal(
                        rows[0:1, 1280:1408], rows[0:1, 1024:1152]),
                        reads=("r_sd", "r_rstd"), writes=("r_rstd",))
                    S.op("pe", lambda e: e.matmul(
                        pb[3][:, 0:128], ones1[0:1, :], rows[0:1, 1280:1408],
                        start=True, stop=True),
                        reads=("r_rstd", "c:ones1", "pb3"), writes=("pb3",))
                    for g in range(16):
                        S.op("dve", (lambda g, grp: lambda e: e.scalar_tensor_tensor(
                            ynb[:, g * 512 + grp * 128:g * 512 + grp * 128 + 128],
                            ygrp[:, g * 128:(g + 1) * 128], rmsp[:, g:g + 1],
                            pb[3][:, 0:128], ALU.mult, ALU.mult))(g, grp),
                            reads=(f"yg:{g}", "c:rmsp", "pb3", f"yn:{g}"), writes=(f"yn:{g}",))
                pend_gate.append((grp, gpar))
                gate_seg(*pend_gate.pop(0))
            while pend_gate:
                gate_seg(*pend_gate.pop(0))
            yn_toks = tuple(f"yn:{g}" for g in range(16))
            for wi, (woff, W) in enumerate(_splits(sgw, 256)):
                for od in range(16):
                    bi = od % 2
                    for k in range(16):
                        S.op("pe", (lambda od, k, bi, woff, W: lambda e: e.matmul(
                            pb[bi][:, 0:W],
                            wo[:, k * 2048 + od * 128:k * 2048 + od * 128 + 128],
                            ynb[:, k * 512 + woff:k * 512 + woff + W],
                            start=(k == 0), stop=(k == 15)))(od, k, bi, woff, W),
                            reads=wo_toks + yn_toks + (f"pb{bi}",), writes=(f"pb{bi}",))
                    S.op("act", (lambda od, bi, W: lambda e: e.activation(
                        mo16[:, od * 256:od * 256 + W], pb[bi][:, 0:W], AF.Identity))(od, bi, W),
                        reads=(f"pb{bi}", f"mo:{od}"), writes=(f"mo:{od}",))
                for g in range(4):
                    pass
                ln_stats(lambda g: mo16[:, g * 256:g * 256 + W],
                         lambda g: (f"mo:{g}",), W, epsc[0:1, 0:1])
                S.dma("qs", "uw2", (lambda sgoff, woff, W: lambda e: e.dma_start(
                    out=uw2[:, :].rearrange("p (g t) -> p g t", g=16)[:, :, 0:W],
                    in_=u_h.ap().rearrange("(g p) t -> p g t", p=128)[
                        :, :, sgoff + woff + 1:sgoff + woff + 1 + W]))(sgoff, woff, W),
                    reads=("uw2",), writes=("uw2",))
                for g in range(16):
                    S.op("dve", (lambda g, W: lambda e: e.tensor_sub(
                        zn[:, (g % 2) * 256:(g % 2) * 256 + W],
                        mo16[:, g * 256:g * 256 + W], pst[:, 0:W]))(g, W),
                        reads=(f"mo:{g}", "pst", f"zn:{g % 2}"), writes=(f"zn:{g % 2}",))
                    S.op("dve", (lambda g, W: lambda e: e.tensor_mul(
                        zn[:, (g % 2) * 256:(g % 2) * 256 + W],
                        zn[:, (g % 2) * 256:(g % 2) * 256 + W], pb[3][:, 0:W]))(g, W),
                        reads=(f"zn:{g % 2}", "pb3"), writes=(f"zn:{g % 2}",))
                    S.op("dve", (lambda g, W: lambda e: e.tensor_add(
                        x1b[:, g * 256:g * 256 + W],
                        zn[:, (g % 2) * 256:(g % 2) * 256 + W],
                        uw2[:, g * 256:g * 256 + W]))(g, W),
                        reads=(f"zn:{g % 2}", "uw2", f"x1:{g}", "sx1R"), writes=(f"x1:{g}",))
                    S.op("act", (lambda g, W: lambda e: e.activation(
                        st16[:, g * 256:g * 256 + W], x1b[:, g * 256:g * 256 + W],
                        AF.Identity))(g, W),
                        reads=(f"x1:{g}", f"st16:{g}", "slxR"), writes=(f"st16:{g}",))
                S.dma("qa", "sx1", (lambda sgoff, woff, W: lambda e: e.dma_start(
                    out=x1_d.ap().rearrange("(g p) t -> p g t", p=128)[
                        :, :, sgoff + woff:sgoff + woff + W],
                    in_=x1b[:, :].rearrange("p (g t) -> p g t", g=16)[:, :, 0:W]))(sgoff, woff, W),
                    reads=tuple(f"x1:{g}" for g in range(16)), writes=("d:x1", "sx1R"))
                ln_stats(lambda g: st16[:, g * 256:g * 256 + W],
                         lambda g: (f"st16:{g}",), W, epsc[0:1, 0:1])
                for g in range(16):
                    S.op("dve", (lambda g, W: lambda e: e.tensor_sub(
                        zn[:, (g % 2) * 256:(g % 2) * 256 + W],
                        x1b[:, g * 256:g * 256 + W], pst[:, 0:W]))(g, W),
                        reads=(f"x1:{g}", "pst", f"zn:{g % 2}"), writes=(f"zn:{g % 2}",))
                    S.op("dve", (lambda g, W: lambda e: e.tensor_mul(
                        st16[:, g * 256:g * 256 + W],
                        zn[:, (g % 2) * 256:(g % 2) * 256 + W], pb[3][:, 0:W]))(g, W),
                        reads=(f"zn:{g % 2}", "pb3", f"st16:{g}"), writes=(f"st16:{g}",))
                S.dma("qa", "slx", (lambda sgoff, woff, W: lambda e: e.dma_start(
                    out=lnx1_d.ap().rearrange("(g p) t -> p g t", p=128)[
                        :, :, sgoff + woff:sgoff + woff + W],
                    in_=st16[:, :].rearrange("p (g t) -> p g t", g=16)[:, :, 0:W]))(sgoff, woff, W),
                    reads=tuple(f"st16:{g}" for g in range(16)), writes=("d:lnx1", "slxR"))

        if DBG["skip_p3"]:
            S.op("dve", lambda e: e.tensor_copy(zn[0:1, 0:1], zn[0:1, 0:1]),
                 writes=("fence",),
                 extra_waits=tuple((s, c) for s, c in S.count.items()
                                   if s.startswith("D:")))
            S.emit(block, lambda s: es.enter_context(
                nc.semaphore(f"sem_{s.replace(':', '_')}")))
            return nc
        S.barrier()
        ar = root.sub()
        lnx1r = ar.alloc("lnx1r", [128, 16 * T], BF16)
        wbufA = ar.alloc("p3wA", [128, 16 * 512], BF16)
        wbufB = ar.alloc("p3wB", [128, 16 * 512], BF16)
        gs = [ar.alloc(f"gs{i}", [128, 512], BF16) for i in range(2)]
        hs = [ar.alloc(f"hs{i}", [128, 512], BF16) for i in range(2)]
        for g in range(16):
            S.dma("qs", "lx", (lambda g: lambda e: e.dma_start(
                out=lnx1r[:, g * T:(g + 1) * T],
                in_=lnx1_d[g * 128:(g + 1) * 128, :]))(g),
                reads=("d:lnx1",), writes=("lxR",))
        lx_toks = ("lxR",)
        t3 = _splits(T, 512)
        w1_v = w1_h.ap().rearrange("(kc p) o -> p kc o", p=128)
        w3_v = w3_h.ap().rearrange("(kc p) o -> p kc o", p=128)
        for osl in range(16):
            S.dma("qg", "p3w1", (lambda osl: lambda e: e.dma_start(
                out=wbufA[:, :].rearrange("p (kc o) -> p kc o", kc=16),
                in_=w1_v[:, :, osl * 512:(osl + 1) * 512]))(osl),
                reads=("p3w1R",), writes=("p3w1R",))
            S.dma("qg", "p3w3", (lambda osl: lambda e: e.dma_start(
                out=wbufB[:, :].rearrange("p (kc o) -> p kc o", kc=16),
                in_=w3_v[:, :, osl * 512:(osl + 1) * 512]))(osl),
                reads=("p3w3R",), writes=("p3w3R",))
            for osub in range(4):
                og = osl * 512 + osub * 128
                for ti, (toff, tw) in enumerate(t3):
                    bi1 = (2 * (osub * len(t3) + ti)) % 4
                    bi3 = bi1 + 1 if bi1 % 2 == 0 else bi1 - 1
                    bi3 = (bi1 + 1) % 4
                    for k in range(16):
                        S.op("pe", (lambda osub, k, toff, tw, bi1: lambda e: e.matmul(
                            pb[bi1][:, 0:tw],
                            wbufA[:, k * 512 + osub * 128:k * 512 + osub * 128 + 128],
                            lnx1r[:, k * T + toff:k * T + toff + tw],
                            start=(k == 0), stop=(k == 15)))(osub, k, toff, tw, bi1),
                            reads=("p3w1R",) + lx_toks, writes=(f"pb{bi1}",))
                    for k in range(16):
                        S.op("pe", (lambda osub, k, toff, tw, bi3: lambda e: e.matmul(
                            pb[bi3][:, 0:tw],
                            wbufB[:, k * 512 + osub * 128:k * 512 + osub * 128 + 128],
                            lnx1r[:, k * T + toff:k * T + toff + tw],
                            start=(k == 0), stop=(k == 15)))(osub, k, toff, tw, bi3),
                            reads=("p3w3R",) + lx_toks, writes=(f"pb{bi3}",))
                    sp = ti % 2
                    S.op("act", (lambda bi1, tw, sp: lambda e: e.activation(
                        gs[sp][:, 0:tw], pb[bi1][:, 0:tw], AF.Silu))(bi1, tw, sp),
                        reads=(f"pb{bi1}", f"gs:{sp}"), writes=(f"gs:{sp}",))
                    S.op("dve", (lambda bi3, tw, sp: lambda e: e.tensor_mul(
                        hs[sp][:, 0:tw], pb[bi3][:, 0:tw], gs[sp][:, 0:tw]))(bi3, tw, sp),
                        reads=(f"pb{bi3}", f"gs:{sp}", f"hs:{sp}"), writes=(f"hs:{sp}",))
                    S.dma("qa", f"shid{sp}", (lambda og, toff, tw, sp: lambda e: e.dma_start(
                        out=hid_d[og:og + 128, toff:toff + tw],
                        in_=hs[sp][:, 0:tw]))(og, toff, tw, sp),
                        reads=(f"hs:{sp}",), writes=("d:hid",))

        S.barrier()
        ar = root.sub()
        pend_p3 = []
        hidb = ar.alloc("hidb", [128, 64 * 512], BF16)
        m2b = [ar.alloc(f"m2b{i}", [128, 64 * 128], BF16) for i in range(2)]
        fo16p = [ar.alloc(f"fo16{i}", [128, 16 * 512], BF16) for i in range(2)]
        x1w = ar.alloc("x1w", [128, 16 * 256], F32)
        outb = ar.alloc("outb", [128, 16 * 256], F32)
        m2_k = m2_d.ap().rearrange("(ds p) c -> ds p c", p=128)
        for ti, (toff, tw) in enumerate(t3):
            hid_v = hid_d.ap().rearrange("(hc p) t -> p hc t", p=128)
            for hh2 in range(2):
                S.dma("qs", "h3", (lambda hh2, toff, tw: lambda e: e.dma_start(
                    out=hidb[:, :].rearrange("p (hc t) -> p hc t", hc=64)[
                        :, hh2 * 32:(hh2 + 1) * 32, 0:tw],
                    in_=hid_v[:, hh2 * 32:(hh2 + 1) * 32, toff:toff + tw]))(hh2, toff, tw),
                    reads=("h3R", "d:hid"), writes=("h3R",))
            for dsub in range(16):
                wpar = dsub % 2
                S.dma("qa" if wpar == 0 else "qs", f"m2w{wpar}", (lambda dsub, wpar: lambda e: e.dma_start(
                    out=m2b[wpar][:, :], in_=m2_k[dsub, :, :]))(dsub, wpar),
                    reads=(f"m2w:{wpar}", "d:m2"), writes=(f"m2w:{wpar}",))
                bi = dsub % 2
                for k in range(64):
                    S.op("pe", (lambda k, wpar, bi, tw: lambda e: e.matmul(
                        pb[bi][:, 0:tw], m2b[wpar][:, k * 128:(k + 1) * 128],
                        hidb[:, k * 512:k * 512 + tw],
                        start=(k == 0), stop=(k == 63)))(k, wpar, bi, tw),
                        reads=(f"m2w:{wpar}", "h3R", f"pb{bi}"), writes=(f"pb{bi}",))
                tpar = ti % 2
                S.op("act", (lambda dsub, bi, tw, tpar: lambda e: e.activation(
                    fo16p[tpar][:, dsub * 512:dsub * 512 + tw], pb[bi][:, 0:tw],
                    AF.Identity))(dsub, bi, tw, tpar),
                    reads=(f"pb{bi}", f"fo:{tpar}:{dsub}"), writes=(f"fo:{tpar}:{dsub}",))
            def p3b_ln(toff, tw, tpar):
                for woff, W in _splits(tw, 256):
                    ln_stats((lambda g, tpar=tpar: fo16p[tpar][:, g * 512 + woff:g * 512 + woff + W]),
                         (lambda g, tpar=tpar: (f"fo:{tpar}:{g}",)), W, epsc[0:1, 0:1])
                    S.dma("qs", "x1w", (lambda toff, woff, W: lambda e: e.dma_start(
                        out=x1w[:, :].rearrange("p (g t) -> p g t", g=16)[:, :, 0:W],
                        in_=x1_d.ap().rearrange("(g p) t -> p g t", p=128)[
                            :, :, toff + woff:toff + woff + W]))(toff, woff, W),
                        reads=("x1w", "d:x1"), writes=("x1w",))
                    for g in range(16):
                        S.op("dve", (lambda g, woff, W, tpar: lambda e: e.tensor_sub(
                            zn[:, (g % 2) * 256:(g % 2) * 256 + W],
                            fo16p[tpar][:, g * 512 + woff:g * 512 + woff + W], pst[:, 0:W]))(g, woff, W, tpar),
                            reads=(f"fo:{tpar}:{g}", "pst", f"zn:{g % 2}"), writes=(f"zn:{g % 2}",))
                        S.op("dve", (lambda g, W: lambda e: e.tensor_mul(
                        zn[:, (g % 2) * 256:(g % 2) * 256 + W],
                        zn[:, (g % 2) * 256:(g % 2) * 256 + W], pb[3][:, 0:W]))(g, W),
                        reads=(f"zn:{g % 2}", "pb3"), writes=(f"zn:{g % 2}",))
                        S.op("dve", (lambda g, W: lambda e: e.tensor_add(
                            outb[:, g * 256:g * 256 + W],
                            zn[:, (g % 2) * 256:(g % 2) * 256 + W],
                            x1w[:, g * 256:g * 256 + W]))(g, W),
                            reads=(f"zn:{g % 2}", "x1w", f"ob:{g}", "obR"), writes=(f"ob:{g}",))
                    S.dma("qa", "so", (lambda toff, woff, W: lambda e: e.dma_start(
                        out=out_h.ap().rearrange("(g p) t -> p g t", p=128)[
                            :, :, toff + woff:toff + woff + W],
                        in_=outb[:, :].rearrange("p (g t) -> p g t", g=16)[:, :, 0:W]))(toff, woff, W),
                        reads=tuple(f"ob:{g}" for g in range(16)), writes=("d:out", "obR"))


            pend_p3.append((toff, tw, ti % 2, p3b_ln))
            if len(pend_p3) > 1:
                a, b, c, f = pend_p3.pop(0)
                f(a, b, c)
        while pend_p3:
            a, b, c, f = pend_p3.pop(0)
            f(a, b, c)
        S.op("dve", lambda e: e.tensor_copy(zn[0:1, 0:1], zn[0:1, 0:1]),
             writes=("fence",),
             extra_waits=tuple((s, c) for s, c in S.count.items()
                               if s.startswith("D:")))
        S.emit(block, lambda s: es.enter_context(
            nc.semaphore(f"sem_{s.replace(':', '_')}")))

    return nc


_CACHE = {}


def _get_program(T, ncores=NCORES):
    key = (T, ncores)
    if key not in _CACHE:
        _CACHE[key] = build_program(T, ncores)
    return _CACHE[key]


def _host_inputs(W_in, W_out, conv_w, conv_b, A_log, rms_w, W1, W2, W3):
    c = {}
    c["win_t"] = np.ascontiguousarray(np.asarray(W_in).T).astype(np.float32)
    c["wout_t"] = np.ascontiguousarray(np.asarray(W_out).T).astype(np.float32)
    c["w1_t"] = np.ascontiguousarray(np.asarray(W1).T).astype(np.float32)
    c["w3_t"] = np.ascontiguousarray(np.asarray(W3).T).astype(np.float32)
    c["w2_t"] = np.ascontiguousarray(np.asarray(W2).T).astype(np.float32)
    cw = np.asarray(conv_w)[:, 0, :]
    cb = np.asarray(conv_b)
    pack = np.concatenate(
        [cw.reshape(48, 128, 3).transpose(1, 0, 2),
         cb.reshape(48, 128).T[:, :, None]], axis=2)
    c["convp"] = np.ascontiguousarray(pack.reshape(128, 192)).astype(np.float32)
    c["rmsp"] = np.ascontiguousarray(np.asarray(rms_w).reshape(16, 128).T).astype(np.float32)
    c["alog"] = np.asarray(A_log).reshape(1, 32).astype(np.float32)
    c["ident"] = np.eye(128, dtype=ml_dtypes.bfloat16)
    c["onesb"] = np.ones((128, 128), dtype=ml_dtypes.bfloat16)
    c["ones1"] = np.ones((1, 128), dtype=np.float32)
    c["h128"] = _hadamard(128).astype(ml_dtypes.bfloat16)
    c["h64"] = _hadamard(64).astype(ml_dtypes.bfloat16)
    m = (np.arange(64)[:, None] <= np.arange(64)[None, :]).astype(np.float32)
    c["mask4"] = np.ascontiguousarray(np.tile(m, (1, 4)))
    c["epsc"] = np.array([[LN_EPS, RMS_EPS]], dtype=np.float32)
    return c


def kernel(L0_attn_u, L0_attn_W_in, L0_attn_W_out, L0_attn_conv_w,
           L0_attn_conv_b, L0_attn_A_log, L0_attn_rms_w, L0_ffn_W1,
           L0_ffn_W2, L0_ffn_W3):
    u = np.asarray(L0_attn_u)
    B, L, _ = u.shape
    T = L // 2
    nc = _get_program(T)
    shared = _host_inputs(L0_attn_W_in, L0_attn_W_out, L0_attn_conv_w,
                          L0_attn_conv_b, L0_attn_A_log, L0_attn_rms_w,
                          L0_ffn_W1, L0_ffn_W2, L0_ffn_W3)
    in_maps = []
    for k in range(NCORES):
        b, half = k // 2, k % 2
        t0 = half * T
        win = np.zeros((T + 2, 2048), dtype=np.float32)
        lo, hi = t0 - 1, t0 + T + 1
        rlo, rhi = max(lo, 0), min(hi, L)
        win[rlo - lo:rhi - lo] = u[b][rlo:rhi]
        m = dict(shared)
        m["u"] = np.ascontiguousarray(win.T)
        ws = np.zeros((128, 4), dtype=np.float32)
        oh = np.zeros((128, 4), dtype=np.float32)
        if half == 0:
            ws[:, b] = 1.0
        else:
            oh[:, b] = 1.0
        m["wsel"] = ws
        m["ohsel"] = oh
        in_maps.append(m)
    res = run_bass_kernel_spmd(nc, in_maps, core_ids=list(range(NCORES)))
    out = np.empty((B, L, 2048), dtype=np.float32)
    for k in range(NCORES):
        b, half = k // 2, k % 2
        out[b, half * T:(half + 1) * T] = np.asarray(res.results[k]["out"]).T
    return out


# version 09
# speedup vs baseline: 1.1596x; 1.0015x over previous
import numpy as np
import ml_dtypes
import concourse.bass as bass
import concourse.mybir as mybir
from concourse.bass_utils import run_bass_kernel_spmd
from contextlib import ExitStack

F32 = mybir.dt.float32
BF16 = mybir.dt.bfloat16
AF = mybir.ActivationFunctionType
ALU = mybir.AluOpType

D = 2048
NH = 32
CONV_DIM = 6144
HID = 8192
CHUNK = 64
LN_EPS = 1e-5
RMS_EPS = 1.1920929e-07
NCORES = 8

CSTREAMS = {"pe": "tensor", "dve": "vector", "act": "scalar",
            "pool": "gpsimd", "cc": "gpsimd"}
QUEUE_ENG = {"qs": "sync", "qa": "scalar", "qg": "gpsimd"}


def _splits(total, size, base=0):
    out, off = [], 0
    while off < total:
        w = min(size, total - off)
        out.append((base + off, w))
        off += w
    return out


def _hadamard(n):
    i = np.arange(n)
    a = i[:, None] & i[None, :]
    pc = np.zeros((n, n), dtype=np.int64)
    while a.any():
        pc += a & 1
        a >>= 1
    return np.where(pc % 2 == 0, 1.0, -1.0).astype(np.float32)


class Sched:

    def __init__(self):
        self.count = {s: 0 for s in CSTREAMS}
        self.ops = {e: [] for e in set(CSTREAMS.values()) | set(QUEUE_ENG.values())}
        self.buf = {}
        self.seen = {e: {} for e in self.ops}
        self.bar = {}
        self.key_q = {}

    def barrier(self):
        self.bar = dict(self.count)

    def _rec(self, eng, stream, inc, fn, reads, writes, extra_waits):
        need = dict(self.bar)
        for t in reads:
            b = self.buf.get(t)
            if b is None:
                continue
            if b[0] is not None:
                s, c = b[0]
                if need.get(s, 0) < c:
                    need[s] = c
        for t in writes:
            b = self.buf.get(t)
            if b is None:
                continue
            if b[0] is not None:
                s, c = b[0]
                if need.get(s, 0) < c:
                    need[s] = c
            for s, c in b[1].items():
                if need.get(s, 0) < c:
                    need[s] = c
        for s, c in extra_waits:
            if need.get(s, 0) < c:
                need[s] = c
        waits = []
        seen = self.seen[eng]
        for s in sorted(need):
            c = need[s]
            if c <= 0:
                continue
            if s == stream and s == "pe":
                continue
            if seen.get(s, 0) >= c:
                continue
            waits.append((s, c))
            seen[s] = c
        cnt = self.count[stream] + inc
        self.count[stream] = cnt
        self.ops[eng].append((waits, fn, stream, inc))
        for t in reads:
            b = self.buf.setdefault(t, [None, {}])
            b[1][stream] = cnt
        for t in writes:
            self.buf[t] = [(stream, cnt), {}]

    def op(self, stream, fn, reads=(), writes=(), inc=1, extra_waits=()):
        self._rec(CSTREAMS[stream], stream, inc, fn, reads, writes, extra_waits)

    def dma(self, queue, key, fn, reads=(), writes=()):
        stream = "D:" + key
        q0 = self.key_q.setdefault(key, queue)
        assert q0 == queue, f"dma key {key} used on two queues {q0}/{queue}"
        if stream not in self.count:
            self.count[stream] = 0
        self._rec(QUEUE_ENG[queue], stream, 16, fn, reads, writes, ())

    def emit(self, block, mk_sem):
        sems = {s: mk_sem(s) for s in self.count}
        for eng_name in ("sync", "gpsimd", "vector", "scalar", "tensor"):
            ops = self.ops[eng_name]

            def make(ops_local):
                def body(e):
                    for waits, fn, stream, inc in ops_local:
                        for s, c in waits:
                            e.wait_ge(sems[s], c)
                        inst = fn(e)
                        if stream == "cc":
                            inst.then_inc(sems[stream])
                        else:
                            inst.then_inc(sems[stream], inc)
                return body

            getattr(block, eng_name)(make(ops))


class Arena:

    def __init__(self, nc, base, top):
        self.nc = nc
        self.base = base
        self.top = top
        self.off = base
        self.uid = [0]

    def alloc(self, name, shape, dtype):
        nb = int(np.prod(shape[1:])) * mybir.dt.size(dtype)
        off = (self.off + 31) // 32 * 32
        assert off + nb <= self.top, (
            f"SBUF arena overflow: {name} needs {nb}B at {off}, top {self.top}")
        self.uid[0] += 1
        h = self.nc.alloc_sbuf_tensor_at(f"{name}_{self.uid[0]}", list(shape), dtype, offset=off)
        self.off = off + nb
        return h

    def sub(self):
        a = Arena(self.nc, (self.off + 31) // 32 * 32, self.top)
        a.uid = self.uid
        return a


DBG = {"skip_p0": False, "halves": (0, 1), "skip_rest": False, "skip_p3": False, "skip_ssd": False, "sprint_only": False, "no_cc": False, "gv": "tp", "parts": {"tp", "g", "ge", "y1", "y2", "gate", "epi"}}


def build_program(T, ncores=NCORES):
    Tc = T + 2
    Th = T // 2
    ngc = T // CHUNK

    nc = bass.Bass("TRN2", target_bir_lowering=False, debug=False)

    u_h = nc.dram_tensor("u", [D, Tc], F32, kind="ExternalInput")
    win_h = nc.dram_tensor("win_t", [D, 8224], F32, kind="ExternalInput")
    wout_h = nc.dram_tensor("wout_t", [D, D], F32, kind="ExternalInput")
    w1_h = nc.dram_tensor("w1_t", [D, HID], F32, kind="ExternalInput")
    w3_h = nc.dram_tensor("w3_t", [D, HID], F32, kind="ExternalInput")
    w2_h = nc.dram_tensor("w2_t", [HID, D], F32, kind="ExternalInput")
    convp_h = nc.dram_tensor("convp", [128, 192], F32, kind="ExternalInput")
    rmsp_h = nc.dram_tensor("rmsp", [128, 16], F32, kind="ExternalInput")
    alog_h = nc.dram_tensor("alog", [1, NH], F32, kind="ExternalInput")
    ident_h = nc.dram_tensor("ident", [128, 128], BF16, kind="ExternalInput")
    onesb_h = nc.dram_tensor("onesb", [128, 128], BF16, kind="ExternalInput")
    ones1_h = nc.dram_tensor("ones1", [1, 128], F32, kind="ExternalInput")
    h128_h = nc.dram_tensor("h128", [128, 128], BF16, kind="ExternalInput")
    h64_h = nc.dram_tensor("h64", [64, 64], BF16, kind="ExternalInput")
    mask4_h = nc.dram_tensor("mask4", [64, 256], F32, kind="ExternalInput")
    wsel_h = nc.dram_tensor("wsel", [128, 4], F32, kind="ExternalInput")
    ohsel_h = nc.dram_tensor("ohsel", [128, 4], F32, kind="ExternalInput")
    epsc_h = nc.dram_tensor("epsc", [1, 2], F32, kind="ExternalInput")
    out_h = nc.dram_tensor("out", [D, T], F32, kind="ExternalOutput")

    xbc_d = nc.dram_tensor("xbc_d", [CONV_DIM, T], BF16)
    zs_d = nc.dram_tensor("zs_d", [D, T], BF16)
    x1_d = nc.dram_tensor("x1_d", [D, T], F32)
    lnx1_d = nc.dram_tensor("lnx1_d", [D, T], BF16)
    hid_d = nc.dram_tensor("hid_d", [HID, T], BF16)
    p1_d = nc.dram_tensor("p1_d", [HID, D], BF16)
    m2_d = nc.dram_tensor("m2_d", [16 * 128, 64 * 128], BF16)
    cc_in = nc.dram_tensor("cc_in", [4, D], F32)
    cc_out = nc.dram_tensor("cc_out", [4, D], F32)

    S = Sched()
    es = ExitStack()
    with es:
        pb = [es.enter_context(nc.psum_tensor(f"pb{i}", [128, 512], F32)) for i in range(4)]
        pT = es.enter_context(nc.psum_tensor("pT", [128, 1024], BF16))
        pS = [es.enter_context(nc.psum_tensor(f"pS{i}", [128, 512], F32)) for i in range(2)]
        pst = es.enter_context(nc.psum_tensor("pst", [128, 512], F32))

        SB_BASE = 16512
        SB_TOP = 229344
        root = Arena(nc, SB_BASE, SB_TOP)
        ident = root.alloc("ident", [128, 128], BF16)
        onesb = root.alloc("onesb", [128, 128], BF16)
        ones1 = root.alloc("ones1", [1, 128], F32)
        h128 = root.alloc("h128", [128, 128], BF16)
        h64 = root.alloc("h64", [64, 64], BF16)
        mask4 = root.alloc("mask4", [64, 256], F32)
        convp = root.alloc("convp", [128, 192], F32)
        rmsp = root.alloc("rmsp", [128, 16], F32)
        wsel = root.alloc("wsel", [128, 4], F32)
        ohsel = root.alloc("ohsel", [128, 4], F32)
        alog = root.alloc("alog", [1, NH], F32)
        d16 = root.alloc("d16", [128, 16], F32)
        rows = root.alloc("rows", [1, 1536], F32)
        zn = root.alloc("zn", [128, 512], F32)
        st16 = root.alloc("st16", [128, 16 * 256], BF16)
        sq16 = root.alloc("sq16", [128, 16 * 256], BF16)
        zeros64 = root.alloc("zeros64", [128, CHUNK], F32)
        epsc = root.alloc("epsc", [1, 2], F32)

        block = es.enter_context(nc.Block())

        for name, dst, src in [
            ("c:ident", ident, ident_h), ("c:onesb", onesb, onesb_h),
            ("c:ones1", ones1, ones1_h), ("c:h128", h128, h128_h),
            ("c:h64", h64, h64_h),
            ("c:convp", convp, convp_h), ("c:rmsp", rmsp, rmsp_h),
            ("c:wsel", wsel, wsel_h), ("c:ohsel", ohsel, ohsel_h),
            ("c:alog", alog, alog_h), ("c:epsc", epsc, epsc_h),
            ("c:mask4", mask4, mask4_h),
        ]:
            S.dma("qs", "c", (lambda d, s: lambda e: e.dma_start(out=d[:, :], in_=s[:, :]))(dst, src),
                 writes=(name, "cR"))
        S.op("pe", lambda e: e.matmul(pb[0][:, 0:16], ones1[0:1, :],
                                      alog[0:1, 0:32:2], start=True, stop=True),
             reads=("c:ones1", "c:alog"), writes=("pb0",))
        S.op("act", lambda e: e.activation(d16[0:64, :], pb[0][0:64, :16], AF.Exp, scale=63.0),
             reads=("pb0",), writes=("c:d16",))
        S.op("pe", lambda e: e.matmul(pb[1][:, 0:16], ones1[0:1, :],
                                      alog[0:1, 1:32:2], start=True, stop=True),
             reads=("c:ones1", "c:alog"), writes=("pb1",))
        S.op("act", lambda e: e.activation(d16[64:128, :], pb[1][64:128, :16], AF.Exp, scale=63.0),
             reads=("pb1", "c:d16"), writes=("c:d16",))
        S.op("dve", lambda e: e.memset(zeros64[:, :], 0.0), writes=("c:zeros64",))

        SC = 1.0 / float(np.sqrt(HID))
        P0_RANGE = (0, 0) if DBG["skip_p0"] else (64, 128)
        ar = root.sub()
        wbufA = ar.alloc("p0wA", [128, 2048], BF16)
        wbufB = ar.alloc("p0wB", [128, 2048], BF16)
        stg = [ar.alloc(f"p0s{i}", [128, 2048], BF16) for i in range(2)]
        for hh in range(P0_RANGE[0]):
            par = hh % 2
            wb = wbufA if par == 0 else wbufB
            S.dma("qg", f"p0w{par}", (lambda hh, wb: lambda e: e.dma_start(
                out=wb[:, :], in_=w2_h[hh * 128:(hh + 1) * 128, :]))(hh, wb),
                writes=(f"p0w:{par}",))
            for dt_i in range(4):
                bi = (hh * 4 + dt_i) % 4
                S.op("pe", (lambda wb, dt_i, bi: lambda e: e.matmul(
                    pb[bi][:, :], h128[:, :], wb[:, dt_i * 512:(dt_i + 1) * 512],
                    start=True, stop=True))(wb, dt_i, bi),
                    reads=(f"p0w:{par}", "c:h128"), writes=(f"pb{bi}",))
                S.op("dve", (lambda dt_i, bi, par: lambda e: e.tensor_copy(
                    stg[par][:, dt_i * 512:(dt_i + 1) * 512], pb[bi][:, :]))(dt_i, bi, par),
                    reads=(f"pb{bi}", f"p0s:{par}"), writes=(f"p0s:{par}",))
            S.dma("qa", f"p0s{par}", (lambda hh, par: lambda e: e.dma_start(
                out=p1_d[hh * 128:(hh + 1) * 128, :], in_=stg[par][:, :]))(hh, par),
                reads=(f"p0s:{par}",), writes=("d:p1",))
        p1_v = p1_d.ap().rearrange("(hh jl) d -> hh jl d", jl=128)
        m2_w = m2_d.ap().rearrange("(ds p) (kc dl) -> p ds kc dl", p=128, dl=128)
        for jl in range(P0_RANGE[1]):
            par = jl % 2
            wb = wbufA if par == 0 else wbufB
            S.dma("qg", f"p0w{par}", (lambda jl, wb: lambda e: e.dma_start(
                out=wb[0:64, :], in_=p1_v[:, jl, :]))(jl, wb),
                reads=("d:p1",), writes=(f"p0w:{par}",))
            for dt_i in range(4):
                bi = (jl * 4 + dt_i) % 4
                S.op("pe", (lambda wb, dt_i, bi: lambda e: e.matmul(
                    pb[bi][0:64, :], h64[:, :], wb[0:64, dt_i * 512:(dt_i + 1) * 512],
                    start=True, stop=True))(wb, dt_i, bi),
                    reads=(f"p0w:{par}", "c:h64"), writes=(f"pb{bi}",))
                S.op("dve", (lambda dt_i, bi, par: lambda e: e.tensor_scalar(
                    stg[par][0:64, dt_i * 512:(dt_i + 1) * 512], pb[bi][0:64, :],
                    SC, None, ALU.mult))(dt_i, bi, par),
                    reads=(f"pb{bi}", f"p0s:{par}"), writes=(f"p0s:{par}",))
            S.dma("qa" if par == 0 else "qs", f"p0t{par}", (lambda jl, par: lambda e: e.dma_start(
                out=m2_w[jl, :, :, :].transpose([1, 0, 2]),
                in_=stg[par][0:64, :].rearrange("j (ds dl) -> j ds dl", dl=128)))(jl, par),
                reads=(f"p0s:{par}",), writes=("d:m2",))

        def ln_stats(get_val, vtoks, W, eps, do_mean=True):
            vals = [get_val(g) for g in range(16)]
            if do_mean:
                for g in range(16):
                    S.op("act", (lambda g, v, W: lambda e: e.activation(
                        sq16[:, g * 256:g * 256 + W], v, AF.Square))(g, vals[g], W),
                        reads=vtoks(g) + (f"sq16:{g}",), writes=(f"sq16:{g}",))
            for g in range(16):
                S.op("pe", (lambda g, v, W: lambda e: e.matmul(
                    pst[0:1, 0:W], onesb[:, 0:1], v,
                    start=(g == 0), stop=(g == 15)))(g, vals[g], W),
                    reads=vtoks(g) + ("c:onesb",), writes=("pst",))
            if do_mean:
                for g in range(16):
                    S.op("pe", (lambda g, W: lambda e: e.matmul(
                        pb[3][0:1, 0:W], onesb[:, 0:1], sq16[:, g * 256:g * 256 + W],
                        start=(g == 0), stop=(g == 15)))(g, W),
                        reads=(f"sq16:{g}", "c:onesb"), writes=("pb3",))
                S.op("dve", (lambda W: lambda e: e.tensor_scalar(
                    rows[0:1, 0:W], pst[0:1, 0:W], 1.0 / D, None, ALU.mult))(W),
                    reads=("pst", "r_mean"), writes=("r_mean",))
                S.op("dve", (lambda W: lambda e: e.tensor_scalar(
                    rows[0:1, 256:256 + W], pb[3][0:1, 0:W], 1.0 / D, None, ALU.mult))(W),
                    reads=("pb3", "r_msq"), writes=("r_msq",))
                S.op("dve", (lambda W: lambda e: e.tensor_mul(
                    rows[0:1, 512:512 + W], rows[0:1, 0:W], rows[0:1, 0:W]))(W),
                    reads=("r_mean", "r_m2"), writes=("r_m2",))
                S.op("dve", (lambda W: lambda e: e.tensor_sub(
                    rows[0:1, 768:768 + W], rows[0:1, 256:256 + W],
                    rows[0:1, 512:512 + W]))(W),
                    reads=("r_msq", "r_m2", "r_var"), writes=("r_var",))
                S.op("act", (lambda W, eps: lambda e: e.activation(
                    rows[0:1, 1024:1024 + W], rows[0:1, 768:768 + W], AF.Sqrt,
                    bias=eps))(W, eps),
                    reads=("r_var", "r_sd", "c:epsc"), writes=("r_sd",))
                S.op("dve", (lambda W: lambda e: e.reciprocal(
                    rows[0:1, 1280:1280 + W], rows[0:1, 1024:1024 + W]))(W),
                    reads=("r_sd", "r_rstd"), writes=("r_rstd",))
                S.op("pe", (lambda W: lambda e: e.matmul(
                    pst[:, 0:W], ones1[0:1, :], rows[0:1, 0:W], start=True, stop=True))(W),
                    reads=("r_mean", "c:ones1", "pst"), writes=("pst",))
                S.op("pe", (lambda W: lambda e: e.matmul(
                    pb[3][:, 0:W], ones1[0:1, :], rows[0:1, 1280:1280 + W],
                    start=True, stop=True))(W),
                    reads=("r_rstd", "c:ones1", "pb3"), writes=("pb3",))
            else:
                S.op("dve", (lambda W: lambda e: e.tensor_scalar(
                    rows[0:1, 256:256 + W], pst[0:1, 0:W], 1.0 / D, None, ALU.mult))(W),
                    reads=("pst", "r_msq"), writes=("r_msq",))
                S.op("act", (lambda W, eps: lambda e: e.activation(
                    rows[0:1, 1024:1024 + W], rows[0:1, 256:256 + W], AF.Sqrt,
                    bias=eps))(W, eps),
                    reads=("r_msq", "r_sd", "c:epsc"), writes=("r_sd",))
                S.op("dve", (lambda W: lambda e: e.reciprocal(
                    rows[0:1, 1280:1280 + W], rows[0:1, 1024:1024 + W]))(W),
                    reads=("r_sd", "r_rstd"), writes=("r_rstd",))
                S.op("pe", (lambda W: lambda e: e.matmul(
                    pb[3][:, 0:W], ones1[0:1, :], rows[0:1, 1280:1280 + W],
                    start=True, stop=True))(W),
                    reads=("r_rstd", "c:ones1", "pb3"), writes=("pb3",))

        for half in DBG["halves"]:
            S.barrier()
            ar = root.sub()
            lnu = ar.alloc("lnu", [128, 16 * (Th + 2)], BF16)
            wbufA = ar.alloc("p1wA", [128, 16 * 512], BF16)
            wbufB = ar.alloc("p1wB", [128, 16 * 512], BF16)
            uwin2 = [ar.alloc(f"uwin{i}", [128, 16 * 256], F32) for i in range(2)]
            zxA = ar.alloc("zxA", [128, Th + 2], F32)
            zxB = ar.alloc("zxB", [128, Th + 2], F32)
            cb0 = ar.alloc("cb0", [128, Th], BF16)
            cb1 = ar.alloc("cb1", [128, Th], BF16)
            cb2 = ar.alloc("cb2", [128, Th], BF16)
            ucol0 = half * Th
            ncols = Th + 2
            n_win = 0
            for wi, (woff, W) in enumerate(_splits(ncols, 256)):
                wp = wi % 2
                uwin = uwin2[wp]
                n_win += 1
                for g in range(16):
                    S.dma("qs", f"uw{wp}", (lambda g, woff, W, ucol0, uwin: lambda e: e.dma_start(
                        out=uwin[:, g * 256:g * 256 + W],
                        in_=u_h[g * 128:(g + 1) * 128, ucol0 + woff:ucol0 + woff + W]))(g, woff, W, ucol0, uwin),
                        reads=(f"uw:{wp}",), writes=(f"uw:{wp}",))
                    S.op("act", (lambda g, W, uwin: lambda e: e.activation(
                        st16[:, g * 256:g * 256 + W], uwin[:, g * 256:g * 256 + W],
                        AF.Identity))(g, W, uwin),
                        reads=(f"uw:{wp}", f"st16:{g}"), writes=(f"st16:{g}",))
                ln_stats(lambda g: st16[:, g * 256:g * 256 + W],
                         lambda g: (f"st16:{g}",), W, epsc[0:1, 0:1])
                for g in range(16):
                    S.op("dve", (lambda g, W, uwin: lambda e: e.tensor_sub(
                        zn[:, (g % 2) * 256:(g % 2) * 256 + W],
                        uwin[:, g * 256:g * 256 + W], pst[:, 0:W]))(g, W, uwin),
                        reads=(f"uw:{wp}", "pst", f"zn:{g % 2}"), writes=(f"zn:{g % 2}",))
                    S.op("dve", (lambda g, W, woff: lambda e: e.tensor_mul(
                        lnu[:, g * (Th + 2) + woff:g * (Th + 2) + woff + W],
                        zn[:, (g % 2) * 256:(g % 2) * 256 + W], pb[3][:, 0:W]))(g, W, woff),
                        reads=(f"zn:{g % 2}", "pb3", f"lnw:{wi}"), writes=(f"lnw:{wi}",))
            def lnu_tok_range(toff, tw):
                w0 = toff // 256
                w1 = (toff + tw - 1) // 256
                return tuple(f"lnw:{w}" for w in range(w0, min(w1, n_win - 1) + 1))
            ttiles = _splits(ncols, 512)
            for osl in range(16):
                par = osl % 2
                wb = wbufA if par == 0 else wbufB
                win_v = win_h.ap().rearrange("(kc p) o -> p kc o", p=128)
                S.dma("qg", f"p1w{par}", (lambda osl, wb: lambda e: e.dma_start(
                    out=wb[:, :].rearrange("p (kc o) -> p kc o", kc=16),
                    in_=win_v[:, :, osl * 512:(osl + 1) * 512]))(osl, wb),
                    reads=(f"p1w:{par}",), writes=(f"p1w:{par}",))
                for osub in range(4):
                    og = osl * 512 + osub * 128
                    zpar = (osl * 4 + osub) % 2
                    zx = zxA if zpar == 0 else zxB
                    zx_tok = f"zx:{zpar}"
                    for ti, (toff, tw) in enumerate(ttiles):
                        bi = (osub * len(ttiles) + ti) % 4
                        for k in range(16):
                            S.op("pe", (lambda wb, osub, k, toff, tw, bi: lambda e: e.matmul(
                                pb[bi][:, 0:tw],
                                wb[:, k * 512 + osub * 128:k * 512 + osub * 128 + 128],
                                lnu[:, k * (Th + 2) + toff:k * (Th + 2) + toff + tw],
                                start=(k == 0), stop=(k == 15)))(wb, osub, k, toff, tw, bi),
                                reads=(f"p1w:{par}",) + lnu_tok_range(toff, tw), writes=(f"pb{bi}",))
                        S.op("dve", (lambda bi, toff, tw, zx: lambda e: e.tensor_copy(
                            zx[:, toff:toff + tw], pb[bi][:, 0:tw]))(bi, toff, tw, zx),
                            reads=(f"pb{bi}", zx_tok), writes=(zx_tok,))
                    if og < D:
                        S.op("act", (lambda zx: lambda e: e.activation(
                            cb0[:, 0:Th], zx[:, 1:Th + 1], AF.Silu))(zx),
                            reads=(zx_tok, "cb0"), writes=("cb0",))
                        S.dma("qa", "scb0", (lambda og, half: lambda e: e.dma_start(
                            out=zs_d[og:og + 128, half * Th:half * Th + Th],
                            in_=cb0[:, 0:Th]))(og, half),
                            reads=("cb0",), writes=("d:zs",))
                    else:
                        c0 = ((og - D) // 128) * 4
                        S.op("dve", (lambda zx, c0: lambda e: e.tensor_scalar(
                            cb0[:, 0:Th], zx[:, 0:Th], convp[:, c0:c0 + 1], None, ALU.mult))(zx, c0),
                            reads=(zx_tok, "c:convp", "cb0"), writes=("cb0",))
                        S.op("dve", (lambda zx, c0: lambda e: e.tensor_scalar(
                            cb1[:, 0:Th], zx[:, 1:Th + 1], convp[:, c0 + 1:c0 + 2], None, ALU.mult))(zx, c0),
                            reads=(zx_tok, "c:convp", "cb1"), writes=("cb1",))
                        S.op("dve", (lambda zx, c0: lambda e: e.tensor_scalar(
                            cb2[:, 0:Th], zx[:, 2:Th + 2], convp[:, c0 + 2:c0 + 3], None, ALU.mult))(zx, c0),
                            reads=(zx_tok, "c:convp", "cb2"), writes=("cb2",))
                        S.op("dve", lambda e: e.tensor_add(cb0[:, 0:Th], cb0[:, 0:Th], cb1[:, 0:Th]),
                             reads=("cb0", "cb1"), writes=("cb0",))
                        S.op("dve", lambda e: e.tensor_add(cb0[:, 0:Th], cb0[:, 0:Th], cb2[:, 0:Th]),
                             reads=("cb0", "cb2"), writes=("cb0",))
                        S.op("act", (lambda c0: lambda e: e.activation(
                            cb1[:, 0:Th], cb0[:, 0:Th], AF.Silu, bias=convp[:, c0 + 3:c0 + 4]))(c0),
                            reads=("cb0", "c:convp", "cb1"), writes=("cb1",))
                        S.dma("qa", "scb1", (lambda og, half: lambda e: e.dma_start(
                            out=xbc_d[og - D:og - D + 128, half * Th:half * Th + Th],
                            in_=cb1[:, 0:Th]))(og, half),
                            reads=("cb1",), writes=("d:xbc",))

        if DBG["skip_rest"]:
            S.op("dve", lambda e: e.tensor_copy(zn[0:1, 0:1], zn[0:1, 0:1]),
                 writes=("fence",),
                 extra_waits=tuple((s, c) for s, c in S.count.items()
                                   if s.startswith("D:")))
            S.emit(block, lambda s: es.enter_context(
                nc.semaphore(f"sem_{s.replace(':', '_')}")))
            return nc
        S.barrier()
        p2ar = root.sub()
        wo = p2ar.alloc("wo", [128, 16 * 2048], BF16)
        st16b = p2ar.alloc("st16b", [64, 32 * ngc], BF16)
        sin16b = p2ar.alloc("sin16b", [64, 32], BF16)
        spar_a = p2ar.sub()
        cs_t = spar_a.alloc("cs_t", [128, 16 * ngc], F32)
        stt_t = spar_a.alloc("stt_t", [128, 16 * ngc], F32)
        st16_t = spar_a.alloc("st16_t", [128, 16 * ngc], BF16)
        s63 = spar_a.alloc("s63", [128, 16], F32)
        sinb = spar_a.alloc("sinb", [128, 64], F32)
        sin_t = spar_a.alloc("sin_t", [128, 16], F32)
        sin16 = spar_a.alloc("sin16", [128, 16], BF16)
        corr = spar_a.alloc("corr", [128, CHUNK], F32)
        spx = [spar_a.alloc(f"spx{i}", [128, T], BF16) for i in range(2)]
        spb = [spar_a.alloc(f"spb{i}", [128, T], BF16) for i in range(2)]
        spP = [spar_a.alloc(f"spP{i}", [128, T], F32) for i in range(2)]
        S.dma("qg", "wo", lambda e: e.dma_start(
            out=wo[:, :].rearrange("p (kc o) -> p kc o", kc=16),
            in_=wout_h.ap().rearrange("(kc p) o -> p kc o", p=128)),
            writes=("woR",))
        wo_toks = ("woR",)
        for g in range(16):
            par = g % 2
            S.dma("qs", f"spx{par}", (lambda g, par: lambda e: e.dma_start(
                out=spx[par][:, :], in_=xbc_d[g * 128:(g + 1) * 128, :]))(g, par),
                reads=(f"spx:{par}", "d:xbc"), writes=(f"spx:{par}",))
            S.dma("qs", f"spb{par}", (lambda g, par: lambda e: e.dma_start(
                out=spb[par][:, :], in_=xbc_d[D + g * 128:D + (g + 1) * 128, :]))(g, par),
                reads=(f"spb:{par}", "d:xbc"), writes=(f"spb:{par}",))
            S.op("dve", (lambda g, par: lambda e: e.tensor_mul(
                spP[par][:, :], spx[par][:, :], spb[par][:, :]))(g, par),
                reads=(f"spx:{par}", f"spb:{par}", f"spP:{par}"), writes=(f"spP:{par}",))
            S.op("dve", (lambda g, par: lambda e: e.tensor_reduce(
                cs_t[:, g * ngc:(g + 1) * ngc],
                spP[par][:, :].rearrange("p (c t) -> p c t", t=CHUNK),
                mybir.AxisListType.X, ALU.add))(g, par),
                reads=(f"spP:{par}",), writes=(f"cs:{g}",))
            S.op("dve", (lambda g: lambda e: e.tensor_tensor_scan(
                stt_t[:, g * ngc:(g + 1) * ngc],
                d16[:, g:g + 1].broadcast_to((128, ngc)),
                cs_t[:, g * ngc:(g + 1) * ngc], 0.0, ALU.mult, ALU.add))(g),
                reads=(f"cs:{g}", "c:d16", f"stt:{g}"), writes=(f"stt:{g}",))
        S.op("dve", lambda e: e.tensor_copy(
            s63[:, :], stt_t[:, :].rearrange("p (g c) -> p g c", c=ngc)[:, :, ngc - 1]),
            reads=tuple(f"stt:{g}" for g in range(16)), writes=("s63",))
        S.op("dve", lambda e: e.memset(sinb[:, :], 0.0), writes=("sinb",))
        for b in range(4):
            S.op("dve", (lambda b: lambda e: e.tensor_scalar(
                sinb[:, b * 16:(b + 1) * 16], s63[:, :], wsel[:, b:b + 1], None, ALU.mult))(b),
                reads=("s63", "c:wsel", "sinb"), writes=("sinb",))
        S.dma("qa", "scc", lambda e: e.dma_start(
            out=cc_in.ap().rearrange("b (r g) -> b r g", r=128).transpose([1, 0, 2]),
            in_=sinb[:, :].rearrange("p (b g) -> p b g", b=4)),
            reads=("sinb",), writes=("d:ccin",))
        if DBG["no_cc"]:
            S.dma("qs", "ccX", lambda e: e.dma_start(out=cc_out.ap(), in_=cc_in.ap()),
                  reads=("d:ccin",), writes=("d:ccout",))
        else:
            S.op("cc", lambda e: e.collective_compute(
                "AllReduce", ALU.add, replica_groups=[list(range(ncores))],
                ins=[cc_in.ap().opt()], outs=[cc_out.ap().opt()]),
                reads=("d:ccin",), writes=("d:ccout",), inc=1)
        S.dma("qs", "rcc", lambda e: e.dma_start(
            out=sinb[:, :].rearrange("p (b g) -> p b g", b=4),
            in_=cc_out.ap().rearrange("b (r g) -> b r g", r=128).transpose([1, 0, 2])),
            reads=("d:ccout", "sinb"), writes=("sinb",))
        S.op("dve", lambda e: e.tensor_scalar(
            sin_t[:, :], sinb[:, 0:16], ohsel[:, 0:1], None, ALU.mult),
            reads=("sinb", "c:ohsel", "sin"), writes=("sin",))
        for b in range(1, 4):
            S.op("dve", (lambda b: lambda e: e.scalar_tensor_tensor(
                sin_t[:, :], sinb[:, b * 16:(b + 1) * 16], ohsel[:, b:b + 1],
                sin_t[:, :], ALU.mult, ALU.add))(b),
                reads=("sinb", "c:ohsel", "sin"), writes=("sin",))
        for g in range(16):
            S.op("dve", (lambda g: lambda e: e.tensor_tensor_scan(
                corr[:, 0:ngc], d16[:, g:g + 1].broadcast_to((128, ngc)),
                zeros64[:, 0:ngc], sin_t[:, g:g + 1], ALU.mult, ALU.add))(g),
                reads=("sin", "c:d16", "c:zeros64", "corrtok"), writes=("corrtok",))
            S.op("dve", (lambda g: lambda e: e.tensor_add(
                stt_t[:, g * ngc:(g + 1) * ngc], stt_t[:, g * ngc:(g + 1) * ngc],
                corr[:, 0:ngc]))(g),
                reads=("corrtok", f"stt:{g}"), writes=(f"stt:{g}",))
        S.op("dve", lambda e: e.tensor_copy(st16_t[:, :], stt_t[:, :]),
             reads=tuple(f"stt:{g}" for g in range(16)), writes=("st16t",))
        S.op("dve", lambda e: e.tensor_copy(sin16[:, :], sin_t[:, :]),
             reads=("sin",), writes=("sin16",))

        if DBG["sprint_only"]:
            S.op("dve", lambda e: e.tensor_copy(zn[0:1, 0:1], zn[0:1, 0:1]),
                 writes=("fence",),
                 extra_waits=tuple((s, c) for s, c in S.count.items()
                                   if s.startswith("D:")))
            S.emit(block, lambda s: es.enter_context(
                nc.semaphore(f"sem_{s.replace(':', '_')}")))
            return nc
        S.dma("qs", "s16b", lambda e: e.dma_start(
            out=st16b[0:64, 0:16 * ngc], in_=st16_t[0:64, :]),
            reads=("st16t",), writes=("st16b",))
        S.dma("qs", "s16b", lambda e: e.dma_start(
            out=st16b[0:64, 16 * ngc:32 * ngc], in_=st16_t[64:128, :]),
            reads=("st16t",), writes=("st16b",))
        S.dma("qs", "s16c", lambda e: e.dma_start(
            out=sin16b[0:64, 0:16], in_=sin16[0:64, :]),
            reads=("sin16",), writes=("sin16b",))
        S.dma("qs", "s16c", lambda e: e.dma_start(
            out=sin16b[0:64, 16:32], in_=sin16[64:128, :]),
            reads=("sin16",), writes=("sin16b",))

        S.barrier()
        ar = p2ar.sub()
        xg = [ar.alloc(f"xg{i}", [128, 16 * 128], BF16) for i in range(1)]
        zg = [ar.alloc(f"zg{i}", [128, 16 * 128], BF16) for i in range(2)]
        bg = [ar.alloc(f"bg{i}", [64, 32 * 128], BF16) for i in range(1)]
        cg = [ar.alloc(f"cg{i}", [64, 32 * 128], BF16) for i in range(1)]
        xTc = [ar.alloc(f"xTc{i}", [64, 2 * 2048], BF16) for i in range(2)]
        gms = ar.alloc("gms", [64, 512], BF16)
        ygrp = ar.alloc("ygrp", [128, 16 * 128], F32)
        ynb = ar.alloc("ynb", [128, 16 * 512], BF16)
        mo16 = ar.alloc("mo16", [128, 16 * 256], BF16)
        x1b = ar.alloc("x1b", [128, 16 * 256], F32)
        uw2 = ar.alloc("uw2", [128, 16 * 256], F32)

        pend_gate = []
        for sgi, (sgoff, sgw) in enumerate(_splits(T, 512)):
            ngrp = sgw // 128
            for grp in range(ngrp):
                goff = sgoff + grp * 128
                gpar = grp % 2
                xbc_g = xbc_d.ap().rearrange("(g p) t -> p g t", p=128)
                xbc_h = xbc_d.ap().rearrange("(h n) t -> n h t", n=64)
                zs_g = zs_d.ap().rearrange("(g p) t -> p g t", p=128)
                S.dma("qs", "xg0", (lambda goff: lambda e: e.dma_start(
                    out=xg[0][:, :].rearrange("p (g t) -> p g t", g=16),
                    in_=xbc_g[:, 0:16, goff:goff + 128]))(goff),
                    reads=("xg:0", "d:xbc"), writes=("xg:0",))
                S.dma("qs", f"zg{gpar}", (lambda goff, gpar: lambda e: e.dma_start(
                    out=zg[gpar][:, :].rearrange("p (g t) -> p g t", g=16),
                    in_=zs_g[:, 0:16, goff:goff + 128]))(goff, gpar),
                    reads=(f"zg:{gpar}", "d:zs"), writes=(f"zg:{gpar}",))
                S.dma("qs", "bgL", (lambda goff, gpar: lambda e: e.dma_start(
                    out=bg[0][0:64, :].rearrange("p (h t) -> p h t", h=32),
                    in_=xbc_h[:, 32:64, goff:goff + 128]))(goff, gpar),
                    reads=("bg:L", "d:xbc"), writes=("bg:L",))
                S.dma("qs", "cgL", (lambda goff, gpar: lambda e: e.dma_start(
                    out=cg[0][0:64, :].rearrange("p (h t) -> p h t", h=32),
                    in_=xbc_h[:, 64:96, goff:goff + 128]))(goff, gpar),
                    reads=("cg:L", "d:xbc"), writes=("cg:L",))
                if "tp" in DBG["parts"]:
                    for ci in range(2):
                        for g in range(16):
                            slot = g % 8
                            S.op("pe", (lambda g, ci, gpar, slot: lambda e: e.transpose(
                                pT[0:64, slot * 128:(slot + 1) * 128],
                                xg[0][:, g * 128 + ci * 64:g * 128 + ci * 64 + 64],
                                ident[:, :]))(g, ci, gpar, slot),
                                reads=("xg:0", "c:ident", "pT"), writes=("pT",))
                            if g % 2 == 1:
                                S.op("dve", (lambda g, ci, gpar, slot: lambda e: e.tensor_copy(
                                    xTc[gpar][0:64, ci * 2048 + (g - 1) * 128:ci * 2048 + (g + 1) * 128],
                                    pT[0:64, (slot - 1) * 128:(slot + 1) * 128]))(g, ci, gpar, slot),
                                    reads=("pT", f"xT:{gpar}"), writes=(f"xT:{gpar}",))
                ssd_banks = [(pS[0], "pS0"), (pS[1], "pS1"), (pb[2], "pb2")]

                def emit_G(gh):
                    bank, btok = ssd_banks[gh % 3]
                    if "g" not in DBG["parts"]:
                        return
                    for hi in range(2):
                        h = 2 * gh + hi
                        for ci in range(2):
                            S.op("pe", (lambda h, hi, ci, gpar, bank: lambda e: e.matmul(
                                bank[0:64, (hi * 2 + ci) * 64:(hi * 2 + ci) * 64 + 64],
                                bg[0][0:64, h * 128 + ci * 64:h * 128 + ci * 64 + 64],
                                cg[0][0:64, h * 128 + ci * 64:h * 128 + ci * 64 + 64],
                                start=True, stop=True))(h, hi, ci, gpar, bank),
                                reads=("bg:L", "cg:L", btok),
                                writes=(btok,))

                def emit_rest(gh):
                    bank, btok = ssd_banks[gh % 3]
                    c_base = goff // CHUNK
                    if "ge" in DBG["parts"]:
                        S.op("dve", (lambda gh, bank: lambda e: e.tensor_mul(
                            gms[0:64, (gh % 2) * 256:(gh % 2) * 256 + 256],
                            bank[0:64, 0:256], mask4[0:64, :]))(gh, bank),
                            reads=(btok, "c:mask4", f"gms:{gh % 2}"), writes=(f"gms:{gh % 2}",))
                    for hi in range(2 if ("y1" in DBG["parts"] or "y2" in DBG["parts"]) else 0):
                        h = 2 * gh + hi
                        hb = hi * 64
                        for ci in range(2):
                            c_loc = c_base + ci
                            if c_loc == 0:
                                prevap = sin16b[0:64, (h % 2) * 16 + h // 2:(h % 2) * 16 + h // 2 + 1]
                                ptok = "sin16b"
                            else:
                                pcol = ((h % 2) * 16 + h // 2) * ngc + c_loc - 1
                                prevap = st16b[0:64, pcol:pcol + 1]
                                ptok = "st16b"
                            if "y1" in DBG["parts"]:
                                S.op("pe", (lambda h, hi, hb, ci, gh, gpar, bank: lambda e: e.matmul(
                                    bank[hb:hb + 64, 256 + ci * 64:256 + ci * 64 + 64],
                                    xTc[gpar][0:64, ci * 2048 + h * 64:ci * 2048 + h * 64 + 64],
                                    gms[0:64, (gh % 2) * 256 + (hi * 2 + ci) * 64:
                                        (gh % 2) * 256 + (hi * 2 + ci) * 64 + 64],
                                    start=True, stop=("y2" not in DBG["parts"])))(h, hi, hb, ci, gh, gpar, bank),
                                    reads=(f"gms:{gh % 2}", f"xT:{gpar}", btok), writes=(btok,))
                            if "y2" in DBG["parts"]:
                                S.op("pe", (lambda h, hb, ci, gpar, prevap, bank: lambda e: e.matmul(
                                    bank[hb:hb + 64, 256 + ci * 64:256 + ci * 64 + 64],
                                    prevap.broadcast_to((64, 64)),
                                    cg[0][0:64, h * 128 + ci * 64:h * 128 + ci * 64 + 64],
                                    start=("y1" not in DBG["parts"]), stop=True))(h, hb, ci, gpar, prevap, bank),
                                    reads=(ptok, "cg:L", btok), writes=(btok,))
                    if "y1" in DBG["parts"] or "y2" in DBG["parts"]:
                        S.op("dve", (lambda gh, gpar, bank: lambda e: e.tensor_mul(
                            ygrp[:, gh * 128:(gh + 1) * 128], bank[:, 256:384],
                            zg[gpar][:, gh * 128:(gh + 1) * 128]))(gh, gpar, bank),
                            reads=(btok, f"zg:{gpar}", f"yg:{gh}"), writes=(f"yg:{gh}",))

                for gh in range(16):
                    emit_G(gh)
                    if gh >= 2:
                        emit_rest(gh - 2)
                emit_rest(14)
                emit_rest(15)
                def gate_seg(grp, gpar):
                    for g in range(16 if "gate" in DBG["parts"] else 0):
                        S.op("act", (lambda g: lambda e: e.activation(
                            sq16[:, g * 256:g * 256 + 128], ygrp[:, g * 128:(g + 1) * 128],
                            AF.Square))(g),
                            reads=(f"yg:{g}", f"sq16:{g}"), writes=(f"sq16:{g}",))
                    for g in range(16 if "gate" in DBG["parts"] else 0):
                        S.op("pe", (lambda g: lambda e: e.matmul(
                            pst[0:1, 0:128], onesb[:, 0:1], sq16[:, g * 256:g * 256 + 128],
                            start=(g == 0), stop=(g == 15)))(g),
                            reads=(f"sq16:{g}", "c:onesb"), writes=("pst",))
                    S.op("dve", lambda e: e.tensor_scalar(
                        rows[0:1, 256:384], pst[0:1, 0:128], 1.0 / D, None, ALU.mult),
                        reads=("pst", "r_msq"), writes=("r_msq",))
                    S.op("act", lambda e: e.activation(
                        rows[0:1, 1024:1152], rows[0:1, 256:384], AF.Sqrt, bias=epsc[0:1, 1:2]),
                        reads=("r_msq", "r_sd", "c:epsc"), writes=("r_sd",))
                    S.op("dve", lambda e: e.reciprocal(
                        rows[0:1, 1280:1408], rows[0:1, 1024:1152]),
                        reads=("r_sd", "r_rstd"), writes=("r_rstd",))
                    S.op("pe", lambda e: e.matmul(
                        pb[3][:, 0:128], ones1[0:1, :], rows[0:1, 1280:1408],
                        start=True, stop=True),
                        reads=("r_rstd", "c:ones1", "pb3"), writes=("pb3",))
                    for g in range(16):
                        S.op("dve", (lambda g, grp: lambda e: e.scalar_tensor_tensor(
                            ynb[:, g * 512 + grp * 128:g * 512 + grp * 128 + 128],
                            ygrp[:, g * 128:(g + 1) * 128], rmsp[:, g:g + 1],
                            pb[3][:, 0:128], ALU.mult, ALU.mult))(g, grp),
                            reads=(f"yg:{g}", "c:rmsp", "pb3", f"yn:{g}"), writes=(f"yn:{g}",))
                pend_gate.append((grp, gpar))
                gate_seg(*pend_gate.pop(0))
            while pend_gate:
                gate_seg(*pend_gate.pop(0))
            yn_toks = tuple(f"yn:{g}" for g in range(16))
            for wi, (woff, W) in enumerate(_splits(sgw, 256)):
                for od in range(16):
                    bi = od % 2
                    for k in range(16):
                        S.op("pe", (lambda od, k, bi, woff, W: lambda e: e.matmul(
                            pb[bi][:, 0:W],
                            wo[:, k * 2048 + od * 128:k * 2048 + od * 128 + 128],
                            ynb[:, k * 512 + woff:k * 512 + woff + W],
                            start=(k == 0), stop=(k == 15)))(od, k, bi, woff, W),
                            reads=wo_toks + yn_toks + (f"pb{bi}",), writes=(f"pb{bi}",))
                    S.op("act", (lambda od, bi, W: lambda e: e.activation(
                        mo16[:, od * 256:od * 256 + W], pb[bi][:, 0:W], AF.Identity))(od, bi, W),
                        reads=(f"pb{bi}", f"mo:{od}"), writes=(f"mo:{od}",))
                for g in range(4):
                    pass
                ln_stats(lambda g: mo16[:, g * 256:g * 256 + W],
                         lambda g: (f"mo:{g}",), W, epsc[0:1, 0:1])
                S.dma("qs", "uw2", (lambda sgoff, woff, W: lambda e: e.dma_start(
                    out=uw2[:, :].rearrange("p (g t) -> p g t", g=16)[:, :, 0:W],
                    in_=u_h.ap().rearrange("(g p) t -> p g t", p=128)[
                        :, :, sgoff + woff + 1:sgoff + woff + 1 + W]))(sgoff, woff, W),
                    reads=("uw2",), writes=("uw2",))
                for g in range(16):
                    S.op("dve", (lambda g, W: lambda e: e.tensor_sub(
                        zn[:, (g % 2) * 256:(g % 2) * 256 + W],
                        mo16[:, g * 256:g * 256 + W], pst[:, 0:W]))(g, W),
                        reads=(f"mo:{g}", "pst", f"zn:{g % 2}"), writes=(f"zn:{g % 2}",))
                    S.op("dve", (lambda g, W: lambda e: e.tensor_mul(
                        zn[:, (g % 2) * 256:(g % 2) * 256 + W],
                        zn[:, (g % 2) * 256:(g % 2) * 256 + W], pb[3][:, 0:W]))(g, W),
                        reads=(f"zn:{g % 2}", "pb3"), writes=(f"zn:{g % 2}",))
                    S.op("dve", (lambda g, W: lambda e: e.tensor_add(
                        x1b[:, g * 256:g * 256 + W],
                        zn[:, (g % 2) * 256:(g % 2) * 256 + W],
                        uw2[:, g * 256:g * 256 + W]))(g, W),
                        reads=(f"zn:{g % 2}", "uw2", f"x1:{g}", "sx1R"), writes=(f"x1:{g}",))
                    S.op("act", (lambda g, W: lambda e: e.activation(
                        st16[:, g * 256:g * 256 + W], x1b[:, g * 256:g * 256 + W],
                        AF.Identity))(g, W),
                        reads=(f"x1:{g}", f"st16:{g}", "slxR"), writes=(f"st16:{g}",))
                S.dma("qa", "sx1", (lambda sgoff, woff, W: lambda e: e.dma_start(
                    out=x1_d.ap().rearrange("(g p) t -> p g t", p=128)[
                        :, :, sgoff + woff:sgoff + woff + W],
                    in_=x1b[:, :].rearrange("p (g t) -> p g t", g=16)[:, :, 0:W]))(sgoff, woff, W),
                    reads=tuple(f"x1:{g}" for g in range(16)), writes=("d:x1", "sx1R"))
                ln_stats(lambda g: st16[:, g * 256:g * 256 + W],
                         lambda g: (f"st16:{g}",), W, epsc[0:1, 0:1])
                for g in range(16):
                    S.op("dve", (lambda g, W: lambda e: e.tensor_sub(
                        zn[:, (g % 2) * 256:(g % 2) * 256 + W],
                        x1b[:, g * 256:g * 256 + W], pst[:, 0:W]))(g, W),
                        reads=(f"x1:{g}", "pst", f"zn:{g % 2}"), writes=(f"zn:{g % 2}",))
                    S.op("dve", (lambda g, W: lambda e: e.tensor_mul(
                        st16[:, g * 256:g * 256 + W],
                        zn[:, (g % 2) * 256:(g % 2) * 256 + W], pb[3][:, 0:W]))(g, W),
                        reads=(f"zn:{g % 2}", "pb3", f"st16:{g}"), writes=(f"st16:{g}",))
                S.dma("qa", "slx", (lambda sgoff, woff, W: lambda e: e.dma_start(
                    out=lnx1_d.ap().rearrange("(g p) t -> p g t", p=128)[
                        :, :, sgoff + woff:sgoff + woff + W],
                    in_=st16[:, :].rearrange("p (g t) -> p g t", g=16)[:, :, 0:W]))(sgoff, woff, W),
                    reads=tuple(f"st16:{g}" for g in range(16)), writes=("d:lnx1", "slxR"))

        if DBG["skip_p3"]:
            S.op("dve", lambda e: e.tensor_copy(zn[0:1, 0:1], zn[0:1, 0:1]),
                 writes=("fence",),
                 extra_waits=tuple((s, c) for s, c in S.count.items()
                                   if s.startswith("D:")))
            S.emit(block, lambda s: es.enter_context(
                nc.semaphore(f"sem_{s.replace(':', '_')}")))
            return nc
        S.barrier()
        ar = root.sub()
        lnx1r = ar.alloc("lnx1r", [128, 16 * T], BF16)
        wbufA = ar.alloc("p3wA", [128, 16 * 512], BF16)
        wbufB = ar.alloc("p3wB", [128, 16 * 512], BF16)
        gs = [ar.alloc(f"gs{i}", [128, 512], BF16) for i in range(2)]
        hs = [ar.alloc(f"hs{i}", [128, 512], BF16) for i in range(2)]
        for g in range(16):
            S.dma("qs", "lx", (lambda g: lambda e: e.dma_start(
                out=lnx1r[:, g * T:(g + 1) * T],
                in_=lnx1_d[g * 128:(g + 1) * 128, :]))(g),
                reads=("d:lnx1",), writes=("lxR",))
        lx_toks = ("lxR",)
        t3 = _splits(T, 512)
        w1_v = w1_h.ap().rearrange("(kc p) o -> p kc o", p=128)
        w3_v = w3_h.ap().rearrange("(kc p) o -> p kc o", p=128)
        for osl in range(16):
            S.dma("qg", "p3w1", (lambda osl: lambda e: e.dma_start(
                out=wbufA[:, :].rearrange("p (kc o) -> p kc o", kc=16),
                in_=w1_v[:, :, osl * 512:(osl + 1) * 512]))(osl),
                reads=("p3w1R",), writes=("p3w1R",))
            S.dma("qg", "p3w3", (lambda osl: lambda e: e.dma_start(
                out=wbufB[:, :].rearrange("p (kc o) -> p kc o", kc=16),
                in_=w3_v[:, :, osl * 512:(osl + 1) * 512]))(osl),
                reads=("p3w3R",), writes=("p3w3R",))
            for osub in range(4):
                og = osl * 512 + osub * 128
                for ti, (toff, tw) in enumerate(t3):
                    bi1 = (2 * (osub * len(t3) + ti)) % 4
                    bi3 = bi1 + 1 if bi1 % 2 == 0 else bi1 - 1
                    bi3 = (bi1 + 1) % 4
                    for k in range(16):
                        S.op("pe", (lambda osub, k, toff, tw, bi1: lambda e: e.matmul(
                            pb[bi1][:, 0:tw],
                            wbufA[:, k * 512 + osub * 128:k * 512 + osub * 128 + 128],
                            lnx1r[:, k * T + toff:k * T + toff + tw],
                            start=(k == 0), stop=(k == 15)))(osub, k, toff, tw, bi1),
                            reads=("p3w1R",) + lx_toks, writes=(f"pb{bi1}",))
                    for k in range(16):
                        S.op("pe", (lambda osub, k, toff, tw, bi3: lambda e: e.matmul(
                            pb[bi3][:, 0:tw],
                            wbufB[:, k * 512 + osub * 128:k * 512 + osub * 128 + 128],
                            lnx1r[:, k * T + toff:k * T + toff + tw],
                            start=(k == 0), stop=(k == 15)))(osub, k, toff, tw, bi3),
                            reads=("p3w3R",) + lx_toks, writes=(f"pb{bi3}",))
                    sp = ti % 2
                    S.op("act", (lambda bi1, tw, sp: lambda e: e.activation(
                        gs[sp][:, 0:tw], pb[bi1][:, 0:tw], AF.Silu))(bi1, tw, sp),
                        reads=(f"pb{bi1}", f"gs:{sp}"), writes=(f"gs:{sp}",))
                    S.op("dve", (lambda bi3, tw, sp: lambda e: e.tensor_mul(
                        hs[sp][:, 0:tw], pb[bi3][:, 0:tw], gs[sp][:, 0:tw]))(bi3, tw, sp),
                        reads=(f"pb{bi3}", f"gs:{sp}", f"hs:{sp}"), writes=(f"hs:{sp}",))
                    S.dma("qa", f"shid{sp}", (lambda og, toff, tw, sp: lambda e: e.dma_start(
                        out=hid_d[og:og + 128, toff:toff + tw],
                        in_=hs[sp][:, 0:tw]))(og, toff, tw, sp),
                        reads=(f"hs:{sp}",), writes=("d:hid",))

        S.barrier()
        ar = root.sub()
        pend_p3 = []
        hidb = ar.alloc("hidb", [128, 64 * 512], BF16)
        m2b = [ar.alloc(f"m2b{i}", [128, 64 * 128], BF16) for i in range(2)]
        fo16p = [ar.alloc(f"fo16{i}", [128, 16 * 512], BF16) for i in range(2)]
        x1w = ar.alloc("x1w", [128, 16 * 256], F32)
        outb = ar.alloc("outb", [128, 16 * 256], F32)
        m2_k = m2_d.ap().rearrange("(ds p) c -> ds p c", p=128)
        for ti, (toff, tw) in enumerate(t3):
            hid_v = hid_d.ap().rearrange("(hc p) t -> p hc t", p=128)
            for hh2 in range(2):
                S.dma("qs", "h3", (lambda hh2, toff, tw: lambda e: e.dma_start(
                    out=hidb[:, :].rearrange("p (hc t) -> p hc t", hc=64)[
                        :, hh2 * 32:(hh2 + 1) * 32, 0:tw],
                    in_=hid_v[:, hh2 * 32:(hh2 + 1) * 32, toff:toff + tw]))(hh2, toff, tw),
                    reads=("h3R", "d:hid"), writes=("h3R",))
            for dsub in range(16):
                wpar = dsub % 2
                S.dma("qa" if wpar == 0 else "qs", f"m2w{wpar}", (lambda dsub, wpar: lambda e: e.dma_start(
                    out=m2b[wpar][:, :], in_=m2_k[dsub, :, :]))(dsub, wpar),
                    reads=(f"m2w:{wpar}", "d:m2"), writes=(f"m2w:{wpar}",))
                bi = dsub % 2
                for k in range(64):
                    S.op("pe", (lambda k, wpar, bi, tw: lambda e: e.matmul(
                        pb[bi][:, 0:tw], m2b[wpar][:, k * 128:(k + 1) * 128],
                        hidb[:, k * 512:k * 512 + tw],
                        start=(k == 0), stop=(k == 63)))(k, wpar, bi, tw),
                        reads=(f"m2w:{wpar}", "h3R", f"pb{bi}"), writes=(f"pb{bi}",))
                tpar = ti % 2
                S.op("act", (lambda dsub, bi, tw, tpar: lambda e: e.activation(
                    fo16p[tpar][:, dsub * 512:dsub * 512 + tw], pb[bi][:, 0:tw],
                    AF.Identity))(dsub, bi, tw, tpar),
                    reads=(f"pb{bi}", f"fo:{tpar}:{dsub}"), writes=(f"fo:{tpar}:{dsub}",))
            def p3b_ln(toff, tw, tpar):
                for woff, W in _splits(tw, 256):
                    ln_stats((lambda g, tpar=tpar: fo16p[tpar][:, g * 512 + woff:g * 512 + woff + W]),
                         (lambda g, tpar=tpar: (f"fo:{tpar}:{g}",)), W, epsc[0:1, 0:1])
                    S.dma("qs", "x1w", (lambda toff, woff, W: lambda e: e.dma_start(
                        out=x1w[:, :].rearrange("p (g t) -> p g t", g=16)[:, :, 0:W],
                        in_=x1_d.ap().rearrange("(g p) t -> p g t", p=128)[
                            :, :, toff + woff:toff + woff + W]))(toff, woff, W),
                        reads=("x1w", "d:x1"), writes=("x1w",))
                    for g in range(16):
                        S.op("dve", (lambda g, woff, W, tpar: lambda e: e.tensor_sub(
                            zn[:, (g % 2) * 256:(g % 2) * 256 + W],
                            fo16p[tpar][:, g * 512 + woff:g * 512 + woff + W], pst[:, 0:W]))(g, woff, W, tpar),
                            reads=(f"fo:{tpar}:{g}", "pst", f"zn:{g % 2}"), writes=(f"zn:{g % 2}",))
                        S.op("dve", (lambda g, W: lambda e: e.tensor_mul(
                        zn[:, (g % 2) * 256:(g % 2) * 256 + W],
                        zn[:, (g % 2) * 256:(g % 2) * 256 + W], pb[3][:, 0:W]))(g, W),
                        reads=(f"zn:{g % 2}", "pb3"), writes=(f"zn:{g % 2}",))
                        S.op("dve", (lambda g, W: lambda e: e.tensor_add(
                            outb[:, g * 256:g * 256 + W],
                            zn[:, (g % 2) * 256:(g % 2) * 256 + W],
                            x1w[:, g * 256:g * 256 + W]))(g, W),
                            reads=(f"zn:{g % 2}", "x1w", f"ob:{g}", "obR"), writes=(f"ob:{g}",))
                    S.dma("qa", "so", (lambda toff, woff, W: lambda e: e.dma_start(
                        out=out_h.ap().rearrange("(g p) t -> p g t", p=128)[
                            :, :, toff + woff:toff + woff + W],
                        in_=outb[:, :].rearrange("p (g t) -> p g t", g=16)[:, :, 0:W]))(toff, woff, W),
                        reads=tuple(f"ob:{g}" for g in range(16)), writes=("d:out", "obR"))


            pend_p3.append((toff, tw, ti % 2, p3b_ln))
            if len(pend_p3) > 1:
                a, b, c, f = pend_p3.pop(0)
                f(a, b, c)
        while pend_p3:
            a, b, c, f = pend_p3.pop(0)
            f(a, b, c)
        S.op("dve", lambda e: e.tensor_copy(zn[0:1, 0:1], zn[0:1, 0:1]),
             writes=("fence",),
             extra_waits=tuple((s, c) for s, c in S.count.items()
                               if s.startswith("D:")))
        S.emit(block, lambda s: es.enter_context(
            nc.semaphore(f"sem_{s.replace(':', '_')}")))

    return nc


_CACHE = {}


def _get_program(T, ncores=NCORES):
    key = (T, ncores)
    if key not in _CACHE:
        _CACHE[key] = build_program(T, ncores)
    return _CACHE[key]


def _host_inputs(W_in, W_out, conv_w, conv_b, A_log, rms_w, W1, W2, W3):
    c = {}
    c["win_t"] = np.ascontiguousarray(np.asarray(W_in).T).astype(np.float32)
    c["wout_t"] = np.ascontiguousarray(np.asarray(W_out).T).astype(np.float32)
    c["w1_t"] = np.ascontiguousarray(np.asarray(W1).T).astype(np.float32)
    c["w3_t"] = np.ascontiguousarray(np.asarray(W3).T).astype(np.float32)
    c["w2_t"] = np.ascontiguousarray(np.asarray(W2).T).astype(np.float32)
    cw = np.asarray(conv_w)[:, 0, :]
    cb = np.asarray(conv_b)
    pack = np.concatenate(
        [cw.reshape(48, 128, 3).transpose(1, 0, 2),
         cb.reshape(48, 128).T[:, :, None]], axis=2)
    c["convp"] = np.ascontiguousarray(pack.reshape(128, 192)).astype(np.float32)
    c["rmsp"] = np.ascontiguousarray(np.asarray(rms_w).reshape(16, 128).T).astype(np.float32)
    c["alog"] = np.asarray(A_log).reshape(1, 32).astype(np.float32)
    c["ident"] = np.eye(128, dtype=ml_dtypes.bfloat16)
    c["onesb"] = np.ones((128, 128), dtype=ml_dtypes.bfloat16)
    c["ones1"] = np.ones((1, 128), dtype=np.float32)
    c["h128"] = _hadamard(128).astype(ml_dtypes.bfloat16)
    c["h64"] = _hadamard(64).astype(ml_dtypes.bfloat16)
    m = (np.arange(64)[:, None] <= np.arange(64)[None, :]).astype(np.float32)
    c["mask4"] = np.ascontiguousarray(np.tile(m, (1, 4)))
    c["epsc"] = np.array([[LN_EPS, RMS_EPS]], dtype=np.float32)
    return c


def kernel(L0_attn_u, L0_attn_W_in, L0_attn_W_out, L0_attn_conv_w,
           L0_attn_conv_b, L0_attn_A_log, L0_attn_rms_w, L0_ffn_W1,
           L0_ffn_W2, L0_ffn_W3):
    u = np.asarray(L0_attn_u)
    B, L, _ = u.shape
    T = L // 2
    nc = _get_program(T)
    shared = _host_inputs(L0_attn_W_in, L0_attn_W_out, L0_attn_conv_w,
                          L0_attn_conv_b, L0_attn_A_log, L0_attn_rms_w,
                          L0_ffn_W1, L0_ffn_W2, L0_ffn_W3)
    in_maps = []
    for k in range(NCORES):
        b, half = k // 2, k % 2
        t0 = half * T
        win = np.zeros((T + 2, 2048), dtype=np.float32)
        lo, hi = t0 - 1, t0 + T + 1
        rlo, rhi = max(lo, 0), min(hi, L)
        win[rlo - lo:rhi - lo] = u[b][rlo:rhi]
        m = dict(shared)
        m["u"] = np.ascontiguousarray(win.T)
        ws = np.zeros((128, 4), dtype=np.float32)
        oh = np.zeros((128, 4), dtype=np.float32)
        if half == 0:
            ws[:, b] = 1.0
        else:
            oh[:, b] = 1.0
        m["wsel"] = ws
        m["ohsel"] = oh
        in_maps.append(m)
    res = run_bass_kernel_spmd(nc, in_maps, core_ids=list(range(NCORES)))
    out = np.empty((B, L, 2048), dtype=np.float32)
    for k in range(NCORES):
        b, half = k // 2, k % 2
        out[b, half * T:(half + 1) * T] = np.asarray(res.results[k]["out"]).T
    return out


# version 10 (final)
# speedup vs baseline: 1.1642x; 1.0040x over previous
import numpy as np
import ml_dtypes
import concourse.bass as bass
import concourse.mybir as mybir
from concourse.bass_utils import run_bass_kernel_spmd
from contextlib import ExitStack

F32 = mybir.dt.float32
BF16 = mybir.dt.bfloat16
AF = mybir.ActivationFunctionType
ALU = mybir.AluOpType

D = 2048
NH = 32
CONV_DIM = 6144
HID = 8192
CHUNK = 64
LN_EPS = 1e-5
RMS_EPS = 1.1920929e-07
NCORES = 8

CSTREAMS = {"pe": "tensor", "dve": "vector", "act": "scalar",
            "pool": "gpsimd", "cc": "gpsimd"}
QUEUE_ENG = {"qs": "sync", "qa": "scalar", "qg": "gpsimd"}


def _splits(total, size, base=0):
    out, off = [], 0
    while off < total:
        w = min(size, total - off)
        out.append((base + off, w))
        off += w
    return out


def _hadamard(n):
    i = np.arange(n)
    a = i[:, None] & i[None, :]
    pc = np.zeros((n, n), dtype=np.int64)
    while a.any():
        pc += a & 1
        a >>= 1
    return np.where(pc % 2 == 0, 1.0, -1.0).astype(np.float32)


class Sched:

    def __init__(self):
        self.count = {s: 0 for s in CSTREAMS}
        self.ops = {e: [] for e in set(CSTREAMS.values()) | set(QUEUE_ENG.values())}
        self.buf = {}
        self.seen = {e: {} for e in self.ops}
        self.bar = {}
        self.key_q = {}

    def barrier(self):
        self.bar = dict(self.count)

    def _rec(self, eng, stream, inc, fn, reads, writes, extra_waits):
        need = dict(self.bar)
        for t in reads:
            b = self.buf.get(t)
            if b is None:
                continue
            if b[0] is not None:
                s, c = b[0]
                if need.get(s, 0) < c:
                    need[s] = c
        for t in writes:
            b = self.buf.get(t)
            if b is None:
                continue
            if b[0] is not None:
                s, c = b[0]
                if need.get(s, 0) < c:
                    need[s] = c
            for s, c in b[1].items():
                if need.get(s, 0) < c:
                    need[s] = c
        for s, c in extra_waits:
            if need.get(s, 0) < c:
                need[s] = c
        waits = []
        seen = self.seen[eng]
        for s in sorted(need):
            c = need[s]
            if c <= 0:
                continue
            if s == stream and s == "pe":
                continue
            if seen.get(s, 0) >= c:
                continue
            waits.append((s, c))
            seen[s] = c
        cnt = self.count[stream] + inc
        self.count[stream] = cnt
        self.ops[eng].append((waits, fn, stream, inc))
        for t in reads:
            b = self.buf.setdefault(t, [None, {}])
            b[1][stream] = cnt
        for t in writes:
            self.buf[t] = [(stream, cnt), {}]

    def op(self, stream, fn, reads=(), writes=(), inc=1, extra_waits=()):
        self._rec(CSTREAMS[stream], stream, inc, fn, reads, writes, extra_waits)

    def dma(self, queue, key, fn, reads=(), writes=()):
        stream = "D:" + key
        q0 = self.key_q.setdefault(key, queue)
        assert q0 == queue, f"dma key {key} used on two queues {q0}/{queue}"
        if stream not in self.count:
            self.count[stream] = 0
        self._rec(QUEUE_ENG[queue], stream, 16, fn, reads, writes, ())

    def emit(self, block, mk_sem):
        sems = {s: mk_sem(s) for s in self.count}
        for eng_name in ("sync", "gpsimd", "vector", "scalar", "tensor"):
            ops = self.ops[eng_name]

            def make(ops_local):
                def body(e):
                    for waits, fn, stream, inc in ops_local:
                        for s, c in waits:
                            e.wait_ge(sems[s], c)
                        inst = fn(e)
                        if stream == "cc":
                            inst.then_inc(sems[stream])
                        else:
                            inst.then_inc(sems[stream], inc)
                return body

            getattr(block, eng_name)(make(ops))


class Arena:

    def __init__(self, nc, base, top):
        self.nc = nc
        self.base = base
        self.top = top
        self.off = base
        self.uid = [0]

    def alloc(self, name, shape, dtype):
        nb = int(np.prod(shape[1:])) * mybir.dt.size(dtype)
        off = (self.off + 31) // 32 * 32
        assert off + nb <= self.top, (
            f"SBUF arena overflow: {name} needs {nb}B at {off}, top {self.top}")
        self.uid[0] += 1
        h = self.nc.alloc_sbuf_tensor_at(f"{name}_{self.uid[0]}", list(shape), dtype, offset=off)
        self.off = off + nb
        return h

    def sub(self):
        a = Arena(self.nc, (self.off + 31) // 32 * 32, self.top)
        a.uid = self.uid
        return a


DBG = {"skip_p0": False, "halves": (0, 1), "skip_rest": False, "skip_p3": False, "skip_ssd": False, "sprint_only": False, "no_cc": False, "gv": "tp", "parts": {"tp", "g", "ge", "y1", "y2", "gate", "epi"}}


def build_program(T, ncores=NCORES):
    Tc = T + 2
    Th = T // 2
    ngc = T // CHUNK

    nc = bass.Bass("TRN2", target_bir_lowering=False, debug=False)

    u_h = nc.dram_tensor("u", [D, Tc], F32, kind="ExternalInput")
    win_h = nc.dram_tensor("win_t", [D, 8224], F32, kind="ExternalInput")
    wout_h = nc.dram_tensor("wout_t", [D, D], F32, kind="ExternalInput")
    w1_h = nc.dram_tensor("w1_t", [D, HID], F32, kind="ExternalInput")
    w3_h = nc.dram_tensor("w3_t", [D, HID], F32, kind="ExternalInput")
    w2_h = nc.dram_tensor("w2_t", [HID, D], F32, kind="ExternalInput")
    convp_h = nc.dram_tensor("convp", [128, 192], F32, kind="ExternalInput")
    rmsp_h = nc.dram_tensor("rmsp", [128, 16], F32, kind="ExternalInput")
    alog_h = nc.dram_tensor("alog", [1, NH], F32, kind="ExternalInput")
    ident_h = nc.dram_tensor("ident", [128, 128], BF16, kind="ExternalInput")
    onesb_h = nc.dram_tensor("onesb", [128, 128], BF16, kind="ExternalInput")
    ones1_h = nc.dram_tensor("ones1", [1, 128], F32, kind="ExternalInput")
    h128_h = nc.dram_tensor("h128", [128, 128], BF16, kind="ExternalInput")
    h64_h = nc.dram_tensor("h64", [64, 64], BF16, kind="ExternalInput")
    mask4_h = nc.dram_tensor("mask4", [64, 256], F32, kind="ExternalInput")
    wsel_h = nc.dram_tensor("wsel", [128, 4], F32, kind="ExternalInput")
    ohsel_h = nc.dram_tensor("ohsel", [128, 4], F32, kind="ExternalInput")
    epsc_h = nc.dram_tensor("epsc", [1, 2], F32, kind="ExternalInput")
    out_h = nc.dram_tensor("out", [D, T], F32, kind="ExternalOutput")

    xbc_d = nc.dram_tensor("xbc_d", [CONV_DIM, T], BF16)
    zs_d = nc.dram_tensor("zs_d", [D, T], BF16)
    x1_d = nc.dram_tensor("x1_d", [D, T], F32)
    lnx1_d = nc.dram_tensor("lnx1_d", [D, T], BF16)
    hid_d = nc.dram_tensor("hid_d", [HID, T], BF16)
    p1_d = nc.dram_tensor("p1_d", [HID, D], BF16)
    m2_d = nc.dram_tensor("m2_d", [16 * 128, 64 * 128], BF16)
    cc_in = nc.dram_tensor("cc_in", [4, D], F32)
    cc_out = nc.dram_tensor("cc_out", [4, D], F32)

    S = Sched()
    es = ExitStack()
    with es:
        pb = [es.enter_context(nc.psum_tensor(f"pb{i}", [128, 512], F32)) for i in range(4)]
        pT = es.enter_context(nc.psum_tensor("pT", [128, 1024], BF16))
        pS = [es.enter_context(nc.psum_tensor(f"pS{i}", [128, 512], F32)) for i in range(2)]
        pst = es.enter_context(nc.psum_tensor("pst", [128, 512], F32))

        SB_BASE = 16512
        SB_TOP = 229344
        root = Arena(nc, SB_BASE, SB_TOP)
        ident = root.alloc("ident", [128, 128], BF16)
        onesb = root.alloc("onesb", [128, 128], BF16)
        ones1 = root.alloc("ones1", [1, 128], F32)
        h128 = root.alloc("h128", [128, 128], BF16)
        h64 = root.alloc("h64", [64, 64], BF16)
        mask4 = root.alloc("mask4", [64, 256], F32)
        convp = root.alloc("convp", [128, 192], F32)
        rmsp = root.alloc("rmsp", [128, 16], F32)
        wsel = root.alloc("wsel", [128, 4], F32)
        ohsel = root.alloc("ohsel", [128, 4], F32)
        alog = root.alloc("alog", [1, NH], F32)
        d16 = root.alloc("d16", [128, 16], F32)
        rows = root.alloc("rows", [1, 1536], F32)
        zn = root.alloc("zn", [128, 512], F32)
        st16 = root.alloc("st16", [128, 16 * 256], BF16)
        sq16 = root.alloc("sq16", [128, 16 * 256], BF16)
        zeros64 = root.alloc("zeros64", [128, CHUNK], F32)
        epsc = root.alloc("epsc", [1, 2], F32)

        block = es.enter_context(nc.Block())

        for name, dst, src in [
            ("c:ident", ident, ident_h), ("c:onesb", onesb, onesb_h),
            ("c:ones1", ones1, ones1_h), ("c:h128", h128, h128_h),
            ("c:h64", h64, h64_h),
            ("c:convp", convp, convp_h), ("c:rmsp", rmsp, rmsp_h),
            ("c:wsel", wsel, wsel_h), ("c:ohsel", ohsel, ohsel_h),
            ("c:alog", alog, alog_h), ("c:epsc", epsc, epsc_h),
            ("c:mask4", mask4, mask4_h),
        ]:
            S.dma("qs", "c", (lambda d, s: lambda e: e.dma_start(out=d[:, :], in_=s[:, :]))(dst, src),
                 writes=(name, "cR"))
        S.op("pe", lambda e: e.matmul(pb[0][:, 0:16], ones1[0:1, :],
                                      alog[0:1, 0:32:2], start=True, stop=True),
             reads=("c:ones1", "c:alog"), writes=("pb0",))
        S.op("act", lambda e: e.activation(d16[0:64, :], pb[0][0:64, :16], AF.Exp, scale=63.0),
             reads=("pb0",), writes=("c:d16",))
        S.op("pe", lambda e: e.matmul(pb[1][:, 0:16], ones1[0:1, :],
                                      alog[0:1, 1:32:2], start=True, stop=True),
             reads=("c:ones1", "c:alog"), writes=("pb1",))
        S.op("act", lambda e: e.activation(d16[64:128, :], pb[1][64:128, :16], AF.Exp, scale=63.0),
             reads=("pb1", "c:d16"), writes=("c:d16",))
        S.op("dve", lambda e: e.memset(zeros64[:, :], 0.0), writes=("c:zeros64",))

        SC = 1.0 / float(np.sqrt(HID))
        P0_RANGE = (0, 0) if DBG["skip_p0"] else (64, 128)
        ar = root.sub()
        wbufA = ar.alloc("p0wA", [128, 2048], BF16)
        wbufB = ar.alloc("p0wB", [128, 2048], BF16)
        stg = [ar.alloc(f"p0s{i}", [128, 2048], BF16) for i in range(2)]
        for hh in range(P0_RANGE[0]):
            par = hh % 2
            wb = wbufA if par == 0 else wbufB
            S.dma("qg", f"p0w{par}", (lambda hh, wb: lambda e: e.dma_start(
                out=wb[:, :], in_=w2_h[hh * 128:(hh + 1) * 128, :]))(hh, wb),
                writes=(f"p0w:{par}",))
            for dt_i in range(4):
                bi = (hh * 4 + dt_i) % 4
                S.op("pe", (lambda wb, dt_i, bi: lambda e: e.matmul(
                    pb[bi][:, :], h128[:, :], wb[:, dt_i * 512:(dt_i + 1) * 512],
                    start=True, stop=True))(wb, dt_i, bi),
                    reads=(f"p0w:{par}", "c:h128"), writes=(f"pb{bi}",))
                S.op("dve", (lambda dt_i, bi, par: lambda e: e.tensor_copy(
                    stg[par][:, dt_i * 512:(dt_i + 1) * 512], pb[bi][:, :]))(dt_i, bi, par),
                    reads=(f"pb{bi}", f"p0s:{par}"), writes=(f"p0s:{par}",))
            S.dma("qa", f"p0s{par}", (lambda hh, par: lambda e: e.dma_start(
                out=p1_d[hh * 128:(hh + 1) * 128, :], in_=stg[par][:, :]))(hh, par),
                reads=(f"p0s:{par}",), writes=("d:p1",))
        p1_v = p1_d.ap().rearrange("(hh jl) d -> hh jl d", jl=128)
        m2_w = m2_d.ap().rearrange("(ds p) (kc dl) -> p ds kc dl", p=128, dl=128)
        for jl in range(P0_RANGE[1]):
            par = jl % 2
            wb = wbufA if par == 0 else wbufB
            S.dma("qg", f"p0w{par}", (lambda jl, wb: lambda e: e.dma_start(
                out=wb[0:64, :], in_=p1_v[:, jl, :]))(jl, wb),
                reads=("d:p1",), writes=(f"p0w:{par}",))
            for dt_i in range(4):
                bi = (jl * 4 + dt_i) % 4
                S.op("pe", (lambda wb, dt_i, bi: lambda e: e.matmul(
                    pb[bi][0:64, :], h64[:, :], wb[0:64, dt_i * 512:(dt_i + 1) * 512],
                    start=True, stop=True))(wb, dt_i, bi),
                    reads=(f"p0w:{par}", "c:h64"), writes=(f"pb{bi}",))
                S.op("dve", (lambda dt_i, bi, par: lambda e: e.tensor_scalar(
                    stg[par][0:64, dt_i * 512:(dt_i + 1) * 512], pb[bi][0:64, :],
                    SC, None, ALU.mult))(dt_i, bi, par),
                    reads=(f"pb{bi}", f"p0s:{par}"), writes=(f"p0s:{par}",))
            S.dma("qa" if par == 0 else "qs", f"p0t{par}", (lambda jl, par: lambda e: e.dma_start(
                out=m2_w[jl, :, :, :].transpose([1, 0, 2]),
                in_=stg[par][0:64, :].rearrange("j (ds dl) -> j ds dl", dl=128)))(jl, par),
                reads=(f"p0s:{par}",), writes=("d:m2",))

        def ln_stats(get_val, vtoks, W, eps, do_mean=True):
            vals = [get_val(g) for g in range(16)]
            if do_mean:
                for g in range(16):
                    S.op("act", (lambda g, v, W: lambda e: e.activation(
                        sq16[:, g * 256:g * 256 + W], v, AF.Square))(g, vals[g], W),
                        reads=vtoks(g) + (f"sq16:{g}",), writes=(f"sq16:{g}",))
            for g in range(16):
                S.op("pe", (lambda g, v, W: lambda e: e.matmul(
                    pst[0:1, 0:W], onesb[:, 0:1], v,
                    start=(g == 0), stop=(g == 15)))(g, vals[g], W),
                    reads=vtoks(g) + ("c:onesb",), writes=("pst",))
            if do_mean:
                for g in range(16):
                    S.op("pe", (lambda g, W: lambda e: e.matmul(
                        pb[3][0:1, 0:W], onesb[:, 0:1], sq16[:, g * 256:g * 256 + W],
                        start=(g == 0), stop=(g == 15)))(g, W),
                        reads=(f"sq16:{g}", "c:onesb"), writes=("pb3",))
                S.op("dve", (lambda W: lambda e: e.tensor_scalar(
                    rows[0:1, 0:W], pst[0:1, 0:W], 1.0 / D, None, ALU.mult))(W),
                    reads=("pst", "r_mean"), writes=("r_mean",))
                S.op("dve", (lambda W: lambda e: e.tensor_scalar(
                    rows[0:1, 256:256 + W], pb[3][0:1, 0:W], 1.0 / D, None, ALU.mult))(W),
                    reads=("pb3", "r_msq"), writes=("r_msq",))
                S.op("dve", (lambda W: lambda e: e.tensor_mul(
                    rows[0:1, 512:512 + W], rows[0:1, 0:W], rows[0:1, 0:W]))(W),
                    reads=("r_mean", "r_m2"), writes=("r_m2",))
                S.op("dve", (lambda W: lambda e: e.tensor_sub(
                    rows[0:1, 768:768 + W], rows[0:1, 256:256 + W],
                    rows[0:1, 512:512 + W]))(W),
                    reads=("r_msq", "r_m2", "r_var"), writes=("r_var",))
                S.op("act", (lambda W, eps: lambda e: e.activation(
                    rows[0:1, 1024:1024 + W], rows[0:1, 768:768 + W], AF.Sqrt,
                    bias=eps))(W, eps),
                    reads=("r_var", "r_sd", "c:epsc"), writes=("r_sd",))
                S.op("dve", (lambda W: lambda e: e.reciprocal(
                    rows[0:1, 1280:1280 + W], rows[0:1, 1024:1024 + W]))(W),
                    reads=("r_sd", "r_rstd"), writes=("r_rstd",))
                S.op("pe", (lambda W: lambda e: e.matmul(
                    pst[:, 0:W], ones1[0:1, :], rows[0:1, 0:W], start=True, stop=True))(W),
                    reads=("r_mean", "c:ones1", "pst"), writes=("pst",))
                S.op("pe", (lambda W: lambda e: e.matmul(
                    pb[3][:, 0:W], ones1[0:1, :], rows[0:1, 1280:1280 + W],
                    start=True, stop=True))(W),
                    reads=("r_rstd", "c:ones1", "pb3"), writes=("pb3",))
            else:
                S.op("dve", (lambda W: lambda e: e.tensor_scalar(
                    rows[0:1, 256:256 + W], pst[0:1, 0:W], 1.0 / D, None, ALU.mult))(W),
                    reads=("pst", "r_msq"), writes=("r_msq",))
                S.op("act", (lambda W, eps: lambda e: e.activation(
                    rows[0:1, 1024:1024 + W], rows[0:1, 256:256 + W], AF.Sqrt,
                    bias=eps))(W, eps),
                    reads=("r_msq", "r_sd", "c:epsc"), writes=("r_sd",))
                S.op("dve", (lambda W: lambda e: e.reciprocal(
                    rows[0:1, 1280:1280 + W], rows[0:1, 1024:1024 + W]))(W),
                    reads=("r_sd", "r_rstd"), writes=("r_rstd",))
                S.op("pe", (lambda W: lambda e: e.matmul(
                    pb[3][:, 0:W], ones1[0:1, :], rows[0:1, 1280:1280 + W],
                    start=True, stop=True))(W),
                    reads=("r_rstd", "c:ones1", "pb3"), writes=("pb3",))

        for half in DBG["halves"]:
            S.barrier()
            ar = root.sub()
            lnu = ar.alloc("lnu", [128, 16 * (Th + 2)], BF16)
            wbufA = ar.alloc("p1wA", [128, 16 * 512], BF16)
            wbufB = ar.alloc("p1wB", [128, 16 * 512], BF16)
            uwin2 = [ar.alloc(f"uwin{i}", [128, 16 * 256], F32) for i in range(2)]
            zxA = ar.alloc("zxA", [128, Th + 2], F32)
            zxB = ar.alloc("zxB", [128, Th + 2], F32)
            cb0 = ar.alloc("cb0", [128, Th], BF16)
            cb1 = ar.alloc("cb1", [128, Th], BF16)
            cb2 = ar.alloc("cb2", [128, Th], BF16)
            ucol0 = half * Th
            ncols = Th + 2
            n_win = 0
            for wi, (woff, W) in enumerate(_splits(ncols, 256)):
                wp = wi % 2
                uwin = uwin2[wp]
                n_win += 1
                for g in range(16):
                    S.dma("qs", f"uw{wp}", (lambda g, woff, W, ucol0, uwin: lambda e: e.dma_start(
                        out=uwin[:, g * 256:g * 256 + W],
                        in_=u_h[g * 128:(g + 1) * 128, ucol0 + woff:ucol0 + woff + W]))(g, woff, W, ucol0, uwin),
                        reads=(f"uw:{wp}",), writes=(f"uw:{wp}",))
                    S.op("act", (lambda g, W, uwin: lambda e: e.activation(
                        st16[:, g * 256:g * 256 + W], uwin[:, g * 256:g * 256 + W],
                        AF.Identity))(g, W, uwin),
                        reads=(f"uw:{wp}", f"st16:{g}"), writes=(f"st16:{g}",))
                ln_stats(lambda g: st16[:, g * 256:g * 256 + W],
                         lambda g: (f"st16:{g}",), W, epsc[0:1, 0:1])
                for g in range(16):
                    S.op("dve", (lambda g, W, uwin: lambda e: e.tensor_sub(
                        zn[:, (g % 2) * 256:(g % 2) * 256 + W],
                        uwin[:, g * 256:g * 256 + W], pst[:, 0:W]))(g, W, uwin),
                        reads=(f"uw:{wp}", "pst", f"zn:{g % 2}"), writes=(f"zn:{g % 2}",))
                    S.op("dve", (lambda g, W, woff: lambda e: e.tensor_mul(
                        lnu[:, g * (Th + 2) + woff:g * (Th + 2) + woff + W],
                        zn[:, (g % 2) * 256:(g % 2) * 256 + W], pb[3][:, 0:W]))(g, W, woff),
                        reads=(f"zn:{g % 2}", "pb3", f"lnw:{wi}"), writes=(f"lnw:{wi}",))
            def lnu_tok_range(toff, tw):
                w0 = toff // 256
                w1 = (toff + tw - 1) // 256
                return tuple(f"lnw:{w}" for w in range(w0, min(w1, n_win - 1) + 1))
            ttiles = _splits(ncols, 512)
            for osl in range(16):
                par = osl % 2
                wb = wbufA if par == 0 else wbufB
                win_v = win_h.ap().rearrange("(kc p) o -> p kc o", p=128)
                S.dma("qg", f"p1w{par}", (lambda osl, wb: lambda e: e.dma_start(
                    out=wb[:, :].rearrange("p (kc o) -> p kc o", kc=16),
                    in_=win_v[:, :, osl * 512:(osl + 1) * 512]))(osl, wb),
                    reads=(f"p1w:{par}",), writes=(f"p1w:{par}",))
                for osub in range(4):
                    og = osl * 512 + osub * 128
                    zpar = (osl * 4 + osub) % 2
                    zx = zxA if zpar == 0 else zxB
                    zx_tok = f"zx:{zpar}"
                    for ti, (toff, tw) in enumerate(ttiles):
                        bi = (osub * len(ttiles) + ti) % 4
                        for k in range(16):
                            S.op("pe", (lambda wb, osub, k, toff, tw, bi: lambda e: e.matmul(
                                pb[bi][:, 0:tw],
                                wb[:, k * 512 + osub * 128:k * 512 + osub * 128 + 128],
                                lnu[:, k * (Th + 2) + toff:k * (Th + 2) + toff + tw],
                                start=(k == 0), stop=(k == 15)))(wb, osub, k, toff, tw, bi),
                                reads=(f"p1w:{par}",) + lnu_tok_range(toff, tw), writes=(f"pb{bi}",))
                        S.op("dve", (lambda bi, toff, tw, zx: lambda e: e.tensor_copy(
                            zx[:, toff:toff + tw], pb[bi][:, 0:tw]))(bi, toff, tw, zx),
                            reads=(f"pb{bi}", zx_tok), writes=(zx_tok,))
                    if og < D:
                        S.op("act", (lambda zx: lambda e: e.activation(
                            cb0[:, 0:Th], zx[:, 1:Th + 1], AF.Silu))(zx),
                            reads=(zx_tok, "cb0"), writes=("cb0",))
                        S.dma("qa", "scb0", (lambda og, half: lambda e: e.dma_start(
                            out=zs_d[og:og + 128, half * Th:half * Th + Th],
                            in_=cb0[:, 0:Th]))(og, half),
                            reads=("cb0",), writes=("d:zs",))
                    else:
                        c0 = ((og - D) // 128) * 4
                        S.op("dve", (lambda zx, c0: lambda e: e.tensor_scalar(
                            cb0[:, 0:Th], zx[:, 0:Th], convp[:, c0:c0 + 1], None, ALU.mult))(zx, c0),
                            reads=(zx_tok, "c:convp", "cb0"), writes=("cb0",))
                        S.op("dve", (lambda zx, c0: lambda e: e.tensor_scalar(
                            cb1[:, 0:Th], zx[:, 1:Th + 1], convp[:, c0 + 1:c0 + 2], None, ALU.mult))(zx, c0),
                            reads=(zx_tok, "c:convp", "cb1"), writes=("cb1",))
                        S.op("dve", (lambda zx, c0: lambda e: e.tensor_scalar(
                            cb2[:, 0:Th], zx[:, 2:Th + 2], convp[:, c0 + 2:c0 + 3], None, ALU.mult))(zx, c0),
                            reads=(zx_tok, "c:convp", "cb2"), writes=("cb2",))
                        S.op("dve", lambda e: e.tensor_add(cb0[:, 0:Th], cb0[:, 0:Th], cb1[:, 0:Th]),
                             reads=("cb0", "cb1"), writes=("cb0",))
                        S.op("dve", lambda e: e.tensor_add(cb0[:, 0:Th], cb0[:, 0:Th], cb2[:, 0:Th]),
                             reads=("cb0", "cb2"), writes=("cb0",))
                        S.op("act", (lambda c0: lambda e: e.activation(
                            cb1[:, 0:Th], cb0[:, 0:Th], AF.Silu, bias=convp[:, c0 + 3:c0 + 4]))(c0),
                            reads=("cb0", "c:convp", "cb1"), writes=("cb1",))
                        S.dma("qa", "scb1", (lambda og, half: lambda e: e.dma_start(
                            out=xbc_d[og - D:og - D + 128, half * Th:half * Th + Th],
                            in_=cb1[:, 0:Th]))(og, half),
                            reads=("cb1",), writes=("d:xbc",))

        if DBG["skip_rest"]:
            S.op("dve", lambda e: e.tensor_copy(zn[0:1, 0:1], zn[0:1, 0:1]),
                 writes=("fence",),
                 extra_waits=tuple((s, c) for s, c in S.count.items()
                                   if s.startswith("D:")))
            S.emit(block, lambda s: es.enter_context(
                nc.semaphore(f"sem_{s.replace(':', '_')}")))
            return nc
        S.barrier()
        p2ar = root.sub()
        wo = p2ar.alloc("wo", [128, 16 * 2048], BF16)
        st16b = p2ar.alloc("st16b", [64, 32 * ngc], BF16)
        sin16b = p2ar.alloc("sin16b", [64, 32], BF16)
        spar_a = p2ar.sub()
        cs_t = spar_a.alloc("cs_t", [128, 16 * ngc], F32)
        stt_t = spar_a.alloc("stt_t", [128, 16 * ngc], F32)
        st16_t = spar_a.alloc("st16_t", [128, 16 * ngc], BF16)
        s63 = spar_a.alloc("s63", [128, 16], F32)
        sinb = spar_a.alloc("sinb", [128, 64], F32)
        sin_t = spar_a.alloc("sin_t", [128, 16], F32)
        sin16 = spar_a.alloc("sin16", [128, 16], BF16)
        corr = spar_a.alloc("corr", [128, CHUNK], F32)
        spx = [spar_a.alloc(f"spx{i}", [128, T], BF16) for i in range(2)]
        spb = [spar_a.alloc(f"spb{i}", [128, T], BF16) for i in range(2)]
        spP = [spar_a.alloc(f"spP{i}", [128, T], F32) for i in range(2)]
        S.dma("qg", "wo", lambda e: e.dma_start(
            out=wo[:, :].rearrange("p (kc o) -> p kc o", kc=16),
            in_=wout_h.ap().rearrange("(kc p) o -> p kc o", p=128)),
            writes=("woR",))
        wo_toks = ("woR",)
        for g in range(16):
            par = g % 2
            S.dma("qs", f"spx{par}", (lambda g, par: lambda e: e.dma_start(
                out=spx[par][:, :], in_=xbc_d[g * 128:(g + 1) * 128, :]))(g, par),
                reads=(f"spx:{par}", "d:xbc"), writes=(f"spx:{par}",))
            S.dma("qs", f"spb{par}", (lambda g, par: lambda e: e.dma_start(
                out=spb[par][:, :], in_=xbc_d[D + g * 128:D + (g + 1) * 128, :]))(g, par),
                reads=(f"spb:{par}", "d:xbc"), writes=(f"spb:{par}",))
            S.op("dve", (lambda g, par: lambda e: e.tensor_mul(
                spP[par][:, :], spx[par][:, :], spb[par][:, :]))(g, par),
                reads=(f"spx:{par}", f"spb:{par}", f"spP:{par}"), writes=(f"spP:{par}",))
            S.op("dve", (lambda g, par: lambda e: e.tensor_reduce(
                cs_t[:, g * ngc:(g + 1) * ngc],
                spP[par][:, :].rearrange("p (c t) -> p c t", t=CHUNK),
                mybir.AxisListType.X, ALU.add))(g, par),
                reads=(f"spP:{par}",), writes=(f"cs:{g}",))
            S.op("dve", (lambda g: lambda e: e.tensor_tensor_scan(
                stt_t[:, g * ngc:(g + 1) * ngc],
                d16[:, g:g + 1].broadcast_to((128, ngc)),
                cs_t[:, g * ngc:(g + 1) * ngc], 0.0, ALU.mult, ALU.add))(g),
                reads=(f"cs:{g}", "c:d16", f"stt:{g}"), writes=(f"stt:{g}",))
        S.op("dve", lambda e: e.tensor_copy(
            s63[:, :], stt_t[:, :].rearrange("p (g c) -> p g c", c=ngc)[:, :, ngc - 1]),
            reads=tuple(f"stt:{g}" for g in range(16)), writes=("s63",))
        S.op("dve", lambda e: e.memset(sinb[:, :], 0.0), writes=("sinb",))
        for b in range(4):
            S.op("dve", (lambda b: lambda e: e.tensor_scalar(
                sinb[:, b * 16:(b + 1) * 16], s63[:, :], wsel[:, b:b + 1], None, ALU.mult))(b),
                reads=("s63", "c:wsel", "sinb"), writes=("sinb",))
        S.dma("qa", "scc", lambda e: e.dma_start(
            out=cc_in.ap().rearrange("b (r g) -> b r g", r=128).transpose([1, 0, 2]),
            in_=sinb[:, :].rearrange("p (b g) -> p b g", b=4)),
            reads=("sinb",), writes=("d:ccin",))
        if DBG["no_cc"]:
            S.dma("qs", "ccX", lambda e: e.dma_start(out=cc_out.ap(), in_=cc_in.ap()),
                  reads=("d:ccin",), writes=("d:ccout",))
        else:
            S.op("cc", lambda e: e.collective_compute(
                "AllReduce", ALU.add, replica_groups=[list(range(ncores))],
                ins=[cc_in.ap().opt()], outs=[cc_out.ap().opt()]),
                reads=("d:ccin",), writes=("d:ccout",), inc=1)
        S.dma("qs", "rcc", lambda e: e.dma_start(
            out=sinb[:, :].rearrange("p (b g) -> p b g", b=4),
            in_=cc_out.ap().rearrange("b (r g) -> b r g", r=128).transpose([1, 0, 2])),
            reads=("d:ccout", "sinb"), writes=("sinb",))
        S.op("dve", lambda e: e.tensor_scalar(
            sin_t[:, :], sinb[:, 0:16], ohsel[:, 0:1], None, ALU.mult),
            reads=("sinb", "c:ohsel", "sin"), writes=("sin",))
        for b in range(1, 4):
            S.op("dve", (lambda b: lambda e: e.scalar_tensor_tensor(
                sin_t[:, :], sinb[:, b * 16:(b + 1) * 16], ohsel[:, b:b + 1],
                sin_t[:, :], ALU.mult, ALU.add))(b),
                reads=("sinb", "c:ohsel", "sin"), writes=("sin",))
        for g in range(16):
            S.op("dve", (lambda g: lambda e: e.tensor_tensor_scan(
                corr[:, 0:ngc], d16[:, g:g + 1].broadcast_to((128, ngc)),
                zeros64[:, 0:ngc], sin_t[:, g:g + 1], ALU.mult, ALU.add))(g),
                reads=("sin", "c:d16", "c:zeros64", "corrtok"), writes=("corrtok",))
            S.op("dve", (lambda g: lambda e: e.tensor_add(
                stt_t[:, g * ngc:(g + 1) * ngc], stt_t[:, g * ngc:(g + 1) * ngc],
                corr[:, 0:ngc]))(g),
                reads=("corrtok", f"stt:{g}"), writes=(f"stt:{g}",))
        S.op("dve", lambda e: e.tensor_copy(st16_t[:, :], stt_t[:, :]),
             reads=tuple(f"stt:{g}" for g in range(16)), writes=("st16t",))
        S.op("dve", lambda e: e.tensor_copy(sin16[:, :], sin_t[:, :]),
             reads=("sin",), writes=("sin16",))

        if DBG["sprint_only"]:
            S.op("dve", lambda e: e.tensor_copy(zn[0:1, 0:1], zn[0:1, 0:1]),
                 writes=("fence",),
                 extra_waits=tuple((s, c) for s, c in S.count.items()
                                   if s.startswith("D:")))
            S.emit(block, lambda s: es.enter_context(
                nc.semaphore(f"sem_{s.replace(':', '_')}")))
            return nc
        S.dma("qs", "s16b", lambda e: e.dma_start(
            out=st16b[0:64, 0:16 * ngc], in_=st16_t[0:64, :]),
            reads=("st16t",), writes=("st16b",))
        S.dma("qs", "s16b", lambda e: e.dma_start(
            out=st16b[0:64, 16 * ngc:32 * ngc], in_=st16_t[64:128, :]),
            reads=("st16t",), writes=("st16b",))
        S.dma("qs", "s16c", lambda e: e.dma_start(
            out=sin16b[0:64, 0:16], in_=sin16[0:64, :]),
            reads=("sin16",), writes=("sin16b",))
        S.dma("qs", "s16c", lambda e: e.dma_start(
            out=sin16b[0:64, 16:32], in_=sin16[64:128, :]),
            reads=("sin16",), writes=("sin16b",))

        S.barrier()
        ar = p2ar.sub()
        xg = [ar.alloc(f"xg{i}", [128, 16 * 128], BF16) for i in range(2)]
        zg = [ar.alloc(f"zg{i}", [128, 16 * 128], BF16) for i in range(1)]
        bg = [ar.alloc(f"bg{i}", [64, 32 * 128], BF16) for i in range(1)]
        cg = [ar.alloc(f"cg{i}", [64, 32 * 128], BF16) for i in range(1)]
        xTc = [ar.alloc(f"xTc{i}", [64, 2 * 2048], BF16) for i in range(2)]
        gms = ar.alloc("gms", [64, 512], BF16)
        ygrp = ar.alloc("ygrp", [128, 16 * 128], F32)
        ynb = ar.alloc("ynb", [128, 16 * 512], BF16)
        mo16 = ar.alloc("mo16", [128, 16 * 256], BF16)
        x1b = ar.alloc("x1b", [128, 16 * 256], F32)
        uw2 = ar.alloc("uw2", [128, 16 * 256], F32)

        pend_gate = []
        for sgi, (sgoff, sgw) in enumerate(_splits(T, 512)):
            ngrp = sgw // 128
            for grp in range(ngrp):
                goff = sgoff + grp * 128
                gpar = grp % 2
                xbc_g = xbc_d.ap().rearrange("(g p) t -> p g t", p=128)
                xbc_h = xbc_d.ap().rearrange("(h n) t -> n h t", n=64)
                zs_g = zs_d.ap().rearrange("(g p) t -> p g t", p=128)
                S.dma("qs", f"xg{gpar}", (lambda goff, gpar: lambda e: e.dma_start(
                    out=xg[gpar][:, :].rearrange("p (g t) -> p g t", g=16),
                    in_=xbc_g[:, 0:16, goff:goff + 128]))(goff, gpar),
                    reads=(f"xg:{gpar}", "d:xbc"), writes=(f"xg:{gpar}",))
                S.dma("qs", f"zg{gpar}", (lambda goff, gpar: lambda e: e.dma_start(
                    out=zg[0][:, :].rearrange("p (g t) -> p g t", g=16),
                    in_=zs_g[:, 0:16, goff:goff + 128]))(goff, gpar),
                    reads=("zg:L", "d:zs"), writes=("zg:L",))
                S.dma("qs", "bgL", (lambda goff, gpar: lambda e: e.dma_start(
                    out=bg[0][0:64, :].rearrange("p (h t) -> p h t", h=32),
                    in_=xbc_h[:, 32:64, goff:goff + 128]))(goff, gpar),
                    reads=("bg:L", "d:xbc"), writes=("bg:L",))
                S.dma("qs", "cgL", (lambda goff, gpar: lambda e: e.dma_start(
                    out=cg[0][0:64, :].rearrange("p (h t) -> p h t", h=32),
                    in_=xbc_h[:, 64:96, goff:goff + 128]))(goff, gpar),
                    reads=("cg:L", "d:xbc"), writes=("cg:L",))
                if "tp" in DBG["parts"]:
                    for ci in range(2):
                        for g in range(16):
                            slot = g % 8
                            S.op("pe", (lambda g, ci, gpar, slot: lambda e: e.transpose(
                                pT[0:64, slot * 128:(slot + 1) * 128],
                                xg[gpar][:, g * 128 + ci * 64:g * 128 + ci * 64 + 64],
                                ident[:, :]))(g, ci, gpar, slot),
                                reads=(f"xg:{gpar}", "c:ident", "pT"), writes=("pT",))
                            if g % 2 == 1:
                                S.op("dve", (lambda g, ci, gpar, slot: lambda e: e.tensor_copy(
                                    xTc[gpar][0:64, ci * 2048 + (g - 1) * 128:ci * 2048 + (g + 1) * 128],
                                    pT[0:64, (slot - 1) * 128:(slot + 1) * 128]))(g, ci, gpar, slot),
                                    reads=("pT", f"xT:{gpar}"), writes=(f"xT:{gpar}",))
                ssd_banks = [(pS[0], "pS0"), (pS[1], "pS1"), (pb[2], "pb2")]

                def emit_G(gh):
                    bank, btok = ssd_banks[gh % 3]
                    if "g" not in DBG["parts"]:
                        return
                    for hi in range(2):
                        h = 2 * gh + hi
                        for ci in range(2):
                            S.op("pe", (lambda h, hi, ci, gpar, bank: lambda e: e.matmul(
                                bank[0:64, (hi * 2 + ci) * 64:(hi * 2 + ci) * 64 + 64],
                                bg[0][0:64, h * 128 + ci * 64:h * 128 + ci * 64 + 64],
                                cg[0][0:64, h * 128 + ci * 64:h * 128 + ci * 64 + 64],
                                start=True, stop=True))(h, hi, ci, gpar, bank),
                                reads=("bg:L", "cg:L", btok),
                                writes=(btok,))

                def emit_rest(gh):
                    bank, btok = ssd_banks[gh % 3]
                    c_base = goff // CHUNK
                    if "ge" in DBG["parts"]:
                        S.op("dve", (lambda gh, bank: lambda e: e.tensor_mul(
                            gms[0:64, (gh % 2) * 256:(gh % 2) * 256 + 256],
                            bank[0:64, 0:256], mask4[0:64, :]))(gh, bank),
                            reads=(btok, "c:mask4", f"gms:{gh % 2}"), writes=(f"gms:{gh % 2}",))
                    for hi in range(2 if ("y1" in DBG["parts"] or "y2" in DBG["parts"]) else 0):
                        h = 2 * gh + hi
                        hb = hi * 64
                        for ci in range(2):
                            c_loc = c_base + ci
                            if c_loc == 0:
                                prevap = sin16b[0:64, (h % 2) * 16 + h // 2:(h % 2) * 16 + h // 2 + 1]
                                ptok = "sin16b"
                            else:
                                pcol = ((h % 2) * 16 + h // 2) * ngc + c_loc - 1
                                prevap = st16b[0:64, pcol:pcol + 1]
                                ptok = "st16b"
                            if "y1" in DBG["parts"]:
                                S.op("pe", (lambda h, hi, hb, ci, gh, gpar, bank: lambda e: e.matmul(
                                    bank[hb:hb + 64, 256 + ci * 64:256 + ci * 64 + 64],
                                    xTc[gpar][0:64, ci * 2048 + h * 64:ci * 2048 + h * 64 + 64],
                                    gms[0:64, (gh % 2) * 256 + (hi * 2 + ci) * 64:
                                        (gh % 2) * 256 + (hi * 2 + ci) * 64 + 64],
                                    start=True, stop=("y2" not in DBG["parts"])))(h, hi, hb, ci, gh, gpar, bank),
                                    reads=(f"gms:{gh % 2}", f"xT:{gpar}", btok), writes=(btok,))
                            if "y2" in DBG["parts"]:
                                S.op("pe", (lambda h, hb, ci, gpar, prevap, bank: lambda e: e.matmul(
                                    bank[hb:hb + 64, 256 + ci * 64:256 + ci * 64 + 64],
                                    prevap.broadcast_to((64, 64)),
                                    cg[0][0:64, h * 128 + ci * 64:h * 128 + ci * 64 + 64],
                                    start=("y1" not in DBG["parts"]), stop=True))(h, hb, ci, gpar, prevap, bank),
                                    reads=(ptok, "cg:L", btok), writes=(btok,))
                    if "y1" in DBG["parts"] or "y2" in DBG["parts"]:
                        S.op("dve", (lambda gh, gpar, bank: lambda e: e.tensor_mul(
                            ygrp[:, gh * 128:(gh + 1) * 128], bank[:, 256:384],
                            zg[0][:, gh * 128:(gh + 1) * 128]))(gh, gpar, bank),
                            reads=(btok, "zg:L", f"yg:{gh}"), writes=(f"yg:{gh}",))

                for gh in range(16):
                    emit_G(gh)
                    if gh >= 2:
                        emit_rest(gh - 2)
                emit_rest(14)
                emit_rest(15)
                def gate_seg(grp, gpar):
                    for g in range(16 if "gate" in DBG["parts"] else 0):
                        S.op("act", (lambda g: lambda e: e.activation(
                            sq16[:, g * 256:g * 256 + 128], ygrp[:, g * 128:(g + 1) * 128],
                            AF.Square))(g),
                            reads=(f"yg:{g}", f"sq16:{g}"), writes=(f"sq16:{g}",))
                    for g in range(16 if "gate" in DBG["parts"] else 0):
                        S.op("pe", (lambda g: lambda e: e.matmul(
                            pst[0:1, 0:128], onesb[:, 0:1], sq16[:, g * 256:g * 256 + 128],
                            start=(g == 0), stop=(g == 15)))(g),
                            reads=(f"sq16:{g}", "c:onesb"), writes=("pst",))
                    S.op("dve", lambda e: e.tensor_scalar(
                        rows[0:1, 256:384], pst[0:1, 0:128], 1.0 / D, None, ALU.mult),
                        reads=("pst", "r_msq"), writes=("r_msq",))
                    S.op("act", lambda e: e.activation(
                        rows[0:1, 1024:1152], rows[0:1, 256:384], AF.Sqrt, bias=epsc[0:1, 1:2]),
                        reads=("r_msq", "r_sd", "c:epsc"), writes=("r_sd",))
                    S.op("dve", lambda e: e.reciprocal(
                        rows[0:1, 1280:1408], rows[0:1, 1024:1152]),
                        reads=("r_sd", "r_rstd"), writes=("r_rstd",))
                    S.op("pe", lambda e: e.matmul(
                        pb[3][:, 0:128], ones1[0:1, :], rows[0:1, 1280:1408],
                        start=True, stop=True),
                        reads=("r_rstd", "c:ones1", "pb3"), writes=("pb3",))
                    for g in range(16):
                        S.op("dve", (lambda g, grp: lambda e: e.scalar_tensor_tensor(
                            ynb[:, g * 512 + grp * 128:g * 512 + grp * 128 + 128],
                            ygrp[:, g * 128:(g + 1) * 128], rmsp[:, g:g + 1],
                            pb[3][:, 0:128], ALU.mult, ALU.mult))(g, grp),
                            reads=(f"yg:{g}", "c:rmsp", "pb3", f"yn:{g}"), writes=(f"yn:{g}",))
                pend_gate.append((grp, gpar))
                gate_seg(*pend_gate.pop(0))
            while pend_gate:
                gate_seg(*pend_gate.pop(0))
            yn_toks = tuple(f"yn:{g}" for g in range(16))
            for wi, (woff, W) in enumerate(_splits(sgw, 256)):
                for od in range(16):
                    bi = od % 2
                    for k in range(16):
                        S.op("pe", (lambda od, k, bi, woff, W: lambda e: e.matmul(
                            pb[bi][:, 0:W],
                            wo[:, k * 2048 + od * 128:k * 2048 + od * 128 + 128],
                            ynb[:, k * 512 + woff:k * 512 + woff + W],
                            start=(k == 0), stop=(k == 15)))(od, k, bi, woff, W),
                            reads=wo_toks + yn_toks + (f"pb{bi}",), writes=(f"pb{bi}",))
                    S.op("act", (lambda od, bi, W: lambda e: e.activation(
                        mo16[:, od * 256:od * 256 + W], pb[bi][:, 0:W], AF.Identity))(od, bi, W),
                        reads=(f"pb{bi}", f"mo:{od}"), writes=(f"mo:{od}",))
                for g in range(4):
                    pass
                ln_stats(lambda g: mo16[:, g * 256:g * 256 + W],
                         lambda g: (f"mo:{g}",), W, epsc[0:1, 0:1])
                S.dma("qs", "uw2", (lambda sgoff, woff, W: lambda e: e.dma_start(
                    out=uw2[:, :].rearrange("p (g t) -> p g t", g=16)[:, :, 0:W],
                    in_=u_h.ap().rearrange("(g p) t -> p g t", p=128)[
                        :, :, sgoff + woff + 1:sgoff + woff + 1 + W]))(sgoff, woff, W),
                    reads=("uw2",), writes=("uw2",))
                for g in range(16):
                    S.op("dve", (lambda g, W: lambda e: e.tensor_sub(
                        zn[:, (g % 2) * 256:(g % 2) * 256 + W],
                        mo16[:, g * 256:g * 256 + W], pst[:, 0:W]))(g, W),
                        reads=(f"mo:{g}", "pst", f"zn:{g % 2}"), writes=(f"zn:{g % 2}",))
                    S.op("dve", (lambda g, W: lambda e: e.tensor_mul(
                        zn[:, (g % 2) * 256:(g % 2) * 256 + W],
                        zn[:, (g % 2) * 256:(g % 2) * 256 + W], pb[3][:, 0:W]))(g, W),
                        reads=(f"zn:{g % 2}", "pb3"), writes=(f"zn:{g % 2}",))
                    S.op("dve", (lambda g, W: lambda e: e.tensor_add(
                        x1b[:, g * 256:g * 256 + W],
                        zn[:, (g % 2) * 256:(g % 2) * 256 + W],
                        uw2[:, g * 256:g * 256 + W]))(g, W),
                        reads=(f"zn:{g % 2}", "uw2", f"x1:{g}", "sx1R"), writes=(f"x1:{g}",))
                    S.op("act", (lambda g, W: lambda e: e.activation(
                        st16[:, g * 256:g * 256 + W], x1b[:, g * 256:g * 256 + W],
                        AF.Identity))(g, W),
                        reads=(f"x1:{g}", f"st16:{g}", "slxR"), writes=(f"st16:{g}",))
                S.dma("qa", "sx1", (lambda sgoff, woff, W: lambda e: e.dma_start(
                    out=x1_d.ap().rearrange("(g p) t -> p g t", p=128)[
                        :, :, sgoff + woff:sgoff + woff + W],
                    in_=x1b[:, :].rearrange("p (g t) -> p g t", g=16)[:, :, 0:W]))(sgoff, woff, W),
                    reads=tuple(f"x1:{g}" for g in range(16)), writes=("d:x1", "sx1R"))
                ln_stats(lambda g: st16[:, g * 256:g * 256 + W],
                         lambda g: (f"st16:{g}",), W, epsc[0:1, 0:1])
                for g in range(16):
                    S.op("dve", (lambda g, W: lambda e: e.tensor_sub(
                        zn[:, (g % 2) * 256:(g % 2) * 256 + W],
                        x1b[:, g * 256:g * 256 + W], pst[:, 0:W]))(g, W),
                        reads=(f"x1:{g}", "pst", f"zn:{g % 2}"), writes=(f"zn:{g % 2}",))
                    S.op("dve", (lambda g, W: lambda e: e.tensor_mul(
                        st16[:, g * 256:g * 256 + W],
                        zn[:, (g % 2) * 256:(g % 2) * 256 + W], pb[3][:, 0:W]))(g, W),
                        reads=(f"zn:{g % 2}", "pb3", f"st16:{g}"), writes=(f"st16:{g}",))
                S.dma("qa", "slx", (lambda sgoff, woff, W: lambda e: e.dma_start(
                    out=lnx1_d.ap().rearrange("(g p) t -> p g t", p=128)[
                        :, :, sgoff + woff:sgoff + woff + W],
                    in_=st16[:, :].rearrange("p (g t) -> p g t", g=16)[:, :, 0:W]))(sgoff, woff, W),
                    reads=tuple(f"st16:{g}" for g in range(16)), writes=("d:lnx1", "slxR"))

        if DBG["skip_p3"]:
            S.op("dve", lambda e: e.tensor_copy(zn[0:1, 0:1], zn[0:1, 0:1]),
                 writes=("fence",),
                 extra_waits=tuple((s, c) for s, c in S.count.items()
                                   if s.startswith("D:")))
            S.emit(block, lambda s: es.enter_context(
                nc.semaphore(f"sem_{s.replace(':', '_')}")))
            return nc
        S.barrier()
        ar = root.sub()
        lnx1r = ar.alloc("lnx1r", [128, 16 * T], BF16)
        wbufA = ar.alloc("p3wA", [128, 16 * 512], BF16)
        wbufB = ar.alloc("p3wB", [128, 16 * 512], BF16)
        gs = [ar.alloc(f"gs{i}", [128, 512], BF16) for i in range(2)]
        hs = [ar.alloc(f"hs{i}", [128, 512], BF16) for i in range(2)]
        for g in range(16):
            S.dma("qs", "lx", (lambda g: lambda e: e.dma_start(
                out=lnx1r[:, g * T:(g + 1) * T],
                in_=lnx1_d[g * 128:(g + 1) * 128, :]))(g),
                reads=("d:lnx1",), writes=("lxR",))
        lx_toks = ("lxR",)
        t3 = _splits(T, 512)
        w1_v = w1_h.ap().rearrange("(kc p) o -> p kc o", p=128)
        w3_v = w3_h.ap().rearrange("(kc p) o -> p kc o", p=128)
        for osl in range(16):
            S.dma("qg", "p3w1", (lambda osl: lambda e: e.dma_start(
                out=wbufA[:, :].rearrange("p (kc o) -> p kc o", kc=16),
                in_=w1_v[:, :, osl * 512:(osl + 1) * 512]))(osl),
                reads=("p3w1R",), writes=("p3w1R",))
            S.dma("qg", "p3w3", (lambda osl: lambda e: e.dma_start(
                out=wbufB[:, :].rearrange("p (kc o) -> p kc o", kc=16),
                in_=w3_v[:, :, osl * 512:(osl + 1) * 512]))(osl),
                reads=("p3w3R",), writes=("p3w3R",))
            for osub in range(4):
                og = osl * 512 + osub * 128
                for ti, (toff, tw) in enumerate(t3):
                    bi1 = (2 * (osub * len(t3) + ti)) % 4
                    bi3 = bi1 + 1 if bi1 % 2 == 0 else bi1 - 1
                    bi3 = (bi1 + 1) % 4
                    for k in range(16):
                        S.op("pe", (lambda osub, k, toff, tw, bi1: lambda e: e.matmul(
                            pb[bi1][:, 0:tw],
                            wbufA[:, k * 512 + osub * 128:k * 512 + osub * 128 + 128],
                            lnx1r[:, k * T + toff:k * T + toff + tw],
                            start=(k == 0), stop=(k == 15)))(osub, k, toff, tw, bi1),
                            reads=("p3w1R",) + lx_toks, writes=(f"pb{bi1}",))
                    for k in range(16):
                        S.op("pe", (lambda osub, k, toff, tw, bi3: lambda e: e.matmul(
                            pb[bi3][:, 0:tw],
                            wbufB[:, k * 512 + osub * 128:k * 512 + osub * 128 + 128],
                            lnx1r[:, k * T + toff:k * T + toff + tw],
                            start=(k == 0), stop=(k == 15)))(osub, k, toff, tw, bi3),
                            reads=("p3w3R",) + lx_toks, writes=(f"pb{bi3}",))
                    sp = ti % 2
                    S.op("act", (lambda bi1, tw, sp: lambda e: e.activation(
                        gs[sp][:, 0:tw], pb[bi1][:, 0:tw], AF.Silu))(bi1, tw, sp),
                        reads=(f"pb{bi1}", f"gs:{sp}"), writes=(f"gs:{sp}",))
                    S.op("dve", (lambda bi3, tw, sp: lambda e: e.tensor_mul(
                        hs[sp][:, 0:tw], pb[bi3][:, 0:tw], gs[sp][:, 0:tw]))(bi3, tw, sp),
                        reads=(f"pb{bi3}", f"gs:{sp}", f"hs:{sp}"), writes=(f"hs:{sp}",))
                    S.dma("qa", f"shid{sp}", (lambda og, toff, tw, sp: lambda e: e.dma_start(
                        out=hid_d[og:og + 128, toff:toff + tw],
                        in_=hs[sp][:, 0:tw]))(og, toff, tw, sp),
                        reads=(f"hs:{sp}",), writes=("d:hid",))

        S.barrier()
        ar = root.sub()
        pend_p3 = []
        hidb = ar.alloc("hidb", [128, 64 * 512], BF16)
        m2b = [ar.alloc(f"m2b{i}", [128, 64 * 128], BF16) for i in range(2)]
        fo16p = [ar.alloc(f"fo16{i}", [128, 16 * 512], BF16) for i in range(2)]
        x1w = ar.alloc("x1w", [128, 16 * 256], F32)
        outb = ar.alloc("outb", [128, 16 * 256], F32)
        m2_k = m2_d.ap().rearrange("(ds p) c -> ds p c", p=128)
        for ti, (toff, tw) in enumerate(t3):
            hid_v = hid_d.ap().rearrange("(hc p) t -> p hc t", p=128)
            for hh2 in range(2):
                S.dma("qs", "h3", (lambda hh2, toff, tw: lambda e: e.dma_start(
                    out=hidb[:, :].rearrange("p (hc t) -> p hc t", hc=64)[
                        :, hh2 * 32:(hh2 + 1) * 32, 0:tw],
                    in_=hid_v[:, hh2 * 32:(hh2 + 1) * 32, toff:toff + tw]))(hh2, toff, tw),
                    reads=("h3R", "d:hid"), writes=("h3R",))
            for dsub in range(16):
                wpar = dsub % 2
                S.dma("qa" if wpar == 0 else "qs", f"m2w{wpar}", (lambda dsub, wpar: lambda e: e.dma_start(
                    out=m2b[wpar][:, :], in_=m2_k[dsub, :, :]))(dsub, wpar),
                    reads=(f"m2w:{wpar}", "d:m2"), writes=(f"m2w:{wpar}",))
                bi = dsub % 2
                for k in range(64):
                    S.op("pe", (lambda k, wpar, bi, tw: lambda e: e.matmul(
                        pb[bi][:, 0:tw], m2b[wpar][:, k * 128:(k + 1) * 128],
                        hidb[:, k * 512:k * 512 + tw],
                        start=(k == 0), stop=(k == 63)))(k, wpar, bi, tw),
                        reads=(f"m2w:{wpar}", "h3R", f"pb{bi}"), writes=(f"pb{bi}",))
                tpar = ti % 2
                S.op("act", (lambda dsub, bi, tw, tpar: lambda e: e.activation(
                    fo16p[tpar][:, dsub * 512:dsub * 512 + tw], pb[bi][:, 0:tw],
                    AF.Identity))(dsub, bi, tw, tpar),
                    reads=(f"pb{bi}", f"fo:{tpar}:{dsub}"), writes=(f"fo:{tpar}:{dsub}",))
            def p3b_ln(toff, tw, tpar):
                for woff, W in _splits(tw, 256):
                    ln_stats((lambda g, tpar=tpar: fo16p[tpar][:, g * 512 + woff:g * 512 + woff + W]),
                         (lambda g, tpar=tpar: (f"fo:{tpar}:{g}",)), W, epsc[0:1, 0:1])
                    S.dma("qs", "x1w", (lambda toff, woff, W: lambda e: e.dma_start(
                        out=x1w[:, :].rearrange("p (g t) -> p g t", g=16)[:, :, 0:W],
                        in_=x1_d.ap().rearrange("(g p) t -> p g t", p=128)[
                            :, :, toff + woff:toff + woff + W]))(toff, woff, W),
                        reads=("x1w", "d:x1"), writes=("x1w",))
                    for g in range(16):
                        S.op("dve", (lambda g, woff, W, tpar: lambda e: e.tensor_sub(
                            zn[:, (g % 2) * 256:(g % 2) * 256 + W],
                            fo16p[tpar][:, g * 512 + woff:g * 512 + woff + W], pst[:, 0:W]))(g, woff, W, tpar),
                            reads=(f"fo:{tpar}:{g}", "pst", f"zn:{g % 2}"), writes=(f"zn:{g % 2}",))
                        S.op("dve", (lambda g, W: lambda e: e.tensor_mul(
                        zn[:, (g % 2) * 256:(g % 2) * 256 + W],
                        zn[:, (g % 2) * 256:(g % 2) * 256 + W], pb[3][:, 0:W]))(g, W),
                        reads=(f"zn:{g % 2}", "pb3"), writes=(f"zn:{g % 2}",))
                        S.op("dve", (lambda g, W: lambda e: e.tensor_add(
                            outb[:, g * 256:g * 256 + W],
                            zn[:, (g % 2) * 256:(g % 2) * 256 + W],
                            x1w[:, g * 256:g * 256 + W]))(g, W),
                            reads=(f"zn:{g % 2}", "x1w", f"ob:{g}", "obR"), writes=(f"ob:{g}",))
                    S.dma("qa", "so", (lambda toff, woff, W: lambda e: e.dma_start(
                        out=out_h.ap().rearrange("(g p) t -> p g t", p=128)[
                            :, :, toff + woff:toff + woff + W],
                        in_=outb[:, :].rearrange("p (g t) -> p g t", g=16)[:, :, 0:W]))(toff, woff, W),
                        reads=tuple(f"ob:{g}" for g in range(16)), writes=("d:out", "obR"))


            pend_p3.append((toff, tw, ti % 2, p3b_ln))
            if len(pend_p3) > 1:
                a, b, c, f = pend_p3.pop(0)
                f(a, b, c)
        while pend_p3:
            a, b, c, f = pend_p3.pop(0)
            f(a, b, c)
        S.op("dve", lambda e: e.tensor_copy(zn[0:1, 0:1], zn[0:1, 0:1]),
             writes=("fence",),
             extra_waits=tuple((s, c) for s, c in S.count.items()
                               if s.startswith("D:")))
        S.emit(block, lambda s: es.enter_context(
            nc.semaphore(f"sem_{s.replace(':', '_')}")))

    return nc


_CACHE = {}


def _get_program(T, ncores=NCORES):
    key = (T, ncores)
    if key not in _CACHE:
        _CACHE[key] = build_program(T, ncores)
    return _CACHE[key]


def _host_inputs(W_in, W_out, conv_w, conv_b, A_log, rms_w, W1, W2, W3):
    c = {}
    c["win_t"] = np.ascontiguousarray(np.asarray(W_in).T).astype(np.float32)
    c["wout_t"] = np.ascontiguousarray(np.asarray(W_out).T).astype(np.float32)
    c["w1_t"] = np.ascontiguousarray(np.asarray(W1).T).astype(np.float32)
    c["w3_t"] = np.ascontiguousarray(np.asarray(W3).T).astype(np.float32)
    c["w2_t"] = np.ascontiguousarray(np.asarray(W2).T).astype(np.float32)
    cw = np.asarray(conv_w)[:, 0, :]
    cb = np.asarray(conv_b)
    pack = np.concatenate(
        [cw.reshape(48, 128, 3).transpose(1, 0, 2),
         cb.reshape(48, 128).T[:, :, None]], axis=2)
    c["convp"] = np.ascontiguousarray(pack.reshape(128, 192)).astype(np.float32)
    c["rmsp"] = np.ascontiguousarray(np.asarray(rms_w).reshape(16, 128).T).astype(np.float32)
    c["alog"] = np.asarray(A_log).reshape(1, 32).astype(np.float32)
    c["ident"] = np.eye(128, dtype=ml_dtypes.bfloat16)
    c["onesb"] = np.ones((128, 128), dtype=ml_dtypes.bfloat16)
    c["ones1"] = np.ones((1, 128), dtype=np.float32)
    c["h128"] = _hadamard(128).astype(ml_dtypes.bfloat16)
    c["h64"] = _hadamard(64).astype(ml_dtypes.bfloat16)
    m = (np.arange(64)[:, None] <= np.arange(64)[None, :]).astype(np.float32)
    c["mask4"] = np.ascontiguousarray(np.tile(m, (1, 4)))
    c["epsc"] = np.array([[LN_EPS, RMS_EPS]], dtype=np.float32)
    return c


def kernel(L0_attn_u, L0_attn_W_in, L0_attn_W_out, L0_attn_conv_w,
           L0_attn_conv_b, L0_attn_A_log, L0_attn_rms_w, L0_ffn_W1,
           L0_ffn_W2, L0_ffn_W3):
    u = np.asarray(L0_attn_u)
    B, L, _ = u.shape
    T = L // 2
    nc = _get_program(T)
    shared = _host_inputs(L0_attn_W_in, L0_attn_W_out, L0_attn_conv_w,
                          L0_attn_conv_b, L0_attn_A_log, L0_attn_rms_w,
                          L0_ffn_W1, L0_ffn_W2, L0_ffn_W3)
    in_maps = []
    for k in range(NCORES):
        b, half = k // 2, k % 2
        t0 = half * T
        win = np.zeros((T + 2, 2048), dtype=np.float32)
        lo, hi = t0 - 1, t0 + T + 1
        rlo, rhi = max(lo, 0), min(hi, L)
        win[rlo - lo:rhi - lo] = u[b][rlo:rhi]
        m = dict(shared)
        m["u"] = np.ascontiguousarray(win.T)
        ws = np.zeros((128, 4), dtype=np.float32)
        oh = np.zeros((128, 4), dtype=np.float32)
        if half == 0:
            ws[:, b] = 1.0
        else:
            oh[:, b] = 1.0
        m["wsel"] = ws
        m["ohsel"] = oh
        in_maps.append(m)
    res = run_bass_kernel_spmd(nc, in_maps, core_ids=list(range(NCORES)))
    out = np.empty((B, L, 2048), dtype=np.float32)
    for k in range(NCORES):
        b, half = k // 2, k % 2
        out[b, half * T:(half + 1) * T] = np.asarray(res.results[k]["out"]).T
    return out
